# version 04
# baseline (speedup 1.0000x reference)
import numpy as np
import ml_dtypes
from contextlib import ExitStack

import concourse.bass as bass
import concourse.mybir as mybir
from concourse.bass_utils import run_bass_kernel_spmd

F32 = mybir.dt.float32
BF16 = mybir.dt.bfloat16
AF = mybir.ActivationFunctionType
ALU = mybir.AluOpType
AX = mybir.AxisListType
BF = ml_dtypes.bfloat16

D = 2048
DT = D // 128
INTER = 8192
IT = INTER // 128
B, S = 4, 4096
NT_S12 = 21
NT_OWN = 16
OWN0 = 4
N_S12 = NT_S12 * 128
N_OWN = NT_OWN * 128
HD = 128
WINDOW, NMETA = 512, 16
SC_A = 1.0 / float(np.sqrt(2048.0))
SC_B = 1.0 / float(np.sqrt(128.0))
EPS = 1e-5
NEG = -1.0e9
QT_W = 640
USB_W = 640

ENG = ("pe", "dve", "act", "pool", "sp")
DBG = False


def _hadamard(n):
    h = np.array([[1.0]])
    while h.shape[0] < n:
        h = np.block([[h, h], [h, -h]])
    return h


def host_constants():
    c = {}
    c["c_ident"] = np.eye(128).astype(BF)
    c["c_identf"] = np.eye(128).astype(np.float32)
    c["c_h128"] = _hadamard(128).astype(BF)
    c["c_bf16"] = np.kron(np.eye(8), _hadamard(16)).astype(BF)
    c["c_bf64"] = np.kron(np.eye(2), _hadamard(64)).astype(BF)
    R = np.zeros((128, 128), np.float32)
    for i in range(64):
        R[2 * i, 2 * i + 1] = -1.0
        R[2 * i + 1, 2 * i] = 1.0
    c["c_rope"] = np.ascontiguousarray(R.T).astype(BF)
    c["c_ones"] = np.ones((1, 512)).astype(BF)
    return c


def slot_tiles(h):
    if h == 0:
        return [16, 17, 18, 19] + list(range(0, 16)) + [0]
    return [12, 13, 14, 15] + list(range(16, 32)) + [0]


def host_rope_tables(h):
    tiles = slot_tiles(h)
    pos = np.concatenate([np.arange(t * 128, (t + 1) * 128) for t in tiles])
    inv = 1.0 / (10000.0 ** (np.arange(0, HD, 2, dtype=np.float64) / HD))
    ang = np.repeat(inv, 2)[:, None] * pos[None, :].astype(np.float64)
    return np.cos(ang).astype(np.float32), np.sin(ang).astype(np.float32)


def host_masks(h):
    tiles = slot_tiles(h)
    own_start_tile = 0 if h == 0 else 16
    m = np.full((16, 128, 768), NEG, np.float32)
    for j in range(16):
        gq = own_start_tile + j
        p = gq * 128 + np.arange(128)
        for bs in range(5):
            slot = j + bs
            gt = tiles[slot]
            if h == 0 and slot < OWN0:
                continue
            kp = gt * 128 + np.arange(128)
            ok = (kp[None, :] <= p[:, None]) & (
                (kp[None, :] >= p[:, None] - (WINDOW - 1)) | (kp[None, :] < NMETA))
            m[j, :, bs * 128:(bs + 1) * 128] = np.where(ok, 0.0, NEG)
        band_tiles = [tiles[j + bs] for bs in range(5)
                      if not (h == 0 and j + bs < OWN0)]
        if 0 not in band_tiles:
            kt = np.arange(128)
            ok = (kt[None, :] < NMETA) & (kt[None, :] <= p[:, None])
            m[j, :, 640:768] = np.where(ok, 0.0, NEG)
    return m


KL = 8
DMAQ = ("sp", "pool")


class Prog:

    def __init__(self, nc, ctx):
        self.nc = nc
        self.keys = list(ENG) + [q + str(i) for q in DMAQ for i in range(KL)]
        self.sems = {k: ctx.enter_context(nc.semaphore("sem_" + k))
                     for k in self.keys}
        self.cnt = {k: 0 for k in self.keys}
        self.steps = {e: [] for e in ENG}
        self.bar = {k: 0 for k in self.keys}
        self.lane_ptr = {q: 0 for q in DMAQ}

    def do(self, eng, emit, deps=(), dmas=0):
        need = {}
        for k, v in self.bar.items():
            if v > 0:
                need[k] = v
        for d in deps:
            if d is None:
                continue
            k, v = d
            need[k] = max(need.get(k, 0), v)
        if dmas:
            lane = self.lane_ptr[eng]
            self.lane_ptr[eng] = (lane + 1) % KL
            key = eng + str(lane)
            if self.cnt[key] > 0:
                need[key] = max(need.get(key, 0), self.cnt[key])
            self.steps[eng].append((need, emit, dmas, key))
            self.cnt[key] += 16 * dmas
        else:
            key = eng
            self.steps[eng].append((need, emit, 0, key))
            self.cnt[key] += 1
        assert self.cnt[key] < 60000, f"sem overflow risk on {key}"
        return (key, self.cnt[key])

    def barrier(self, exclude_pool=False):
        if exclude_pool:
            keep = {k: v for k, v in self.bar.items() if k.startswith("pool")}
            self.bar = {k: (keep.get(k, 0) if k.startswith("pool") else v)
                        for k, v in self.cnt.items()}
        else:
            self.bar = dict(self.cnt)

    def emit_all(self, block):
        prog = self

        def runner(eng):
            def run(engobj):
                water = {k: 0 for k in prog.keys}
                for need, emit, dmas, key in prog.steps[eng]:
                    for k in prog.keys:
                        v = need.get(k, 0)
                        if v > water[k]:
                            engobj.wait_ge(prog.sems[k], v)
                            water[k] = v
                    if dmas:
                        n_done = [0]

                        def dma(out, in_, **kw):
                            engobj.dma_start(out=out, in_=in_, **kw).then_inc(
                                prog.sems[key], 16)
                            n_done[0] += 1

                        def dmat(out, in_):
                            engobj.dma_start_transpose(out, in_).then_inc(
                                prog.sems[key], 16)
                            n_done[0] += 1
                        emit(engobj, dma, dmat)
                        assert n_done[0] == dmas, \
                            f"{eng}: {n_done[0]} != {dmas} dmas"
                    else:
                        last = emit(engobj)
                        assert last is not None, f"{eng} compute step -> None"
                        last.then_inc(prog.sems[key], 1)
            return run

        block.tensor(runner("pe"))
        block.vector(runner("dve"))
        block.scalar(runner("act"))
        block.gpsimd(runner("pool"))
        block.sync(runner("sp"))

    def check(self):
        ptr = {e: 0 for e in ENG}
        done = {k: 0 for k in self.keys}
        progress = True
        while progress:
            progress = False
            for e in ENG:
                while ptr[e] < len(self.steps[e]):
                    need, _, dmas, key = self.steps[e][ptr[e]]
                    if all(done[k] >= v for k, v in need.items()
                           if not (k == key and dmas)):
                        done[key] += 16 * dmas if dmas else 1
                        ptr[e] += 1
                        progress = True
                    else:
                        break
        stuck = {e: (ptr[e], len(self.steps[e])) for e in ENG
                 if ptr[e] < len(self.steps[e])}
        assert not stuck, f"DEADLOCK at {stuck}; counts={done}"

    def final_wait(self):
        tgt = dict(self.cnt)

        def emit(engobj):
            for k in self.keys:
                if k != "pool" and tgt[k] > 0:
                    engobj.wait_ge(self.sems[k], tgt[k])
            return engobj.nop()
        self.steps["pool"].append(({}, emit, 0, "pool"))
        self.cnt["pool"] += 1


class Rot:
    def __init__(self, bufs):
        self.bufs = bufs
        self.toks = [[] for _ in bufs]
        self.i = 0

    def next(self):
        s = self.i % len(self.bufs)
        self.i += 1
        war = list(self.toks[s])
        self.toks[s] = []

        def done(tok):
            self.toks[s].append(tok)
        return self.bufs[s], war, done


class NS:
    pass


def emit_ln_tile(p, t, xf, lnbf, deps):
    def dve_stats(e):
        last = None
        for i in range(4):
            last = e.bn_stats(t.bn6[:, 6 * i:6 * i + 6],
                              xf[:, 512 * i:512 * (i + 1)])
        return last
    tk0 = p.do("dve", dve_stats, deps=list(deps) + t.ln_prev)
    tk1 = p.do("dve", lambda e: e.bn_aggr(t.mv[:], t.bn6[:]), deps=[tk0])

    tk2a = p.do("act", lambda e: e.activation(
        t.t0[:], t.mv[:, 1:2], AF.Ln, bias=t.c_eps[:], scale=1.0), deps=[tk1])
    tk2 = p.do("act", lambda e: e.activation(
        t.rstd[:], t.t0[:], AF.Exp, bias=0.0, scale=-0.5), deps=[tk2a])

    tk3 = p.do("dve", lambda e: e.tensor_scalar(
        t.negmr[:], t.mv[:, 0:1], t.rstd[:], -1.0, ALU.mult, ALU.mult),
        deps=[tk2])
    tk4 = p.do("act", lambda e: e.activation(
        lnbf[:], xf[:], AF.Identity, bias=t.negmr[:], scale=t.rstd[:]),
        deps=[tk3])
    t.ln_prev = [tk3, tk4]
    return tk4


def emit_transpose16(p, t, src_bf, dst_fn, deps, evac_deps=()):
    toks = []
    ev_extra = list(evac_deps)
    for half in range(2):
        ps, war, done = t.trrot.next()

        def pe_tr(e, half=half, ps=ps):
            last = None
            for i in range(8):
                a = half * 8 + i
                last = e.transpose(ps[:, i * 128:(i + 1) * 128],
                                   src_bf[:, a * 128:(a + 1) * 128],
                                   t.c_ident[:])
            return last
        tk_pe = p.do("pe", pe_tr, deps=list(deps) + war)

        def evac(e, half=half, ps=ps):
            last = None
            for i in range(8):
                a = half * 8 + i
                last = e.tensor_copy(dst_fn(a), ps[:, i * 128:(i + 1) * 128])
            return last
        tk_ev = p.do("dve", evac, deps=[tk_pe] + ev_extra)
        ev_extra = []
        done(tk_ev)
        toks.append(tk_ev)
    return toks


def emit_ln_group(p, t, xin_rows, lnT, gn, lnT_war, xrot, lbrot, lnt_w=512):
    tr_toks = []
    ev_extra = list(lnT_war)
    for tt in range(gn):
        xf, war_x, done_x = xrot.next()
        src = xin_rows[tt]
        tk_x = p.do("sp", lambda e, dma, dmat, src=src, xf=xf: dma(xf[:], src),
                    deps=war_x, dmas=1)
        lnbf, war_l, done_l = lbrot.next()
        tk_ln = emit_ln_tile(p, t, xf, lnbf, [tk_x] + war_l)
        done_x(tk_ln)
        toks = emit_transpose16(
            p, t, lnbf,
            lambda a, tt=tt: lnT[:, a * lnt_w + tt * 128:
                                 a * lnt_w + (tt + 1) * 128],
            [tk_ln], evac_deps=ev_extra)
        ev_extra = []
        done_l(toks[-1])
        tr_toks += toks
    return tr_toks


def emit_const_prep(p, t, hb):
    def ld_consts(e, dma, dmat):
        dma(t.c_ident[:], hb["c_ident"])
        dma(t.c_identf[:], hb["c_identf"])
        dma(t.c_h128[:], hb["c_h128"])
        dma(t.c_bf16[:], hb["c_bf16"])
        dma(t.c_bf64[:], hb["c_bf64"])
        dma(t.c_rope[:], hb["c_rope"])
        dma(t.c_ones[:], hb["c_ones"])
    tk_c = p.do("sp", ld_consts, dmas=7)

    p.do("dve", lambda e: e.memset(t.c_eps[:], EPS))

    def ld_bias(e, dma, dmat):
        dma(t.c_bqkv[:], hb["b_qkv"])
        dma(t.c_bout[:], hb["b_out"])
        dma(t.c_bq[:], hb["bq"])
        dma(t.c_bkv[:], hb["bkv"])
        dma(t.c_bo[:], hb["bo"])
    p.do("pool", ld_bias, dmas=5)

    def ld_vecs(e, dma, dmat):
        dma(t.v_wsug[:], hb["ws_ug"])
        dma(t.v_wsdp[:], hb["ws_dp"])
        dma(t.v_suug[:], hb["su_ug"])
        dma(t.v_sudp[:], hb["su_dp"])
    tk_v = p.do("sp", ld_vecs, dmas=4)

    def pe_t(e):
        e.transpose(t.ps_v0[:, 0:128], t.v_wsug[:], t.c_identf[:])
        e.transpose(t.ps_v1[:, 0:16], t.v_wsdp[0:16, :], t.c_identf[0:16, 0:16])
        e.transpose(t.ps_v1[:, 128:144], t.v_suug[0:16, :],
                    t.c_identf[0:16, 0:16])
        return e.transpose(t.ps_v1[:, 256:320], t.v_sudp[0:64, :],
                           t.c_identf[0:64, 0:64])
    tk_t = p.do("pe", pe_t, deps=[tk_c, tk_v])

    sc2048 = float(2.0 ** (-5.5))
    sc8192 = float(2.0 ** (-6.5))

    def evac(e):
        e.tensor_scalar(t.c_vup[:], t.ps_v0[:, 0:64], sc2048, None, ALU.mult)
        e.tensor_scalar(t.c_vgate[:], t.ps_v0[:, 64:128], sc2048, None,
                        ALU.mult)
        e.tensor_scalar(t.c_vdp[:], t.ps_v1[:, 0:16], sc8192, None, ALU.mult)
        e.tensor_copy(t.c_suug[:], t.ps_v1[:, 128:144])
        return e.tensor_copy(t.c_sudp[:], t.ps_v1[:, 256:320])
    tk_e = p.do("dve", evac, deps=[tk_t])

    def mk_sh(e):
        last = None
        for a in range(DT):
            last = e.tensor_scalar(t.c_sh128[:, a * 128:(a + 1) * 128],
                                   t.c_h128[:], t.c_suug[:, a:a + 1], None,
                                   ALU.mult)
        return last
    p.do("dve", mk_sh, deps=[tk_e])
    p.barrier()


def gen_cache_fold(p, t, src_ap, n_tiles, bfmat, width, out_toks, store_ap,
                   cstep=1024):
    for n in range(n_tiles):
        for c0 in range(0, width, cstep):
            cw = min(cstep, width - c0)
            wb, war_w, done_w = t.fbrot.next()
            ob, war_o, done_o = t.forot.next()
            tk_l = p.do("pool", lambda e, dma, dmat, n=n, c0=c0, cw=cw, wb=wb:
                        dma(wb[:, 0:cw],
                            src_ap[n * 128:(n + 1) * 128, c0:c0 + cw]),
                        deps=war_w, dmas=1)
            evs = []
            for ci in range((cw + 511) // 512):
                d0 = ci * 512
                dw = min(512, cw - d0)
                ps, war_p, done_p = t.fprot.next()
                tk_m = p.do("pe", lambda e, ps=ps, d0=d0, dw=dw, wb=wb:
                            e.matmul(ps[:, 0:dw], bfmat[:],
                                     wb[:, d0:d0 + dw], start=True, stop=True),
                            deps=[tk_l] + war_p)
                tk_e = p.do("dve", lambda e, ps=ps, d0=d0, dw=dw, ob=ob:
                            e.tensor_copy(ob[:, d0:d0 + dw], ps[:, 0:dw]),
                            deps=[tk_m] + war_o)
                war_o = []
                done_p(tk_e)
                evs.append(tk_e)
            done_w(evs[-1])
            dest, src = store_ap(n, c0, cw, ob)
            tk_s = p.do("pool", lambda e, dma, dmat, dest=dest, src=src:
                        dma(dest, src), deps=evs, dmas=1)
            done_o(tk_s)
            out_toks.append(tk_s)
            yield


def pump(t, k):
    while k > 0 and t.bg:
        try:
            next(t.bg[0])
            k -= 1
        except StopIteration:
            t.bg.pop(0)


def emit_attn_a_kv(p, t, hb, sc):
    def ldw(e, dma, dmat):
        for a in range(DT):
            dma(t.ak_wk[:, a * 2048:(a + 1) * 2048],
                hb["w_qkvT"][a * 128:(a + 1) * 128, 2048:4096])
            dma(t.ak_wv[:, a * 2048:(a + 1) * 2048],
                hb["w_qkvT"][a * 128:(a + 1) * 128, 4096:6144])
    tk_w = p.do("pool", ldw, dmas=2 * DT)

    for g in range(16):
        pump(t, 6)
        lnT, war_ln, done_ln = t.ak_lnrot.next()
        rows = [hb["xb"][(g * 2 + tt) * 128:(g * 2 + tt + 1) * 128, :]
                for tt in range(2)]
        tr_toks = emit_ln_group(p, t, rows, lnT, 2, war_ln,
                                t.ak_xrot, t.ak_lbrot, lnt_w=256)
        last_k = None
        for ot in range(DT):
            ps, war_p, done_p = t.ak_psrot.next()

            def mmk(e, ot=ot, ps=ps, lnT=lnT):
                for a in range(DT):
                    e.matmul(ps[:, 0:256],
                             t.ak_wk[:, a * 2048 + ot * 128:
                                     a * 2048 + (ot + 1) * 128],
                             lnT[:, a * 256:(a + 1) * 256],
                             start=(a == 0), stop=False)
                return e.matmul(ps[:, 0:256],
                                t.c_bqkv[0:1, 2048 + ot * 128:
                                         2048 + (ot + 1) * 128],
                                t.c_ones[0:1, 0:256], start=False, stop=True)
            tk_m = p.do("pe", mmk, deps=tr_toks + [tk_w] + war_p)
            kb, war_k, done_k = t.ak_krot.next()
            tk_e = p.do("act", lambda e, kb=kb, ps=ps: e.copy(
                kb[:, 0:256], ps[:, 0:256]), deps=[tk_m] + war_k)
            done_p(tk_e)
            tk_s = p.do("sp", lambda e, dma, dmat, ot=ot, g=g, kb=kb: dma(
                sc["KT"][ot * 128:(ot + 1) * 128, g * 256:(g + 1) * 256],
                kb[:, 0:256]), deps=[tk_e], dmas=1)
            done_k(tk_s)
            last_k = tk_s
        last_v = None
        for tt in range(2):
            vb, war_v, done_v = t.ak_vrot.next()
            ev_toks = []
            for oc in range(4):
                ps, war_p, done_p = t.ak_psrot.next()

                def mmv(e, tt=tt, oc=oc, ps=ps, lnT=lnT):
                    for a in range(DT):
                        e.matmul(ps[:, 0:512],
                                 lnT[:, a * 256 + tt * 128:
                                     a * 256 + (tt + 1) * 128],
                                 t.ak_wv[:, a * 2048 + oc * 512:
                                         a * 2048 + (oc + 1) * 512],
                                 start=(a == 0), stop=False)
                    return e.matmul(ps[:, 0:512], t.c_ones[0:1, 0:128],
                                    t.c_bqkv[0:1, 4096 + oc * 512:
                                             4096 + (oc + 1) * 512],
                                    start=False, stop=True)
                tk_m = p.do("pe", mmv, deps=tr_toks + [tk_w] + war_p)
                tk_e = p.do("act", lambda e, oc=oc, vb=vb, ps=ps: e.copy(
                    vb[:, oc * 512:(oc + 1) * 512], ps[:, 0:512]),
                    deps=[tk_m] + war_v)
                war_v = []
                done_p(tk_e)
                ev_toks.append(tk_e)
            tile = g * 2 + tt
            tk_s = p.do("sp", lambda e, dma, dmat, tile=tile, vb=vb: dma(
                sc["VV"][tile * 128:(tile + 1) * 128, :], vb[:]),
                deps=ev_toks, dmas=1)
            done_v(tk_s)
            last_v = tk_s
        done_ln(last_k)
        done_ln(last_v)
    p.barrier(exclude_pool=True)


def emit_attn_a_q(p, t, hb, sc):
    def ldwq(e, dma, dmat):
        for a in range(DT):
            dma(t.aq_wq[:, a * 2048:(a + 1) * 2048],
                hb["w_qkvT"][a * 128:(a + 1) * 128, 0:2048])
    tk_w = p.do("pool", ldwq, dmas=DT)

    qgroups = [(0, 4), (4, 4), (8, 4), (12, 4), (16, 4), (20, 1)]
    for (g0, gn) in qgroups:
        pump(t, 16)
        gw = gn * 128
        lnT, war_ln, done_ln = t.aq_lnrot.next()
        rows = [hb["xs"][(g0 + tt) * 128:(g0 + tt + 1) * 128, :]
                for tt in range(gn)]
        tr_toks = emit_ln_group(p, t, rows, lnT, gn, war_ln,
                                t.ak_xrot, t.ak_lbrot)
        last_q = None
        for ot in range(DT):
            ps, war_p, done_p = t.ak_psrot.next()

            def mmq(e, ot=ot, gw=gw, ps=ps, lnT=lnT):
                for a in range(DT):
                    e.matmul(ps[:, 0:gw],
                             t.aq_wq[:, a * 2048 + ot * 128:
                                     a * 2048 + (ot + 1) * 128],
                             lnT[:, a * 512:a * 512 + gw],
                             start=(a == 0), stop=False)
                return e.matmul(ps[:, 0:gw],
                                t.c_bqkv[0:1, ot * 128:(ot + 1) * 128],
                                t.c_ones[0:1, 0:gw], start=False, stop=True)
            tk_m = p.do("pe", mmq, deps=tr_toks + [tk_w] + war_p)
            kb, war_k, done_k = t.ak_krot.next()
            tk_e = p.do("act", lambda e, kb=kb, gw=gw, ps=ps: e.copy(
                kb[:, 0:gw], ps[:, 0:gw]), deps=[tk_m] + war_k)
            done_p(tk_e)
            tk_s = p.do("sp", lambda e, dma, dmat, ot=ot, g0=g0, gw=gw, kb=kb:
                        dma(sc["QT"][ot * 128:(ot + 1) * 128,
                                     g0 * 128:g0 * 128 + gw],
                            kb[:, 0:gw]), deps=[tk_e], dmas=1)
            done_k(tk_s)
            last_q = tk_s
        done_ln(last_q)
    p.barrier(exclude_pool=True)


def emit_attn_a_flash(p, t, sc):
    fgroups = [(0, 5), (5, 4), (9, 4), (13, 4), (17, 4)]
    g_war = []
    qt_r = t.fl_un.rearrange("p (a c) -> p a c", a=DT)
    for (g0, gn) in fgroups:
        gw = gn * 128

        def ldq(e, dma, dmat, g0=g0, gw=gw):
            dma(qt_r[:, :, 0:gw],
                sc["QT"].rearrange("(a p) c -> p a c", p=128)[
                    :, :, g0 * 128:g0 * 128 + gw])
        tk_q = p.do("sp", ldq, deps=t.fl_qt_war, dmas=1)
        t.fl_qt_war = []

        exp_toks = []
        for kc in range(8):
            pump(t, 3)
            ktc, war_kt, done_kt = t.fl_kvrot.next()

            def ldk(e, dma, dmat, kc=kc, ktc=ktc):
                dma(ktc.rearrange("p (a c) -> p a c", a=DT),
                    sc["KT"].rearrange("(a p) c -> p a c", p=128)[
                        :, :, kc * 512:(kc + 1) * 512])
            tk_k = p.do("sp", ldk, deps=war_kt + g_war + [tk_q], dmas=1)
            for qi in range(gn):
                ps, war_p, done_p = t.ak_psrot.next()

                def mms(e, qi=qi, ps=ps, ktc=ktc):
                    last = None
                    for a in range(DT):
                        last = e.matmul(
                            ps[:, 0:512],
                            t.fl_un[:, a * QT_W + qi * 128:
                                    a * QT_W + (qi + 1) * 128],
                            ktc[:, a * 512:(a + 1) * 512],
                            start=(a == 0), stop=(a == DT - 1))
                    return last
                tk_m = p.do("pe", mms, deps=[tk_q, tk_k] + war_p)
                col = qi * 8 + kc
                tk_x = p.do("dve", lambda e, col=col, ps=ps: e.reduce_max(
                    t.fl_mx[:, col:col + 1], ps[:, 0:512], AX.X),
                    deps=[tk_m] + g_war)
                tk_b = p.do("dve", lambda e, col=col: e.tensor_scalar(
                    t.fl_bb[:, col:col + 1], t.fl_mx[:, col:col + 1],
                    -SC_A, None, ALU.mult), deps=[tk_x])
                tk_e = p.do("act", lambda e, qi=qi, kc=kc, col=col, ps=ps:
                            e.activation(
                    t.fl_probs[:, qi * 4096 + kc * 512:
                               qi * 4096 + (kc + 1) * 512],
                    ps[:, 0:512], AF.Exp, bias=t.fl_bb[:, col:col + 1],
                    scale=SC_A, accum_out=t.fl_sm[:, col:col + 1]),
                    deps=[tk_b] + g_war)
                done_p(tk_e)
                exp_toks.append(tk_e)
            done_kt(exp_toks[-1])
        g_war = []

        stat_toks = []
        w_toks = []
        for qi in range(gn):
            tk_g = p.do("dve", lambda e, qi=qi: e.reduce_max(
                t.fl_gmx[:, qi:qi + 1], t.fl_mx[:, qi * 8:(qi + 1) * 8], AX.X),
                deps=exp_toks)
            tk_nb = p.do("dve", lambda e, qi=qi: e.tensor_scalar(
                t.fl_gnb[:, qi:qi + 1], t.fl_gmx[:, qi:qi + 1], -SC_A, None,
                ALU.mult), deps=[tk_g])
            tk_rf = p.do("act", lambda e, qi=qi: e.activation(
                t.fl_rf[:, qi * 8:(qi + 1) * 8],
                t.fl_mx[:, qi * 8:(qi + 1) * 8],
                AF.Exp, bias=t.fl_gnb[:, qi:qi + 1], scale=SC_A),
                deps=[tk_nb])
            tk_lw = p.do("dve", lambda e, qi=qi: e.tensor_mul(
                t.fl_lw[:, qi * 8:(qi + 1) * 8],
                t.fl_sm[:, qi * 8:(qi + 1) * 8],
                t.fl_rf[:, qi * 8:(qi + 1) * 8]), deps=[tk_rf])
            tk_ls = p.do("dve", lambda e, qi=qi: e.reduce_sum(
                t.fl_l[:, qi:qi + 1], t.fl_lw[:, qi * 8:(qi + 1) * 8], AX.X),
                deps=[tk_lw])
            tk_l = p.do("dve", lambda e, qi=qi: e.reciprocal(
                t.fl_rl[:, qi:qi + 1], t.fl_l[:, qi:qi + 1]), deps=[tk_ls])
            stat_toks.append(tk_l)
            tk_w = p.do("dve", lambda e, qi=qi: e.tensor_scalar(
                t.fl_w[:, qi * 8:(qi + 1) * 8],
                t.fl_rf[:, qi * 8:(qi + 1) * 8],
                t.fl_rl[:, qi:qi + 1], None, ALU.mult), deps=[tk_l])
            w_toks.append(tk_w)

        last_cm = None
        for vh in range(2):
            pt_toks = {}
            for qi in range(gn):
                ptq = t.fl_un[:, qi * 2048:(qi + 1) * 2048]
                toks = []
                war0 = t.fl_ptq_war.pop(qi, [])
                for kt in range(16):
                    kk = vh * 16 + kt
                    kc = kk // 4
                    sb_sc, war_s, done_s = t.fl_scrot.next()
                    tk_s2 = p.do("dve", lambda e, qi=qi, kk=kk, kc=kc,
                                 sb_sc=sb_sc: e.tensor_scalar(
                        sb_sc[:], t.fl_probs[:, qi * 4096 + kk * 128:
                                             qi * 4096 + (kk + 1) * 128],
                        t.fl_w[:, qi * 8 + kc:qi * 8 + kc + 1], None,
                        ALU.mult), deps=[w_toks[qi]] + war_s)
                    t.fl_probs_war.append(tk_s2)
                    psd, war_pd, done_pd = t.ak_psrot.next()
                    tk_tm = p.do("pe", lambda e, sb_sc=sb_sc, psd=psd:
                                 e.matmul(psd[:, 0:128], sb_sc[:],
                                          t.c_ident[:], start=True,
                                          stop=True),
                                 deps=[tk_s2] + war_pd)
                    done_s(tk_tm)
                    tk_pte = p.do("dve", lambda e, kt=kt, psd=psd, ptq=ptq:
                                  e.tensor_copy(
                        ptq[:, kt * 128:(kt + 1) * 128], psd[:, 0:128]),
                        deps=[tk_tm] + war0)
                    war0 = []
                    done_pd(tk_pte)
                    toks.append(tk_pte)
                pt_toks[qi] = toks
            for dh in range(2):
                def ldv(e, dma, dmat, vh=vh, dh=dh):
                    dma(t.fl_kv.rearrange("p (a c) -> p a c", a=DT),
                        sc["VV"][vh * 2048:(vh + 1) * 2048,
                                 dh * 1024:(dh + 1) * 1024].rearrange(
                            "(a p) c -> p a c", p=128))
                tk_v = p.do("sp", ldv,
                            deps=t.fl_kv_war + [exp_toks[-1]], dmas=1)
                t.fl_kv_war = []
                for qi in range(gn):
                    ptq = t.fl_un[:, qi * 2048:(qi + 1) * 2048]

                    def mmc(e, qi=qi, ptq=ptq):
                        last = None
                        for kt in range(16):
                            for dc in range(2):
                                last = e.matmul(
                                    t.fl_psctx[:, dc * 512:(dc + 1) * 512],
                                    ptq[:, kt * 128:(kt + 1) * 128],
                                    t.fl_kv[:, kt * 1024 + dc * 512:
                                            kt * 1024 + (dc + 1) * 512],
                                    start=(kt == 0), stop=(kt == 15))
                        return last
                    tk_cm = p.do("pe", mmc, deps=pt_toks[qi] + [tk_v] +
                                 t.fl_ctx_war)
                    t.fl_ctx_war = []
                    t.fl_kv_war.append(tk_cm)
                    t.fl_ptq_war.setdefault(qi, []).append(tk_cm)
                    last_cm = tk_cm
                    par = t.fl_par[qi]
                    if vh == 0:
                        tk_ev = p.do("act", lambda e, dh=dh, par=par: e.copy(
                            par[:, dh * 1024:(dh + 1) * 1024], t.fl_psctx[:]),
                            deps=[tk_cm] + t.fl_par_war.pop(qi, []))
                        t.fl_ctx_war.append(tk_ev)
                    else:

                        ctxh, war_ch, done_ch = t.fl_chrot.next()
                        tk_ev = p.do("dve", lambda e, dh=dh, par=par,
                                     ctxh=ctxh: e.tensor_tensor(
                            ctxh[:], par[:, dh * 1024:(dh + 1) * 1024],
                            t.fl_psctx[:], ALU.add),
                            deps=[tk_cm] + war_ch)
                        t.fl_par_war.setdefault(qi, []).append(tk_ev)
                        t.fl_ctx_war.append(tk_ev)
                        ps_tr, war_tr, done_tr = t.trrot.next()

                        def tr8(e, ctxh=ctxh, ps_tr=ps_tr):
                            last = None
                            for i in range(8):
                                last = e.transpose(
                                    ps_tr[:, i * 128:(i + 1) * 128],
                                    ctxh[:, i * 128:(i + 1) * 128],
                                    t.c_ident[:])
                            return last
                        tk_tr = p.do("pe", tr8, deps=[tk_ev] + war_tr)
                        done_ch(tk_tr)
                        cth, war_ct, done_ct = t.fl_ctrot.next()
                        tk_te = p.do("dve", lambda e, cth=cth, ps_tr=ps_tr:
                                     e.tensor_copy(cth[:], ps_tr[:, 0:1024]),
                                     deps=[tk_tr] + war_ct)
                        done_tr(tk_te)
                        slot = g0 + qi
                        tk_ct = p.do("sp", lambda e, dma, dmat, slot=slot,
                                     dh=dh, cth=cth: dma(
                            sc["CT"][slot * 128:(slot + 1) * 128,
                                     dh * 1024:(dh + 1) * 1024].rearrange(
                                "p (a c) -> p a c", a=8),
                            cth.rearrange("p (a c) -> p a c", a=8)),
                            deps=[tk_te], dmas=1)
                        done_ct(tk_ct)
        g_war = list(t.fl_probs_war) + [last_cm]
        t.fl_probs_war = []
        t.fl_qt_war = [last_cm]
    p.barrier(exclude_pool=True)


def emit_attn_a_out(p, t, hb, sc):
    def ldwo(e, dma, dmat):
        for a in range(DT):
            dma(t.ao_wo[:, a * 2048:(a + 1) * 2048],
                hb["w_outT"][a * 128:(a + 1) * 128, :])
    tk_w = p.do("pool", ldwo, dmas=DT)

    for slot in range(NT_S12):
        pump(t, 8)
        ctb, war_ct, done_ct = t.ao_ctrot.next()
        tk_c = p.do("sp", lambda e, dma, dmat, slot=slot, ctb=ctb: dma(
            ctb[:], sc["CT"][slot * 128:(slot + 1) * 128, :]),
            deps=war_ct, dmas=1)
        xrb, war_x, done_x = t.ak_xrot.next()
        tk_x = p.do("sp", lambda e, dma, dmat, slot=slot, xrb=xrb: dma(
            xrb[:], hb["xs"][slot * 128:(slot + 1) * 128, :]),
            deps=war_x, dmas=1)
        add_toks = []
        for oc in range(4):
            ps, war_p, done_p = t.ak_psrot.next()

            def mmo(e, oc=oc, ps=ps, ctb=ctb):
                for a in range(DT):
                    e.matmul(ps[:, 0:512], ctb[:, a * 128:(a + 1) * 128],
                             t.ao_wo[:, a * 2048 + oc * 512:
                                     a * 2048 + (oc + 1) * 512],
                             start=(a == 0), stop=False)
                return e.matmul(ps[:, 0:512], t.c_ones[0:1, 0:128],
                                t.c_bout[0:1, oc * 512:(oc + 1) * 512],
                                start=False, stop=True)
            tk_m = p.do("pe", mmo, deps=[tk_c, tk_w] + war_p)
            tk_a = p.do("dve", lambda e, oc=oc, ps=ps, xrb=xrb:
                        e.tensor_tensor(
                t.ao_xnew[:, oc * 512:(oc + 1) * 512],
                xrb[:, oc * 512:(oc + 1) * 512], ps[:, 0:512], ALU.add),
                deps=[tk_m, tk_x] + t.ao_xnew_war)
            t.ao_xnew_war = []
            done_p(tk_a)
            add_toks.append(tk_a)
        done_x(add_toks[-1])
        done_ct(add_toks[-1])
        tk_o = p.do("sp", lambda e, dma, dmat, slot=slot: dma(
            sc["XR1"][slot * 128:(slot + 1) * 128, :], t.ao_xnew[:]),
            deps=add_toks, dmas=1)
        t.ao_xnew_war.append(tk_o)
    p.barrier(exclude_pool=True)


def emit_ffn(p, nc, ctx_outer, t, sc, xin, xout, n_tiles, tag):
    N = n_tiles * 128
    with ExitStack() as c2:
        sb2 = lambda n, s, d: c2.enter_context(
            nc.sbuf_tensor("sb_" + tag + n, s, d))
        pst2 = lambda n, s, d: c2.enter_context(nc.psum_tensor(tag + n, s, d))
        lnrot = Rot([sb2("fu_ln0", [128, DT * 512], BF16)])
        xfT = sb2("fu_xfT", [128, DT * N], BF16)
        wrot_u = Rot([sb2("fu_wu0", [128, DT * 256], BF16),
                      sb2("fu_wu1", [128, DT * 256], BF16)])
        wrot_g = Rot([sb2("fu_wg0", [128, DT * 256], BF16),
                      sb2("fu_wg1", [128, DT * 256], BF16)])
        psu = Rot([pst2("fu_psu0", [128, 512], F32),
                   pst2("fu_psu1", [128, 512], F32)])
        psg = Rot([pst2("fu_psg0", [128, 512], F32),
                   pst2("fu_psg1", [128, 512], F32)])
        uprot = Rot([sb2("fu_up0", [128, 512], BF16),
                     sb2("fu_up1", [128, 512], BF16)])
        gtrot = Rot([sb2("fu_gt0", [128, 512], BF16),
                     sb2("fu_gt1", [128, 512], BF16)])
        hrot = Rot([sb2("fu_h0", [128, 512], BF16),
                    sb2("fu_h1", [128, 512], BF16)])
        urot = Rot([sb2("fu_u0", [128, 512], BF16),
                    sb2("fu_u1", [128, 512], BF16)])

        xft_toks = []
        pgroups = [(i * 4, min(4, n_tiles - i * 4))
                   for i in range((n_tiles + 3) // 4)]
        for (g0, gn) in pgroups:
            gw = gn * 128
            lnT, war_ln, done_ln = lnrot.next()
            rows = [xin[(g0 + tt) * 128:(g0 + tt + 1) * 128, :]
                    for tt in range(gn)]
            tr_toks = emit_ln_group(p, t, rows, lnT, gn, war_ln,
                                    t.ak_xrot, t.ak_lbrot)
            last = None
            for a in range(DT):
                ps, war_p, done_p = t.ak_psrot.next()
                tk_m = p.do("pe", lambda e, a=a, gw=gw, ps=ps, lnT=lnT:
                            e.matmul(
                    ps[:, 0:gw], t.c_sh128[:, a * 128:(a + 1) * 128],
                    lnT[:, a * 512:a * 512 + gw], start=True, stop=True),
                    deps=tr_toks + war_p)
                tk_e = p.do("dve", lambda e, a=a, g0=g0, gw=gw, ps=ps:
                            e.tensor_copy(
                    xfT[:, a * N + g0 * 128:a * N + g0 * 128 + gw],
                    ps[:, 0:gw]), deps=[tk_m])
                done_p(tk_e)
                last = tk_e
            done_ln(last)
            xft_toks.append(last)

        tchunks = []
        c0 = 0
        while c0 < N:
            tchunks.append((c0, min(512, N - c0)))
            c0 += 512
        for c in range(32):
            wub, war_wu, done_wu = wrot_u.next()
            wgb, war_wg, done_wg = wrot_g.next()

            def ldw(e, dma, dmat, c=c, wub=wub, wgb=wgb):
                dma(wub[:], sc["C_up"][c * 128:(c + 1) * 128, :])
                dma(wgb[:], sc["C_gate"][c * 128:(c + 1) * 128, :])
            tk_w = p.do("sp", ldw, deps=war_wu + war_wg + t.fold_up +
                        t.fold_gate, dmas=2)

            last_store = None
            for ot in range(2):
                og = c * 2 + ot
                for (t0, tw) in tchunks:
                    pu, war_pu, done_pu = psu.next()
                    pg, war_pg, done_pg = psg.next()

                    def mmu(e, ot=ot, t0=t0, tw=tw, wub=wub, pu=pu):
                        last = None
                        for a in range(DT):
                            last = e.matmul(
                                pu[:, 0:tw],
                                wub[:, a * 256 + ot * 128:
                                    a * 256 + (ot + 1) * 128],
                                xfT[:, a * N + t0:a * N + t0 + tw],
                                start=(a == 0), stop=(a == DT - 1))
                        return last
                    tk_mu = p.do("pe", mmu, deps=[tk_w] + xft_toks + war_pu)

                    def mmg(e, ot=ot, t0=t0, tw=tw, wgb=wgb, pg=pg):
                        last = None
                        for a in range(DT):
                            last = e.matmul(
                                pg[:, 0:tw],
                                wgb[:, a * 256 + ot * 128:
                                    a * 256 + (ot + 1) * 128],
                                xfT[:, a * N + t0:a * N + t0 + tw],
                                start=(a == 0), stop=(a == DT - 1))
                        return last
                    tk_mg = p.do("pe", mmg, deps=[tk_w] + xft_toks + war_pg)

                    upb, war_ub, done_ub = uprot.next()
                    gtb, war_gb, done_gb = gtrot.next()
                    tk_eu = p.do("dve", lambda e, og=og, tw=tw, pu=pu,
                                 upb=upb: e.tensor_scalar(
                        upb[:, 0:tw], pu[:, 0:tw], t.c_vup[:, og:og + 1],
                        None, ALU.mult), deps=[tk_mu] + war_ub)
                    done_pu(tk_eu)
                    tk_eg = p.do("act", lambda e, og=og, tw=tw, pg=pg,
                                 gtb=gtb: e.activation(
                        gtb[:, 0:tw], pg[:, 0:tw], AF.Silu, bias=0.0,
                        scale=t.c_vgate[:, og:og + 1]),
                        deps=[tk_mg] + war_gb)
                    done_pg(tk_eg)
                    hb_, war_h, done_h = hrot.next()
                    tk_h = p.do("dve", lambda e, og=og, tw=tw, hb_=hb_,
                                gtb=gtb, upb=upb: e.scalar_tensor_tensor(
                        hb_[:, 0:tw], gtb[:, 0:tw], t.c_sudp[:, og:og + 1],
                        upb[:, 0:tw], ALU.mult, ALU.mult),
                        deps=[tk_eu, tk_eg] + war_h)
                    done_ub(tk_h)
                    done_gb(tk_h)
                    pm, war_pm, done_pm = t.ak_psrot.next()
                    tk_mm = p.do("pe", lambda e, tw=tw, hb_=hb_, pm=pm:
                                 e.matmul(pm[:, 0:tw], t.c_h128[:],
                                          hb_[:, 0:tw], start=True, stop=True),
                                 deps=[tk_h] + war_pm)
                    done_h(tk_mm)
                    ub, war_u, done_u = urot.next()
                    tk_ue = p.do("act", lambda e, tw=tw, pm=pm, ub=ub: e.copy(
                        ub[:, 0:tw], pm[:, 0:tw]), deps=[tk_mm] + war_u)
                    done_pm(tk_ue)
                    tk_us = p.do("sp", lambda e, dma, dmat, og=og, t0=t0,
                                 tw=tw, ub=ub: dma(
                        sc["US"][og * 128:(og + 1) * 128, t0:t0 + tw],
                        ub[:, 0:tw]), deps=[tk_ue], dmas=1)
                    done_u(tk_us)
                    last_store = tk_us
            done_wu(last_store)
            done_wg(last_store)
        p.barrier()

    with ExitStack() as c2:
        sb2 = lambda n, s, d: c2.enter_context(
            nc.sbuf_tensor("sb_" + tag + n, s, d))
        pst2 = lambda n, s, d: c2.enter_context(nc.psum_tensor(tag + n, s, d))
        usb = sb2("fd_usb", [128, IT * USB_W], BF16)
        wdrot = Rot([sb2("fd_wd0", [128, IT * 128], BF16),
                     sb2("fd_wd1", [128, IT * 128], BF16)])
        psd = Rot([pst2("fd_psd0", [128, 512], F32),
                   pst2("fd_psd1", [128, 512], F32)])
        yT = sb2("fd_yT", [128, DT * USB_W], BF16)
        xnew = sb2("fd_xnew", [128, D], F32)
        usb_war = []
        xnew_war = []

        if n_tiles == 21:
            dgroups = [(0, 5), (5, 4), (9, 4), (13, 4), (17, 4)]
        else:
            dgroups = [(0, 4), (4, 4), (8, 4), (12, 4)]
        for (g0, gn) in dgroups:
            gN = gn * 128

            def ldu(e, dma, dmat, g0=g0, gN=gN):
                dma(usb.rearrange("p (a c) -> p a c", a=IT)[:, :, 0:gN],
                    sc["US"].rearrange("(a p) c -> p a c", p=128)[
                        :, :, g0 * 128:g0 * 128 + gN])
            tk_u = p.do("sp", ldu, deps=usb_war, dmas=1)
            usb_war = []

            y_toks = []
            for ot in range(DT):
                wdb, war_wd, done_wd = wdrot.next()

                def ldwd(e, dma, dmat, ot=ot, wdb=wdb):
                    dma(wdb[:], sc["C_down"][ot * 128:(ot + 1) * 128, :])
                tk_wd = p.do("sp", ldwd, deps=war_wd + t.fold_down, dmas=1)
                for ci in range((gN + 511) // 512):
                    c0 = ci * 512
                    cw = min(512, gN - c0)
                    ps, war_p, done_p = psd.next()

                    def mmd(e, c0=c0, cw=cw, wdb=wdb, ps=ps):
                        last = None
                        for a in range(IT):
                            last = e.matmul(
                                ps[:, 0:cw], wdb[:, a * 128:(a + 1) * 128],
                                usb[:, a * USB_W + c0:a * USB_W + c0 + cw],
                                start=(a == 0), stop=(a == IT - 1))
                        return last
                    tk_md = p.do("pe", mmd, deps=[tk_wd, tk_u] + war_p)
                    tk_ed = p.do("act", lambda e, ot=ot, c0=c0, cw=cw, ps=ps:
                                 e.activation(
                        yT[:, ot * USB_W + c0:ot * USB_W + c0 + cw],
                        ps[:, 0:cw], AF.Identity, bias=0.0,
                        scale=t.c_vdp[:, ot:ot + 1]), deps=[tk_md])
                    done_p(tk_ed)
                    done_wd(tk_ed)
                    y_toks.append(tk_ed)
            last_o = None
            for i in range(gn):
                tile = g0 + i
                xrb, war_x, done_x = t.ak_xrot.next()
                tk_x = p.do("sp", lambda e, dma, dmat, tile=tile, xrb=xrb: dma(
                    xrb[:], xin[tile * 128:(tile + 1) * 128, :]),
                    deps=war_x, dmas=1)
                add_toks = []
                for half in range(2):
                    ps, war_p, done_p = t.trrot.next()

                    def tr(e, i=i, half=half, ps=ps):
                        last = None
                        for k in range(8):
                            a = half * 8 + k
                            last = e.transpose(
                                ps[:, k * 128:(k + 1) * 128],
                                yT[:, a * USB_W + i * 128:
                                   a * USB_W + (i + 1) * 128],
                                t.c_ident[:])
                        return last
                    tk_tr = p.do("pe", tr, deps=y_toks + war_p)

                    def addres(e, half=half, ps=ps, xrb=xrb):
                        last = None
                        for k in range(2):
                            cc = half * 1024 + k * 512
                            last = e.tensor_tensor(
                                xnew[:, cc:cc + 512], xrb[:, cc:cc + 512],
                                ps[:, k * 512:(k + 1) * 512], ALU.add)
                        return last
                    tk_a = p.do("dve", addres,
                                deps=[tk_tr, tk_x] + xnew_war)
                    xnew_war = []
                    done_p(tk_a)
                    add_toks.append(tk_a)
                done_x(add_toks[-1])
                tk_o = p.do("sp", lambda e, dma, dmat, tile=tile: dma(
                    xout[tile * 128:(tile + 1) * 128, :], xnew[:]),
                    deps=add_toks, dmas=1)
                xnew_war.append(tk_o)
                last_o = tk_o
            usb_war.append(last_o)
        p.barrier()


def emit_kvrope(p, t, hb, sc):
    def ldcs(e, dma, dmat):
        dma(t.kr_cos[:], hb["cosq"])
        dma(t.kr_sin[:], hb["sinq"])
    tk_cs = p.do("sp", ldcs, dmas=2)

    def ldw(e, dma, dmat):
        for a in range(DT):
            dma(t.kr_wq[:, a * 2048:(a + 1) * 2048],
                hb["wq"][a * 128:(a + 1) * 128, :])
            dma(t.kr_wkv[:, a * 1024:(a + 1) * 1024],
                hb["wkv"][a * 128:(a + 1) * 128, :])
    tk_w = p.do("pool", ldw, dmas=2 * DT)

    def rope_block(src_bf, gw, g0, outbf, deps):
        cos_sl = t.kr_cos[:, g0 * 128:g0 * 128 + gw]
        sin_sl = t.kr_sin[:, g0 * 128:g0 * 128 + gw]
        psr, war_r, done_r = t.kr_psr.next()
        tk_r = p.do("pe", lambda e: e.matmul(
            psr[:, 0:gw], t.c_rope[:], src_bf, start=True, stop=True),
            deps=deps + war_r)
        tb, war_t, done_t = t.kr_t1rot.next()
        tk_1 = p.do("dve", lambda e: e.tensor_tensor(
            tb[:, 0:gw], psr[:, 0:gw], sin_sl, ALU.mult),
            deps=[tk_r] + war_t)
        done_r(tk_1)
        tk_2 = p.do("dve", lambda e: e.tensor_tensor(
            t.kr_t2[:, 0:gw], src_bf, cos_sl, ALU.mult),
            deps=list(deps) + t.kr_t2_war)
        tk_3 = p.do("dve", lambda e: e.tensor_tensor(
            outbf, tb[:, 0:gw], t.kr_t2[:, 0:gw], ALU.add),
            deps=[tk_1, tk_2])
        t.kr_t2_war = [tk_3]
        done_t(tk_3)
        return tk_3

    kgroups = [(0, 4), (4, 4), (8, 4), (12, 4), (16, 4), (20, 1)]
    for (g0, gn) in kgroups:
        gw = gn * 128
        own = (0 < g0 < 20)
        lnT, war_ln, done_ln = t.kr_lnrot.next()
        rows = [sc["XR2"][(g0 + tt) * 128:(g0 + tt + 1) * 128, :]
                for tt in range(gn)]
        tr_toks = emit_ln_group(p, t, rows, lnT, gn, war_ln,
                                t.ak_xrot, t.ak_lbrot)
        last_tok = None
        for ot in range(4):
            ps, war_p, done_p = t.ak_psrot.next()

            def mmk(e, ot=ot, gw=gw, ps=ps, lnT=lnT):
                for a in range(DT):
                    e.matmul(ps[:, 0:gw],
                             t.kr_wkv[:, a * 1024 + ot * 128:
                                      a * 1024 + (ot + 1) * 128],
                             lnT[:, a * 512:a * 512 + gw],
                             start=(a == 0), stop=False)
                return e.matmul(ps[:, 0:gw],
                                t.c_bkv[0:1, ot * 128:(ot + 1) * 128],
                                t.c_ones[0:1, 0:gw], start=False, stop=True)
            tk_m = p.do("pe", mmk, deps=tr_toks + [tk_w] + war_p)
            kb, war_k, done_k = t.ak_krot.next()
            tk_e = p.do("act", lambda e, gw=gw, kb=kb, ps=ps: e.copy(
                kb[:, 0:gw], ps[:, 0:gw]), deps=[tk_m] + war_k)
            done_p(tk_e)
            rob, war_ro, done_ro = t.kr_rorot.next()
            tk_ro = rope_block(kb[:, 0:gw], gw, g0, rob[:, 0:gw],
                               [tk_e, tk_cs] + war_ro)
            done_k(tk_ro)
            tk_s = p.do("sp", lambda e, dma, dmat, ot=ot, g0=g0, gw=gw,
                        rob=rob: dma(
                sc["K3T"][ot * 128:(ot + 1) * 128, g0 * 128:g0 * 128 + gw],
                rob[:, 0:gw]), deps=[tk_ro], dmas=1)
            done_ro(tk_s)
            last_tok = tk_s
        for tt in range(gn):
            slot = g0 + tt
            ps, war_p, done_p = t.ak_psrot.next()

            def mmv(e, tt=tt, ps=ps, lnT=lnT):
                for a in range(DT):
                    e.matmul(ps[:, 0:512],
                             lnT[:, a * 512 + tt * 128:
                                 a * 512 + (tt + 1) * 128],
                             t.kr_wkv[:, a * 1024 + 512:(a + 1) * 1024],
                             start=(a == 0), stop=False)
                return e.matmul(ps[:, 0:512], t.c_ones[0:1, 0:128],
                                t.c_bkv[0:1, 512:1024], start=False, stop=True)
            tk_m = p.do("pe", mmv, deps=tr_toks + [tk_w] + war_p)
            kb, war_k, done_k = t.ak_krot.next()
            tk_e = p.do("act", lambda e, kb=kb, ps=ps: e.copy(
                kb[:, 0:512], ps[:, 0:512]), deps=[tk_m] + war_k)
            done_p(tk_e)
            tk_s = p.do("sp", lambda e, dma, dmat, slot=slot, kb=kb: dma(
                sc["V3"][slot * 128:(slot + 1) * 128, :], kb[:, 0:512]),
                deps=[tk_e], dmas=1)
            done_k(tk_s)
            last_tok = tk_s
        if own:
            for ot in range(DT):
                ps, war_p, done_p = t.ak_psrot.next()

                def mmq(e, ot=ot, ps=ps, lnT=lnT):
                    for a in range(DT):
                        e.matmul(ps[:, 0:512],
                                 t.kr_wq[:, a * 2048 + ot * 128:
                                         a * 2048 + (ot + 1) * 128],
                                 lnT[:, a * 512:(a + 1) * 512],
                                 start=(a == 0), stop=False)
                    return e.matmul(ps[:, 0:512],
                                    t.c_bq[0:1, ot * 128:(ot + 1) * 128],
                                    t.c_ones[0:1, 0:512],
                                    start=False, stop=True)
                tk_m = p.do("pe", mmq, deps=tr_toks + [tk_w] + war_p)
                kb, war_k, done_k = t.ak_krot.next()
                tk_e = p.do("act", lambda e, kb=kb, ps=ps: e.copy(
                    kb[:, 0:512], ps[:, 0:512]), deps=[tk_m] + war_k)
                done_p(tk_e)
                rob, war_ro, done_ro = t.kr_rorot.next()
                tk_ro = rope_block(kb[:, 0:512], 512, g0, rob[:, 0:512],
                                   [tk_e, tk_cs] + war_ro)
                done_k(tk_ro)
                tk_s = p.do("sp", lambda e, dma, dmat, ot=ot, g0=g0, rob=rob:
                            dma(sc["Q3T"][ot * 128:(ot + 1) * 128,
                                          (g0 - OWN0) * 128:
                                          (g0 - OWN0) * 128 + 512],
                                rob[:, 0:512]), deps=[tk_ro], dmas=1)
                done_ro(tk_s)
                last_tok = tk_s
        done_ln(last_tok)
    p.barrier()


def emit_attn_b(p, t, hb, sc):
    def ldk(e, dma, dmat):
        dma(t.ab_k3.rearrange("p (a c) -> p a c", a=4),
            sc["K3T"].rearrange("(a p) c -> p a c", p=128))
        dma(t.ab_v3.rearrange("p (a c) -> p a c", a=NT_S12),
            sc["V3"].rearrange("(a p) c -> p a c", p=128))
    tk_k = p.do("sp", ldk, dmas=2)

    def ldwo(e, dma, dmat):
        for a in range(DT):
            dma(t.ab_wo[:, a * 2048:(a + 1) * 2048],
                hb["wo"][a * 128:(a + 1) * 128, :])
    tk_w = p.do("pool", ldwo, dmas=DT)

    for j in range(NT_OWN):
        qb, war_q, done_q = t.ab_qrot.next()
        tk_q = p.do("sp", lambda e, dma, dmat, j=j, qb=qb: dma(
            qb.rearrange("p (a c) -> p a c", a=DT),
            sc["Q3T"].rearrange("(a p) c -> p a c", p=128)[
                :, :, j * 128:(j + 1) * 128]), deps=war_q, dmas=1)
        mb, war_m, done_m = t.ab_mrot.next()
        tk_m = p.do("sp", lambda e, dma, dmat, j=j, mb=mb: dma(
            mb[:], hb["masks"][j * 128:(j + 1) * 128, :]), deps=war_m, dmas=1)
        ctx_ev = []
        for g in range(4):
            pt_done = []
            pt_toks = []
            for hh in range(4):
                qh = g * 4 + hh
                ps, war_p, done_p = t.ab_scrot.next()

                def mms(e, j=j, g=g, qh=qh, ps=ps, qb=qb):
                    e.matmul(ps[:, 0:512],
                             qb[:, qh * 128:(qh + 1) * 128],
                             t.ab_k3[:, g * N_S12 + j * 128:
                                     g * N_S12 + j * 128 + 512],
                             start=True, stop=True)
                    e.matmul(ps[:, 512:640],
                             qb[:, qh * 128:(qh + 1) * 128],
                             t.ab_k3[:, g * N_S12 + j * 128 + 512:
                                     g * N_S12 + j * 128 + 640],
                             start=True, stop=False)
                    return e.matmul(ps[:, 640:768],
                                    qb[:, qh * 128:(qh + 1) * 128],
                                    t.ab_k3[:, g * N_S12 + 2560:
                                            g * N_S12 + 2688],
                                    start=False, stop=True)
                tk_sc = p.do("pe", mms, deps=[tk_k, tk_q] + war_p)
                tk_msk = p.do("dve", lambda e, ps=ps, mb=mb:
                              e.scalar_tensor_tensor(
                    t.ab_s[:], ps[:, 0:768], SC_B, mb[:], ALU.mult, ALU.add),
                    deps=[tk_sc, tk_m] + t.ab_s_war)
                t.ab_s_war = []
                done_p(tk_msk)
                tk_mx = p.do("dve", lambda e: e.reduce_max(
                    t.ab_nmx[:], t.ab_s[:], AX.X, negate=True),
                    deps=[tk_msk])
                prb, war_pr, done_pr = t.ab_prrot.next()
                tk_ex = p.do("act", lambda e, prb=prb: e.activation(
                    prb[:], t.ab_s[:], AF.Exp, bias=t.ab_nmx[:], scale=1.0,
                    accum_out=t.ab_l[:]), deps=[tk_mx] + war_pr)
                t.ab_s_war.append(tk_ex)

                tk_rc = p.do("dve", lambda e: e.reciprocal(
                    t.ab_rl[:], t.ab_l[:]), deps=[tk_ex])
                tk_d = p.do("dve", lambda e: e.tensor_scalar(
                    t.ab_dmat[:], t.c_ident[:], t.ab_rl[:], None, ALU.mult),
                    deps=[tk_rc] + t.ab_dm_war)
                t.ab_dm_war = []
                ptb, war_pt, done_pt = t.ab_ptrot.next()
                ev_toks = []
                for piece, (k0, kn) in enumerate([(0, 4), (4, 2)]):
                    psx, war_px, done_px = t.ak_psrot.next()

                    def trp(e, piece=piece, k0=k0, kn=kn, prb=prb, psx=psx):
                        last = None
                        for ks in range(kn):
                            last = e.matmul(
                                psx[:, ks * 128:(ks + 1) * 128],
                                prb[:, (k0 + ks) * 128:(k0 + ks + 1) * 128],
                                t.ab_dmat[:], start=(ks == 0),
                                stop=(ks == kn - 1))
                        return last
                    tk_tp = p.do("pe", trp, deps=[tk_d] + war_px)
                    t.ab_dm_war.append(tk_tp)
                    tk_pte = p.do("dve", lambda e, k0=k0, kn=kn, psx=psx,
                                  ptb=ptb: e.tensor_copy(
                        ptb[:, k0 * 128:(k0 + kn) * 128],
                        psx[:, 0:kn * 128]), deps=[tk_tp] + war_pt)
                    war_pt = []
                    done_px(tk_pte)
                    ev_toks.append(tk_pte)
                done_pr(ev_toks[-1])
                pt_toks.append((hh, ptb, done_pt, ev_toks))
            psc, war_c, done_c = t.ab_ctxps.next()

            def mmc(e, j=j, g=g, pt_toks=pt_toks, psc=psc):
                last = None
                for ks in range(6):
                    vslot = j + ks if ks < 5 else 20
                    for (hh, ptb, _, _) in pt_toks:
                        last = e.matmul(
                            psc[:, hh * 128:(hh + 1) * 128],
                            t.ab_v3[:, vslot * 512 + g * 128:
                                    vslot * 512 + (g + 1) * 128],
                            ptb[:, ks * 128:(ks + 1) * 128],
                            start=(ks == 0 and hh == 0),
                            stop=(ks == 5 and hh == 3))
                return last
            all_pt = [tk for (_, _, _, evs) in pt_toks for tk in evs]
            tk_cm = p.do("pe", mmc, deps=all_pt + war_c + [tk_k])
            for (_, _, done_pt, _) in pt_toks:
                done_pt(tk_cm)
            tk_ce = p.do("act", lambda e, g=g, psc=psc: e.copy(
                t.ab_ctxT[:, g * 512:(g + 1) * 512], psc[:, 0:512]),
                deps=[tk_cm] + t.ab_ctxT_war)
            t.ab_ctxT_war = []
            done_c(tk_ce)
            ctx_ev.append(tk_ce)
        done_m(ctx_ev[-1])
        done_q(ctx_ev[-1])
        xrb, war_x, done_x = t.ak_xrot.next()
        slot = OWN0 + j
        tk_x = p.do("sp", lambda e, dma, dmat, slot=slot, xrb=xrb: dma(
            xrb[:], sc["XR2"][slot * 128:(slot + 1) * 128, :]),
            deps=war_x, dmas=1)
        add_toks = []
        for oc in range(4):
            ps, war_p, done_p = t.ak_psrot.next()

            def mmo(e, oc=oc, ps=ps):
                for a in range(DT):
                    e.matmul(ps[:, 0:512],
                             t.ab_ctxT[:, a * 128:(a + 1) * 128],
                             t.ab_wo[:, a * 2048 + oc * 512:
                                     a * 2048 + (oc + 1) * 512],
                             start=(a == 0), stop=False)
                return e.matmul(ps[:, 0:512], t.c_ones[0:1, 0:128],
                                t.c_bo[0:1, oc * 512:(oc + 1) * 512],
                                start=False, stop=True)
            tk_mo = p.do("pe", mmo, deps=ctx_ev + [tk_w] + war_p)
            t.ab_ctxT_war.append(tk_mo)
            tk_a = p.do("dve", lambda e, oc=oc, ps=ps, xrb=xrb:
                        e.tensor_tensor(
                t.ab_xnew[:, oc * 512:(oc + 1) * 512],
                xrb[:, oc * 512:(oc + 1) * 512], ps[:, 0:512], ALU.add),
                deps=[tk_mo, tk_x] + t.ab_xnew_war)
            t.ab_xnew_war = []
            done_p(tk_a)
            add_toks.append(tk_a)
        done_x(add_toks[-1])
        tk_o = p.do("sp", lambda e, dma, dmat, slot=slot: dma(
            sc["XR3"][slot * 128:(slot + 1) * 128, :], t.ab_xnew[:]),
            deps=add_toks, dmas=1)
        t.ab_xnew_war.append(tk_o)
    p.barrier()


def build_nc():
    nc = bass.Bass("TRN2", target_bir_lowering=False, debug=False)
    hb = {}

    def din(name, shape, dt=F32):
        hb[name] = nc.dram_tensor(name, list(shape), dt,
                                  kind="ExternalInput").ap()

    din("xb", (S, D))
    din("xs", (N_S12, D))
    din("w_qkvT", (D, 6144))
    din("b_qkv", (1, 6144))
    din("w_outT", (D, D))
    din("b_out", (1, D))
    din("wq", (D, D))
    din("bq", (1, D))
    din("wkv", (D, 1024))
    din("bkv", (1, 1024))
    din("wo", (D, D))
    din("bo", (1, D))
    din("su_ug", (16, 128))
    din("su_dp", (64, 128))
    din("ws_ug", (128, 128))
    din("ws_dp", (16, 128))
    din("wupTf", (D, INTER))
    din("wgateTf", (D, INTER))
    din("wdownTf", (INTER, D))
    din("cosq", (128, N_S12))
    din("sinq", (128, N_S12))
    din("masks", (16 * 128, 768))
    din("c_ident", (128, 128), BF16)
    din("c_identf", (128, 128), F32)
    din("c_h128", (128, 128), BF16)
    din("c_bf16", (128, 128), BF16)
    din("c_bf64", (128, 128), BF16)
    din("c_rope", (128, 128), BF16)
    din("c_ones", (1, 512), BF16)

    out = nc.dram_tensor("out", [N_OWN, D], F32, kind="ExternalOutput").ap()

    sc = {}

    def dsc(name, shape, dt=BF16):
        sc[name] = nc.dram_tensor(name, list(shape), dt, kind="Internal").ap()

    dsc("C_up", (32 * 128, DT * 256))
    dsc("C_gate", (32 * 128, DT * 256))
    dsc("C_down", (DT * 128, IT * 128))
    if DBG:
        dsc("DBGP", (N_S12, D))
    dsc("KT", (D, S))
    dsc("VV", (S, D))
    dsc("QT", (D, N_S12))
    dsc("CT", (N_S12, D))
    dsc("US", (INTER, N_S12))
    dsc("K3T", (512, N_S12))
    dsc("V3", (N_S12, 512))
    dsc("Q3T", (D, N_OWN))
    dsc("XR1", (N_S12, D), F32)
    dsc("XR2", (N_S12, D), F32)
    dsc("XR3", (N_S12, D), F32)

    t = NS()
    t.ln_prev = []
    with ExitStack() as ctx:
        p = Prog(nc, ctx)
        sb = lambda n, s, d: ctx.enter_context(nc.sbuf_tensor("sb_" + n, s, d))
        pst = lambda n, s, d: ctx.enter_context(nc.psum_tensor(n, s, d))

        t.c_ident = sb("c_ident", [128, 128], BF16)
        t.c_h128 = sb("c_h128", [128, 128], BF16)
        t.c_sh128 = sb("c_sh128", [128, DT * 128], BF16)
        t.c_bf16 = sb("c_bf16", [128, 128], BF16)
        t.c_bf64 = sb("c_bf64", [128, 128], BF16)
        t.c_rope = sb("c_rope", [128, 128], BF16)
        t.c_ones = sb("c_ones", [1, 512], BF16)
        t.c_bqkv = sb("c_bqkv", [1, 6144], BF16)
        t.c_bout = sb("c_bout", [1, D], BF16)
        t.c_bq = sb("c_bq", [1, D], BF16)
        t.c_bkv = sb("c_bkv", [1, 1024], BF16)
        t.c_bo = sb("c_bo", [1, D], BF16)
        t.c_vup = sb("c_vup", [128, 64], F32)
        t.c_vgate = sb("c_vgate", [128, 64], F32)
        t.c_vdp = sb("c_vdp", [128, 16], F32)
        t.c_suug = sb("c_suug", [128, 16], F32)
        t.c_sudp = sb("c_sudp", [128, 64], F32)
        t.bn6 = sb("bn6", [128, 24], F32)
        t.mv = sb("mv", [128, 2], F32)
        t.t0 = sb("t0", [128, 1], F32)
        t.rstd = sb("rstd", [128, 1], F32)
        t.negmr = sb("negmr", [128, 1], F32)
        t.c_eps = sb("c_eps", [128, 1], F32)
        t.trrot = Rot([pst("ps_tr0", [128, 1024], BF16),
                       pst("ps_tr1", [128, 1024], BF16)])
        t.ak_psrot = Rot([pst("ps_mm0", [128, 512], F32),
                          pst("ps_mm1", [128, 512], F32)])
        t.ak_xrot = Rot([sb("xf0", [128, D], F32), sb("xf1", [128, D], F32)])
        t.ak_lbrot = Rot([sb("lnbf0", [128, D], BF16),
                          sb("lnbf1", [128, D], BF16)])
        t.ak_krot = Rot([sb("kb0", [128, 512], BF16),
                         sb("kb1", [128, 512], BF16)])

        with ExitStack() as c2:
            sb2 = lambda n, s, d: c2.enter_context(
                nc.sbuf_tensor("sb_" + n, s, d))
            t.c_identf = sb2("c_identf", [128, 128], F32)
            t.v_wsug = sb2("v_wsug", [128, 128], F32)
            t.v_wsdp = sb2("v_wsdp", [16, 128], F32)
            t.v_suug = sb2("v_suug", [16, 128], F32)
            t.v_sudp = sb2("v_sudp", [64, 128], F32)
            t.ps_v0 = c2.enter_context(nc.psum_tensor("ps_v0", [128, 512], F32))
            t.ps_v1 = c2.enter_context(nc.psum_tensor("ps_v1", [128, 512], F32))
            emit_const_prep(p, t, hb)

        with ExitStack() as cf:
            sbf = lambda n, s, d: cf.enter_context(
                nc.sbuf_tensor("sb_" + n, s, d))
            t.fbrot = Rot([sbf("fb0", [128, 1024], BF16),
                           sbf("fb1", [128, 1024], BF16)])
            t.forot = Rot([sbf("fo0", [128, 1024], BF16),
                           sbf("fo1", [128, 1024], BF16)])
            t.fprot = Rot([cf.enter_context(nc.psum_tensor("fp0", [128, 512],
                                                           F32)),
                           cf.enter_context(nc.psum_tensor("fp1", [128, 512],
                                                           F32))])
            t.fold_up, t.fold_gate, t.fold_down = [], [], []

            def mk_ug_store(c_ap):
                c_r = c_ap.rearrange("(cb n s) (a v) -> n (s a) cb v",
                                     cb=32, n=16, s=8, a=16, v=256)

                def store(n, c0, cw, ob):
                    cb0 = c0 // 256
                    ncb = cw // 256
                    return (c_r[n][:, cb0:cb0 + ncb, :],
                            ob.rearrange("p (cb v) -> p cb v",
                                         cb=ncb, v=256))
                return store

            def mk_dn_store(c_ap):
                c_r = c_ap.rearrange("(o n s) (a u) -> n (s a) o u",
                                     o=16, n=64, s=2, a=64, u=128)

                def store(n, c0, cw, ob):
                    o0 = c0 // 128
                    no = cw // 128
                    return (c_r[n][:, o0:o0 + no, :],
                            ob.rearrange("p (o u) -> p o u", o=no, u=128))
                return store

            t.bg = [
                gen_cache_fold(p, t, hb["wupTf"], DT, t.c_bf16, INTER,
                               t.fold_up, mk_ug_store(sc["C_up"])),
                gen_cache_fold(p, t, hb["wgateTf"], DT, t.c_bf16, INTER,
                               t.fold_gate, mk_ug_store(sc["C_gate"])),
                gen_cache_fold(p, t, hb["wdownTf"], IT, t.c_bf64, D,
                               t.fold_down, mk_dn_store(sc["C_down"])),
            ]

            with ExitStack() as c2:
                sb2 = lambda n, s, d: c2.enter_context(
                    nc.sbuf_tensor("sb_" + n, s, d))
                t.ak_wk = sb2("ak_wk", [128, DT * 2048], BF16)
                t.ak_wv = sb2("ak_wv", [128, DT * 2048], BF16)
                t.ak_lnrot = Rot([sb2("ak_ln0", [128, DT * 256], BF16)])
                t.ak_vrot = Rot([sb2("ak_v0", [128, D], BF16)])
                emit_attn_a_kv(p, t, hb, sc)

            with ExitStack() as c2:
                sb2 = lambda n, s, d: c2.enter_context(
                    nc.sbuf_tensor("sb_" + n, s, d))
                t.aq_wq = sb2("aq_wq", [128, DT * 2048], BF16)
                t.aq_lnrot = Rot([sb2("aq_ln0", [128, DT * 512], BF16),
                                  sb2("aq_ln1", [128, DT * 512], BF16)])
                emit_attn_a_q(p, t, hb, sc)

            with ExitStack() as c2:
                sb2 = lambda n, s, d: c2.enter_context(
                    nc.sbuf_tensor("sb_" + n, s, d))
                pst2 = lambda n, s, d: c2.enter_context(
                    nc.psum_tensor(n, s, d))
                t.fl_un = sb2("fl_un", [128, DT * QT_W], BF16)
                t.fl_probs = sb2("fl_probs", [128, 5 * 4096], BF16)
                t.fl_kv = sb2("fl_kv", [128, DT * 1024], BF16)
                t.fl_kvrot = Rot([t.fl_kv[:, i * DT * 512:(i + 1) * DT * 512]
                                  for i in range(2)])
                t.fl_mx = sb2("fl_mx", [128, 48], F32)
                t.fl_bb = sb2("fl_bb", [128, 48], F32)
                t.fl_sm = sb2("fl_sm", [128, 48], F32)
                t.fl_gmx = sb2("fl_gmx", [128, 5], F32)
                t.fl_gnb = sb2("fl_gnb", [128, 5], F32)
                t.fl_rf = sb2("fl_rf", [128, 48], F32)
                t.fl_lw = sb2("fl_lw", [128, 48], F32)
                t.fl_l = sb2("fl_l", [128, 5], F32)
                t.fl_rl = sb2("fl_rl", [128, 5], F32)
                t.fl_w = sb2("fl_w", [128, 48], F32)
                t.fl_scrot = Rot([sb2("fl_sc0", [128, 128], BF16),
                                  sb2("fl_sc1", [128, 128], BF16)])
                t.fl_psctx = pst2("fl_ctx", [128, 1024], F32)
                t.fl_par = [sb2("fl_par%d" % i, [128, 2048], BF16)
                            for i in range(5)]
                t.fl_chrot = Rot([sb2("fl_ch0", [128, 1024], BF16),
                                  sb2("fl_ch1", [128, 1024], BF16)])
                t.fl_ctrot = Rot([sb2("fl_ct0", [128, 1024], BF16),
                                  sb2("fl_ct1", [128, 1024], BF16)])
                t.fl_qt_war = []
                t.fl_kv_war = []
                t.fl_probs_war = []
                t.fl_ctx_war = []
                t.fl_ptq_war = {}
                t.fl_par_war = {}
                emit_attn_a_flash(p, t, sc)

            with ExitStack() as c2:
                sb2 = lambda n, s, d: c2.enter_context(
                    nc.sbuf_tensor("sb_" + n, s, d))
                t.ao_wo = sb2("ao_wo", [128, DT * 2048], BF16)
                t.ao_ctrot = Rot([sb2("ao_ct0", [128, D], BF16),
                                  sb2("ao_ct1", [128, D], BF16)])
                t.ao_xnew = sb2("ao_xnew", [128, D], F32)
                t.ao_xnew_war = []
                emit_attn_a_out(p, t, hb, sc)

            pump(t, 10 ** 9)
            assert not t.bg
            p.barrier()

        emit_ffn(p, nc, ctx, t, sc, sc["XR1"], sc["XR2"], NT_S12, "f1_")

        with ExitStack() as c2:
            sb2 = lambda n, s, d: c2.enter_context(nc.sbuf_tensor("sb_" + n, s, d))
            pst2 = lambda n, s, d: c2.enter_context(nc.psum_tensor(n, s, d))
            t.kr_cos = sb2("kr_cos", [128, N_S12], F32)
            t.kr_sin = sb2("kr_sin", [128, N_S12], F32)
            t.kr_wq = sb2("kr_wq", [128, DT * 2048], BF16)
            t.kr_wkv = sb2("kr_wkv", [128, DT * 1024], BF16)
            t.kr_lnrot = Rot([sb2("kr_ln0", [128, DT * 512], BF16)])
            t.kr_psr = Rot([pst2("kr_psr0", [128, 512], F32),
                            pst2("kr_psr1", [128, 512], F32)])
            t.kr_t1rot = Rot([sb2("kr_t10", [128, 512], F32),
                              sb2("kr_t11", [128, 512], F32)])
            t.kr_t2 = sb2("kr_t2", [128, 512], F32)
            t.kr_t2_war = []
            t.kr_rorot = Rot([sb2("kr_ro0", [128, 512], BF16),
                              sb2("kr_ro1", [128, 512], BF16)])
            emit_kvrope(p, t, hb, sc)

        with ExitStack() as c2:
            sb2 = lambda n, s, d: c2.enter_context(nc.sbuf_tensor("sb_" + n, s, d))
            pst2 = lambda n, s, d: c2.enter_context(nc.psum_tensor(n, s, d))
            t.ab_k3 = sb2("ab_k3", [128, 4 * N_S12], BF16)
            t.ab_v3 = sb2("ab_v3", [128, NT_S12 * 512], BF16)
            t.ab_wo = sb2("ab_wo", [128, DT * 2048], BF16)
            t.ab_qrot = Rot([sb2("ab_q0", [128, DT * 128], BF16),
                             sb2("ab_q1", [128, DT * 128], BF16)])
            t.ab_mrot = Rot([sb2("ab_m0", [128, 768], F32),
                             sb2("ab_m1", [128, 768], F32)])
            t.ab_s = sb2("ab_s", [128, 768], F32)
            t.ab_nmx = sb2("ab_nmx", [128, 1], F32)
            t.ab_l = sb2("ab_l", [128, 1], F32)
            t.ab_rl = sb2("ab_rl", [128, 1], F32)
            t.ab_dmat = sb2("ab_dmat", [128, 128], BF16)
            t.ab_prrot = Rot([sb2("ab_pr0", [128, 768], BF16),
                              sb2("ab_pr1", [128, 768], BF16)])
            t.ab_ptrot = Rot([sb2("ab_pt0", [128, 768], BF16),
                              sb2("ab_pt1", [128, 768], BF16),
                              sb2("ab_pt2", [128, 768], BF16),
                              sb2("ab_pt3", [128, 768], BF16)])
            t.ab_scrot = Rot([pst2("ab_sc0", [128, 1024], F32)])
            t.ab_ctxps = Rot([pst2("ab_ctx0", [128, 512], F32),
                              pst2("ab_ctx1", [128, 512], F32)])
            t.ab_ctxT = sb2("ab_ctxT", [128, DT * 128], BF16)
            t.ab_xnew = sb2("ab_xnew", [128, D], F32)
            t.ab_s_war = []
            t.ab_dm_war = []
            t.ab_ctxT_war = []
            t.ab_xnew_war = []
            emit_attn_b(p, t, hb, sc)

        emit_ffn(p, nc, ctx, t, sc,
                 sc["XR3"][OWN0 * 128:(OWN0 + NT_OWN) * 128, :], out, NT_OWN,
                 "f2_")

        p.final_wait()
        p.check()
        with nc.Block() as block:
            p.emit_all(block)
    return nc


def _fold_perm(wT, a_factor, n_tiles):
    sgrp = 128 // a_factor
    r = wT.reshape(a_factor, n_tiles, sgrp, wT.shape[1])
    return np.ascontiguousarray(r.transpose(1, 2, 0, 3).reshape(wT.shape))


def stage_inputs(inputs):
    hc = host_constants()
    x = np.asarray(inputs["L0_attn_a_hidden_states"], np.float32)
    shared = dict(
        w_qkvT=np.ascontiguousarray(
            np.asarray(inputs["L0_attn_a_w_qkv"], np.float32).T),
        b_qkv=np.asarray(inputs["L0_attn_a_b_qkv"], np.float32).reshape(1, -1),
        w_outT=np.ascontiguousarray(
            np.asarray(inputs["L0_attn_a_w_out"], np.float32).T),
        b_out=np.asarray(inputs["L0_attn_a_b_out"], np.float32).reshape(1, -1),
        wq=np.ascontiguousarray(inputs["L0_attn_b_Wq"], np.float32),
        bq=np.asarray(inputs["L0_attn_b_bq"], np.float32).reshape(1, -1),
        wkv=np.ascontiguousarray(inputs["L0_attn_b_Wkv"], np.float32),
        bkv=np.asarray(inputs["L0_attn_b_bkv"], np.float32).reshape(1, -1),
        wo=np.ascontiguousarray(inputs["L0_attn_b_Wo"], np.float32),
        bo=np.asarray(inputs["L0_attn_b_bo"], np.float32).reshape(1, -1),
        su_ug=np.asarray(inputs["L0_ffn_SU_ug"], np.float32).reshape(16, 128),
        su_dp=np.asarray(inputs["L0_ffn_SU_dp"], np.float32).reshape(64, 128),
        ws_ug=np.asarray(inputs["L0_ffn_Wscale_ug"],
                         np.float32).reshape(128, 128),
        ws_dp=np.asarray(inputs["L0_ffn_Wscale_dp"],
                         np.float32).reshape(16, 128),
        wupTf=_fold_perm(np.ascontiguousarray(
            np.asarray(inputs["L0_ffn_W_up"], np.float32).T), 16, 16),
        wgateTf=_fold_perm(np.ascontiguousarray(
            np.asarray(inputs["L0_ffn_W_gate"], np.float32).T), 16, 16),
        wdownTf=_fold_perm(np.ascontiguousarray(
            np.asarray(inputs["L0_ffn_W_down"], np.float32).T), 64, 64),
        **hc,
    )
    in_maps = []
    for c in range(8):
        b, h = c // 2, c % 2
        tiles = slot_tiles(h)
        xb = np.ascontiguousarray(x[b])
        xs = np.ascontiguousarray(
            np.concatenate([x[b, tt * 128:(tt + 1) * 128, :] for tt in tiles],
                           axis=0))
        cos, sin = host_rope_tables(h)
        masks = host_masks(h).reshape(16 * 128, 768)
        m = dict(shared)
        m.update(xb=xb, xs=xs, cosq=cos, sinq=sin,
                 masks=np.ascontiguousarray(masks))
        in_maps.append(m)
    return in_maps


_CACHED = {}


def kernel(**inputs):
    if "nc" not in _CACHED:
        _CACHED["nc"] = build_nc()
    nc = _CACHED["nc"]
    in_maps = stage_inputs(inputs)
    res = run_bass_kernel_spmd(nc, in_maps, core_ids=list(range(8)))
    out = np.zeros((B, S, D), np.float32)
    for c in range(8):
        b, h = c // 2, c % 2
        out[b, h * 2048:(h + 1) * 2048, :] = res.results[c]["out"]
    return out


# version 05 (final)
# speedup vs baseline: 1.1131x; 1.1131x over previous
import numpy as np
import ml_dtypes
from contextlib import ExitStack

import concourse.bass as bass
import concourse.mybir as mybir
from concourse.bass_utils import run_bass_kernel_spmd

F32 = mybir.dt.float32
BF16 = mybir.dt.bfloat16
AF = mybir.ActivationFunctionType
ALU = mybir.AluOpType
AX = mybir.AxisListType
BF = ml_dtypes.bfloat16

D = 2048
DT = D // 128
INTER = 8192
IT = INTER // 128
B, S = 4, 4096
NT_S12 = 21
NT_OWN = 16
OWN0 = 4
N_S12 = NT_S12 * 128
N_OWN = NT_OWN * 128
HD = 128
WINDOW, NMETA = 512, 16
SC_A = 1.0 / float(np.sqrt(2048.0))
SC_B = 1.0 / float(np.sqrt(128.0))
EPS = 1e-5
NEG = -1.0e9
QT_W = 640
USB_W = 640

ENG = ("pe", "dve", "act", "pool", "sp")
DBG = False


def _hadamard(n):
    h = np.array([[1.0]])
    while h.shape[0] < n:
        h = np.block([[h, h], [h, -h]])
    return h


def host_constants():
    c = {}
    c["c_ident"] = np.eye(128).astype(BF)
    c["c_identf"] = np.eye(128).astype(np.float32)
    c["c_h128"] = _hadamard(128).astype(BF)
    c["c_bf16"] = np.kron(np.eye(8), _hadamard(16)).astype(BF)
    c["c_bf64"] = np.kron(np.eye(2), _hadamard(64)).astype(BF)
    R = np.zeros((128, 128), np.float32)
    for i in range(64):
        R[2 * i, 2 * i + 1] = -1.0
        R[2 * i + 1, 2 * i] = 1.0
    c["c_rope"] = np.ascontiguousarray(R.T).astype(BF)
    c["c_ones"] = np.ones((1, 512)).astype(BF)
    return c


def slot_tiles(h):
    if h == 0:
        return [16, 17, 18, 19] + list(range(0, 16)) + [0]
    return [12, 13, 14, 15] + list(range(16, 32)) + [0]


def host_rope_tables(h):
    tiles = slot_tiles(h)
    pos = np.concatenate([np.arange(t * 128, (t + 1) * 128) for t in tiles])
    inv = 1.0 / (10000.0 ** (np.arange(0, HD, 2, dtype=np.float64) / HD))
    ang = np.repeat(inv, 2)[:, None] * pos[None, :].astype(np.float64)
    return np.cos(ang).astype(np.float32), np.sin(ang).astype(np.float32)


def host_masks(h):
    tiles = slot_tiles(h)
    own_start_tile = 0 if h == 0 else 16
    m = np.full((16, 128, 768), NEG, np.float32)
    for j in range(16):
        gq = own_start_tile + j
        p = gq * 128 + np.arange(128)
        for bs in range(5):
            slot = j + bs
            gt = tiles[slot]
            if h == 0 and slot < OWN0:
                continue
            kp = gt * 128 + np.arange(128)
            ok = (kp[None, :] <= p[:, None]) & (
                (kp[None, :] >= p[:, None] - (WINDOW - 1)) | (kp[None, :] < NMETA))
            m[j, :, bs * 128:(bs + 1) * 128] = np.where(ok, 0.0, NEG)
        band_tiles = [tiles[j + bs] for bs in range(5)
                      if not (h == 0 and j + bs < OWN0)]
        if 0 not in band_tiles:
            kt = np.arange(128)
            ok = (kt[None, :] < NMETA) & (kt[None, :] <= p[:, None])
            m[j, :, 640:768] = np.where(ok, 0.0, NEG)
    return m


KL = 8
DMAQ = ("sp", "pool")


class Prog:

    def __init__(self, nc, ctx):
        self.nc = nc
        self.keys = list(ENG) + [q + str(i) for q in DMAQ for i in range(KL)]
        self.sems = {k: ctx.enter_context(nc.semaphore("sem_" + k))
                     for k in self.keys}
        self.cnt = {k: 0 for k in self.keys}
        self.steps = {e: [] for e in ENG}
        self.bar = {k: 0 for k in self.keys}
        self.lane_ptr = {q: 0 for q in DMAQ}

    def do(self, eng, emit, deps=(), dmas=0):
        need = {}
        for k, v in self.bar.items():
            if v > 0:
                need[k] = v
        for d in deps:
            if d is None:
                continue
            k, v = d
            need[k] = max(need.get(k, 0), v)
        if dmas:
            lane = self.lane_ptr[eng]
            self.lane_ptr[eng] = (lane + 1) % KL
            key = eng + str(lane)
            if self.cnt[key] > 0:
                need[key] = max(need.get(key, 0), self.cnt[key])
            self.steps[eng].append((need, emit, dmas, key))
            self.cnt[key] += 16 * dmas
        else:
            key = eng
            self.steps[eng].append((need, emit, 0, key))
            self.cnt[key] += 1
        assert self.cnt[key] < 60000, f"sem overflow risk on {key}"
        return (key, self.cnt[key])

    def barrier(self, exclude_pool=False):
        if exclude_pool:
            keep = {k: v for k, v in self.bar.items() if k.startswith("pool")}
            self.bar = {k: (keep.get(k, 0) if k.startswith("pool") else v)
                        for k, v in self.cnt.items()}
        else:
            self.bar = dict(self.cnt)

    def emit_all(self, block):
        prog = self

        def runner(eng):
            def run(engobj):
                water = {k: 0 for k in prog.keys}
                for need, emit, dmas, key in prog.steps[eng]:
                    for k in prog.keys:
                        v = need.get(k, 0)
                        if v > water[k]:
                            engobj.wait_ge(prog.sems[k], v)
                            water[k] = v
                    if dmas:
                        n_done = [0]

                        def dma(out, in_, **kw):
                            engobj.dma_start(out=out, in_=in_, **kw).then_inc(
                                prog.sems[key], 16)
                            n_done[0] += 1

                        def dmat(out, in_):
                            engobj.dma_start_transpose(out, in_).then_inc(
                                prog.sems[key], 16)
                            n_done[0] += 1
                        emit(engobj, dma, dmat)
                        assert n_done[0] == dmas, \
                            f"{eng}: {n_done[0]} != {dmas} dmas"
                    else:
                        last = emit(engobj)
                        assert last is not None, f"{eng} compute step -> None"
                        last.then_inc(prog.sems[key], 1)
            return run

        block.tensor(runner("pe"))
        block.vector(runner("dve"))
        block.scalar(runner("act"))
        block.gpsimd(runner("pool"))
        block.sync(runner("sp"))

    def check(self):
        ptr = {e: 0 for e in ENG}
        done = {k: 0 for k in self.keys}
        progress = True
        while progress:
            progress = False
            for e in ENG:
                while ptr[e] < len(self.steps[e]):
                    need, _, dmas, key = self.steps[e][ptr[e]]
                    if all(done[k] >= v for k, v in need.items()
                           if not (k == key and dmas)):
                        done[key] += 16 * dmas if dmas else 1
                        ptr[e] += 1
                        progress = True
                    else:
                        break
        stuck = {e: (ptr[e], len(self.steps[e])) for e in ENG
                 if ptr[e] < len(self.steps[e])}
        assert not stuck, f"DEADLOCK at {stuck}; counts={done}"

    def final_wait(self):
        tgt = dict(self.cnt)

        def emit(engobj):
            for k in self.keys:
                if k != "pool" and tgt[k] > 0:
                    engobj.wait_ge(self.sems[k], tgt[k])
            return engobj.nop()
        self.steps["pool"].append(({}, emit, 0, "pool"))
        self.cnt["pool"] += 1


class Rot:
    def __init__(self, bufs):
        self.bufs = bufs
        self.toks = [[] for _ in bufs]
        self.i = 0

    def next(self):
        s = self.i % len(self.bufs)
        self.i += 1
        war = list(self.toks[s])
        self.toks[s] = []

        def done(tok):
            self.toks[s].append(tok)
        return self.bufs[s], war, done


class NS:
    pass


def emit_ln_tile(p, t, xf, lnbf, deps):
    def dve_stats(e):
        last = None
        for i in range(4):
            last = e.bn_stats(t.bn6[:, 6 * i:6 * i + 6],
                              xf[:, 512 * i:512 * (i + 1)])
        return last
    tk0 = p.do("dve", dve_stats, deps=list(deps) + t.ln_prev)
    tk1 = p.do("dve", lambda e: e.bn_aggr(t.mv[:], t.bn6[:]), deps=[tk0])

    tk2a = p.do("act", lambda e: e.activation(
        t.t0[:], t.mv[:, 1:2], AF.Ln, bias=t.c_eps[:], scale=1.0), deps=[tk1])
    tk2 = p.do("act", lambda e: e.activation(
        t.rstd[:], t.t0[:], AF.Exp, bias=0.0, scale=-0.5), deps=[tk2a])

    tk3 = p.do("dve", lambda e: e.tensor_scalar(
        t.negmr[:], t.mv[:, 0:1], t.rstd[:], -1.0, ALU.mult, ALU.mult),
        deps=[tk2])
    tk4 = p.do("act", lambda e: e.activation(
        lnbf[:], xf[:], AF.Identity, bias=t.negmr[:], scale=t.rstd[:]),
        deps=[tk3])
    t.ln_prev = [tk3, tk4]
    return tk4


def emit_transpose16(p, t, src_bf, dst_fn, deps, evac_deps=()):
    toks = []
    ev_extra = list(evac_deps)
    for half in range(2):
        ps, war, done = t.trrot.next()

        def pe_tr(e, half=half, ps=ps):
            last = None
            for i in range(8):
                a = half * 8 + i
                last = e.transpose(ps[:, i * 128:(i + 1) * 128],
                                   src_bf[:, a * 128:(a + 1) * 128],
                                   t.c_ident[:])
            return last
        tk_pe = p.do("pe", pe_tr, deps=list(deps) + war)

        def evac(e, half=half, ps=ps):
            last = None
            for i in range(8):
                a = half * 8 + i
                last = e.tensor_copy(dst_fn(a), ps[:, i * 128:(i + 1) * 128])
            return last
        tk_ev = p.do("dve", evac, deps=[tk_pe] + ev_extra)
        ev_extra = []
        done(tk_ev)
        toks.append(tk_ev)
    return toks


def emit_ln_group(p, t, xin_rows, lnT, gn, lnT_war, xrot, lbrot, lnt_w=512):
    tr_toks = []
    ev_extra = list(lnT_war)
    for tt in range(gn):
        xf, war_x, done_x = xrot.next()
        src = xin_rows[tt]
        tk_x = p.do("sp", lambda e, dma, dmat, src=src, xf=xf: dma(xf[:], src),
                    deps=war_x, dmas=1)
        lnbf, war_l, done_l = lbrot.next()
        tk_ln = emit_ln_tile(p, t, xf, lnbf, [tk_x] + war_l)
        done_x(tk_ln)
        toks = emit_transpose16(
            p, t, lnbf,
            lambda a, tt=tt: lnT[:, a * lnt_w + tt * 128:
                                 a * lnt_w + (tt + 1) * 128],
            [tk_ln], evac_deps=ev_extra)
        ev_extra = []
        done_l(toks[-1])
        tr_toks += toks
    return tr_toks


def emit_const_prep(p, t, hb):
    def ld_consts(e, dma, dmat):
        dma(t.c_ident[:], hb["c_ident"])
        dma(t.c_identf[:], hb["c_identf"])
        dma(t.c_h128[:], hb["c_h128"])
        dma(t.c_bf16[:], hb["c_bf16"])
        dma(t.c_bf64[:], hb["c_bf64"])
        dma(t.c_rope[:], hb["c_rope"])
        dma(t.c_ones[:], hb["c_ones"])
    tk_c = p.do("sp", ld_consts, dmas=7)

    p.do("dve", lambda e: e.memset(t.c_eps[:], EPS))

    def ld_bias(e, dma, dmat):
        dma(t.c_bqkv[:], hb["b_qkv"])
        dma(t.c_bout[:], hb["b_out"])
        dma(t.c_bq[:], hb["bq"])
        dma(t.c_bkv[:], hb["bkv"])
        dma(t.c_bo[:], hb["bo"])
    p.do("pool", ld_bias, dmas=5)

    def ld_vecs(e, dma, dmat):
        dma(t.v_wsug[:], hb["ws_ug"])
        dma(t.v_wsdp[:], hb["ws_dp"])
        dma(t.v_suug[:], hb["su_ug"])
        dma(t.v_sudp[:], hb["su_dp"])
    tk_v = p.do("sp", ld_vecs, dmas=4)

    def pe_t(e):
        e.transpose(t.ps_v0[:, 0:128], t.v_wsug[:], t.c_identf[:])
        e.transpose(t.ps_v1[:, 0:16], t.v_wsdp[0:16, :], t.c_identf[0:16, 0:16])
        e.transpose(t.ps_v1[:, 128:144], t.v_suug[0:16, :],
                    t.c_identf[0:16, 0:16])
        return e.transpose(t.ps_v1[:, 256:320], t.v_sudp[0:64, :],
                           t.c_identf[0:64, 0:64])
    tk_t = p.do("pe", pe_t, deps=[tk_c, tk_v])

    sc2048 = float(2.0 ** (-5.5))
    sc8192 = float(2.0 ** (-6.5))

    def evac(e):
        e.tensor_scalar(t.c_vup[:], t.ps_v0[:, 0:64], sc2048, None, ALU.mult)
        e.tensor_scalar(t.c_vgate[:], t.ps_v0[:, 64:128], sc2048, None,
                        ALU.mult)
        e.tensor_scalar(t.c_vdp[:], t.ps_v1[:, 0:16], sc8192, None, ALU.mult)
        e.tensor_copy(t.c_suug[:], t.ps_v1[:, 128:144])
        return e.tensor_copy(t.c_sudp[:], t.ps_v1[:, 256:320])
    tk_e = p.do("dve", evac, deps=[tk_t])

    def mk_sh(e):
        last = None
        for a in range(DT):
            last = e.tensor_scalar(t.c_sh128[:, a * 128:(a + 1) * 128],
                                   t.c_h128[:], t.c_suug[:, a:a + 1], None,
                                   ALU.mult)
        return last
    p.do("dve", mk_sh, deps=[tk_e])
    p.barrier()


def gen_cache_fold(p, t, src_ap, n_tiles, bfmat, width, out_toks, store_ap,
                   cstep=1024):
    for n in range(n_tiles):
        for c0 in range(0, width, cstep):
            cw = min(cstep, width - c0)
            wb, war_w, done_w = t.fbrot.next()
            ob, war_o, done_o = t.forot.next()
            tk_l = p.do("pool", lambda e, dma, dmat, n=n, c0=c0, cw=cw, wb=wb:
                        dma(wb[:, 0:cw],
                            src_ap[n * 128:(n + 1) * 128, c0:c0 + cw]),
                        deps=war_w, dmas=1)
            evs = []
            for ci in range((cw + 511) // 512):
                d0 = ci * 512
                dw = min(512, cw - d0)
                ps, war_p, done_p = t.fprot.next()
                tk_m = p.do("pe", lambda e, ps=ps, d0=d0, dw=dw, wb=wb:
                            e.matmul(ps[:, 0:dw], bfmat[:],
                                     wb[:, d0:d0 + dw], start=True, stop=True),
                            deps=[tk_l] + war_p)
                tk_e = p.do("dve", lambda e, ps=ps, d0=d0, dw=dw, ob=ob:
                            e.tensor_copy(ob[:, d0:d0 + dw], ps[:, 0:dw]),
                            deps=[tk_m] + war_o)
                war_o = []
                done_p(tk_e)
                evs.append(tk_e)
            done_w(evs[-1])
            dest, src = store_ap(n, c0, cw, ob)
            tk_s = p.do("pool", lambda e, dma, dmat, dest=dest, src=src:
                        dma(dest, src), deps=evs, dmas=1)
            done_o(tk_s)
            out_toks.append(tk_s)
            yield


def pump(t, k):
    while k > 0 and t.bg:
        try:
            next(t.bg[0])
            k -= 1
        except StopIteration:
            t.bg.pop(0)


def emit_attn_a_kv(p, t, hb, sc):
    def ldw(e, dma, dmat):
        for a in range(DT):
            dma(t.ak_wk[:, a * 2048:(a + 1) * 2048],
                hb["w_qkvT"][a * 128:(a + 1) * 128, 2048:4096])
            dma(t.ak_wv[:, a * 2048:(a + 1) * 2048],
                hb["w_qkvT"][a * 128:(a + 1) * 128, 4096:6144])
    tk_w = p.do("pool", ldw, dmas=2 * DT)

    for g in range(16):
        pump(t, 6)
        lnT, war_ln, done_ln = t.ak_lnrot.next()
        rows = [hb["xb"][(g * 2 + tt) * 128:(g * 2 + tt + 1) * 128, :]
                for tt in range(2)]
        tr_toks = emit_ln_group(p, t, rows, lnT, 2, war_ln,
                                t.ak_xrot, t.ak_lbrot, lnt_w=256)
        last_k = None
        for ot in range(DT):
            ps, war_p, done_p = t.ak_psrot.next()

            def mmk(e, ot=ot, ps=ps, lnT=lnT):
                for a in range(DT):
                    e.matmul(ps[:, 0:256],
                             t.ak_wk[:, a * 2048 + ot * 128:
                                     a * 2048 + (ot + 1) * 128],
                             lnT[:, a * 256:(a + 1) * 256],
                             start=(a == 0), stop=False)
                return e.matmul(ps[:, 0:256],
                                t.c_bqkv[0:1, 2048 + ot * 128:
                                         2048 + (ot + 1) * 128],
                                t.c_ones[0:1, 0:256], start=False, stop=True)
            tk_m = p.do("pe", mmk, deps=tr_toks + [tk_w] + war_p)
            kb, war_k, done_k = t.ak_krot.next()
            tk_e = p.do("act", lambda e, kb=kb, ps=ps: e.copy(
                kb[:, 0:256], ps[:, 0:256]), deps=[tk_m] + war_k)
            done_p(tk_e)
            tk_s = p.do("sp", lambda e, dma, dmat, ot=ot, g=g, kb=kb: dma(
                sc["KT"][ot * 128:(ot + 1) * 128, g * 256:(g + 1) * 256],
                kb[:, 0:256]), deps=[tk_e], dmas=1)
            done_k(tk_s)
            last_k = tk_s
        last_v = None
        for tt in range(2):
            vb, war_v, done_v = t.ak_vrot.next()
            ev_toks = []
            for oc in range(4):
                ps, war_p, done_p = t.ak_psrot.next()

                def mmv(e, tt=tt, oc=oc, ps=ps, lnT=lnT):
                    for a in range(DT):
                        e.matmul(ps[:, 0:512],
                                 lnT[:, a * 256 + tt * 128:
                                     a * 256 + (tt + 1) * 128],
                                 t.ak_wv[:, a * 2048 + oc * 512:
                                         a * 2048 + (oc + 1) * 512],
                                 start=(a == 0), stop=False)
                    return e.matmul(ps[:, 0:512], t.c_ones[0:1, 0:128],
                                    t.c_bqkv[0:1, 4096 + oc * 512:
                                             4096 + (oc + 1) * 512],
                                    start=False, stop=True)
                tk_m = p.do("pe", mmv, deps=tr_toks + [tk_w] + war_p)
                tk_e = p.do("act", lambda e, oc=oc, vb=vb, ps=ps: e.copy(
                    vb[:, oc * 512:(oc + 1) * 512], ps[:, 0:512]),
                    deps=[tk_m] + war_v)
                war_v = []
                done_p(tk_e)
                ev_toks.append(tk_e)
            tile = g * 2 + tt
            tk_s = p.do("sp", lambda e, dma, dmat, tile=tile, vb=vb: dma(
                sc["VV"][tile * 128:(tile + 1) * 128, :], vb[:]),
                deps=ev_toks, dmas=1)
            done_v(tk_s)
            last_v = tk_s
        done_ln(last_k)
        done_ln(last_v)
    p.barrier(exclude_pool=True)


def emit_attn_a_q(p, t, hb, sc):
    def ldwq(e, dma, dmat):
        for a in range(DT):
            dma(t.aq_wq[:, a * 2048:(a + 1) * 2048],
                hb["w_qkvT"][a * 128:(a + 1) * 128, 0:2048])
    tk_w = p.do("pool", ldwq, dmas=DT)

    qgroups = [(0, 4), (4, 4), (8, 4), (12, 4), (16, 4), (20, 1)]
    for (g0, gn) in qgroups:
        pump(t, 16)
        gw = gn * 128
        lnT, war_ln, done_ln = t.aq_lnrot.next()
        rows = [hb["xs"][(g0 + tt) * 128:(g0 + tt + 1) * 128, :]
                for tt in range(gn)]
        tr_toks = emit_ln_group(p, t, rows, lnT, gn, war_ln,
                                t.ak_xrot, t.ak_lbrot)
        last_q = None
        for ot in range(DT):
            ps, war_p, done_p = t.ak_psrot.next()

            def mmq(e, ot=ot, gw=gw, ps=ps, lnT=lnT):
                for a in range(DT):
                    e.matmul(ps[:, 0:gw],
                             t.aq_wq[:, a * 2048 + ot * 128:
                                     a * 2048 + (ot + 1) * 128],
                             lnT[:, a * 512:a * 512 + gw],
                             start=(a == 0), stop=False)
                return e.matmul(ps[:, 0:gw],
                                t.c_bqkv[0:1, ot * 128:(ot + 1) * 128],
                                t.c_ones[0:1, 0:gw], start=False, stop=True)
            tk_m = p.do("pe", mmq, deps=tr_toks + [tk_w] + war_p)
            kb, war_k, done_k = t.ak_krot.next()
            tk_e = p.do("act", lambda e, kb=kb, gw=gw, ps=ps: e.copy(
                kb[:, 0:gw], ps[:, 0:gw]), deps=[tk_m] + war_k)
            done_p(tk_e)
            tk_s = p.do("sp", lambda e, dma, dmat, ot=ot, g0=g0, gw=gw, kb=kb:
                        dma(sc["QT"][ot * 128:(ot + 1) * 128,
                                     g0 * 128:g0 * 128 + gw],
                            kb[:, 0:gw]), deps=[tk_e], dmas=1)
            done_k(tk_s)
            last_q = tk_s
        done_ln(last_q)
    p.barrier(exclude_pool=True)


def emit_attn_a_flash(p, t, sc):
    fgroups = [(0, 5), (5, 4), (9, 4), (13, 4), (17, 4)]
    g_war = []
    qt_r = t.fl_un.rearrange("p (a c) -> p a c", a=DT)
    for (g0, gn) in fgroups:
        gw = gn * 128

        def ldq(e, dma, dmat, g0=g0, gw=gw):
            dma(qt_r[:, :, 0:gw],
                sc["QT"].rearrange("(a p) c -> p a c", p=128)[
                    :, :, g0 * 128:g0 * 128 + gw])
        tk_q = p.do("sp", ldq, deps=t.fl_qt_war, dmas=1)
        t.fl_qt_war = []

        exp_toks = []
        for kc in range(8):
            pump(t, 3)
            ktc, war_kt, done_kt = t.fl_kvrot.next()

            def ldk(e, dma, dmat, kc=kc, ktc=ktc):
                dma(ktc.rearrange("p (a c) -> p a c", a=DT),
                    sc["KT"].rearrange("(a p) c -> p a c", p=128)[
                        :, :, kc * 512:(kc + 1) * 512])
            tk_k = p.do("sp", ldk, deps=war_kt + g_war + [tk_q], dmas=1)
            for qi in range(gn):
                ps, war_p, done_p = t.ak_psrot.next()

                def mms(e, qi=qi, ps=ps, ktc=ktc):
                    last = None
                    for a in range(DT):
                        last = e.matmul(
                            ps[:, 0:512],
                            t.fl_un[:, a * QT_W + qi * 128:
                                    a * QT_W + (qi + 1) * 128],
                            ktc[:, a * 512:(a + 1) * 512],
                            start=(a == 0), stop=(a == DT - 1))
                    return last
                tk_m = p.do("pe", mms, deps=[tk_q, tk_k] + war_p)
                col = qi * 8 + kc
                tk_x = p.do("dve", lambda e, col=col, ps=ps: e.reduce_max(
                    t.fl_mx[:, col:col + 1], ps[:, 0:512], AX.X),
                    deps=[tk_m] + g_war)
                tk_b = p.do("dve", lambda e, col=col: e.tensor_scalar(
                    t.fl_bb[:, col:col + 1], t.fl_mx[:, col:col + 1],
                    -SC_A, None, ALU.mult), deps=[tk_x])
                tk_e = p.do("act", lambda e, qi=qi, kc=kc, col=col, ps=ps:
                            e.activation(
                    t.fl_probs[:, qi * 4096 + kc * 512:
                               qi * 4096 + (kc + 1) * 512],
                    ps[:, 0:512], AF.Exp, bias=t.fl_bb[:, col:col + 1],
                    scale=SC_A, accum_out=t.fl_sm[:, col:col + 1]),
                    deps=[tk_b] + g_war)
                done_p(tk_e)
                exp_toks.append(tk_e)
            done_kt(exp_toks[-1])
        g_war = []

        stat_toks = []
        w_toks = []
        for qi in range(gn):
            tk_g = p.do("dve", lambda e, qi=qi: e.reduce_max(
                t.fl_gmx[:, qi:qi + 1], t.fl_mx[:, qi * 8:(qi + 1) * 8], AX.X),
                deps=exp_toks)
            tk_nb = p.do("dve", lambda e, qi=qi: e.tensor_scalar(
                t.fl_gnb[:, qi:qi + 1], t.fl_gmx[:, qi:qi + 1], -SC_A, None,
                ALU.mult), deps=[tk_g])
            tk_rf = p.do("act", lambda e, qi=qi: e.activation(
                t.fl_rf[:, qi * 8:(qi + 1) * 8],
                t.fl_mx[:, qi * 8:(qi + 1) * 8],
                AF.Exp, bias=t.fl_gnb[:, qi:qi + 1], scale=SC_A),
                deps=[tk_nb])
            tk_lw = p.do("dve", lambda e, qi=qi: e.tensor_mul(
                t.fl_lw[:, qi * 8:(qi + 1) * 8],
                t.fl_sm[:, qi * 8:(qi + 1) * 8],
                t.fl_rf[:, qi * 8:(qi + 1) * 8]), deps=[tk_rf])
            tk_ls = p.do("dve", lambda e, qi=qi: e.reduce_sum(
                t.fl_l[:, qi:qi + 1], t.fl_lw[:, qi * 8:(qi + 1) * 8], AX.X),
                deps=[tk_lw])
            tk_l = p.do("dve", lambda e, qi=qi: e.reciprocal(
                t.fl_rl[:, qi:qi + 1], t.fl_l[:, qi:qi + 1]), deps=[tk_ls])
            stat_toks.append(tk_l)
            tk_w = p.do("dve", lambda e, qi=qi: e.tensor_scalar(
                t.fl_w[:, qi * 8:(qi + 1) * 8],
                t.fl_rf[:, qi * 8:(qi + 1) * 8],
                t.fl_rl[:, qi:qi + 1], None, ALU.mult), deps=[tk_l])
            w_toks.append(tk_w)

        last_cm = None
        for vh in range(2):
            pt_toks = {}
            for qi in range(gn):
                ptq = t.fl_un[:, qi * 2048:(qi + 1) * 2048]
                war0 = t.fl_ptq_war.pop(qi, [])

                def presc(e, qi=qi, vh=vh):
                    last = None
                    for c in range(4):
                        kc = vh * 4 + c
                        last = e.tensor_scalar(
                            t.fl_sc[:, c * 512:(c + 1) * 512],
                            t.fl_probs[:, qi * 4096 + kc * 512:
                                       qi * 4096 + (kc + 1) * 512],
                            t.fl_w[:, qi * 8 + kc:qi * 8 + kc + 1], None,
                            ALU.mult)
                    return last
                tk_s2 = p.do("dve", presc,
                             deps=[w_toks[qi]] + t.fl_sc_war)
                t.fl_sc_war = []
                t.fl_probs_war.append(tk_s2)
                toks = []
                for half in range(2):
                    ps_tr, war_tr, done_tr = t.trrot.next()

                    def tr8(e, half=half, ps_tr=ps_tr):
                        last = None
                        for i in range(8):
                            last = e.transpose(
                                ps_tr[:, i * 128:(i + 1) * 128],
                                t.fl_sc[:, half * 1024 + i * 128:
                                        half * 1024 + (i + 1) * 128],
                                t.c_ident[:])
                        return last
                    tk_tr = p.do("pe", tr8, deps=[tk_s2] + war_tr)
                    t.fl_sc_war.append(tk_tr)
                    tk_pte = p.do("dve", lambda e, half=half, ps_tr=ps_tr,
                                  ptq=ptq: e.tensor_copy(
                        ptq[:, half * 1024:(half + 1) * 1024],
                        ps_tr[:, 0:1024]), deps=[tk_tr] + war0)
                    war0 = []
                    done_tr(tk_pte)
                    toks.append(tk_pte)
                pt_toks[qi] = toks
            for dh in range(2):
                def ldv(e, dma, dmat, vh=vh, dh=dh):
                    dma(t.fl_kv.rearrange("p (a c) -> p a c", a=DT),
                        sc["VV"][vh * 2048:(vh + 1) * 2048,
                                 dh * 1024:(dh + 1) * 1024].rearrange(
                            "(a p) c -> p a c", p=128))
                tk_v = p.do("sp", ldv,
                            deps=t.fl_kv_war + [exp_toks[-1]], dmas=1)
                t.fl_kv_war = []
                for qi in range(gn):
                    ptq = t.fl_un[:, qi * 2048:(qi + 1) * 2048]

                    def mmc(e, qi=qi, ptq=ptq):
                        last = None
                        for kt in range(16):
                            for dc in range(2):
                                last = e.matmul(
                                    t.fl_psctx[:, dc * 512:(dc + 1) * 512],
                                    ptq[:, kt * 128:(kt + 1) * 128],
                                    t.fl_kv[:, kt * 1024 + dc * 512:
                                            kt * 1024 + (dc + 1) * 512],
                                    start=(kt == 0), stop=(kt == 15))
                        return last
                    tk_cm = p.do("pe", mmc, deps=pt_toks[qi] + [tk_v] +
                                 t.fl_ctx_war)
                    t.fl_ctx_war = []
                    t.fl_kv_war.append(tk_cm)
                    t.fl_ptq_war.setdefault(qi, []).append(tk_cm)
                    last_cm = tk_cm
                    par = t.fl_par[qi]
                    if vh == 0:
                        tk_ev = p.do("act", lambda e, dh=dh, par=par: e.copy(
                            par[:, dh * 1024:(dh + 1) * 1024], t.fl_psctx[:]),
                            deps=[tk_cm] + t.fl_par_war.pop(qi, []))
                        t.fl_ctx_war.append(tk_ev)
                    else:

                        ctxh, war_ch, done_ch = t.fl_chrot.next()
                        tk_ev = p.do("dve", lambda e, dh=dh, par=par,
                                     ctxh=ctxh: e.tensor_tensor(
                            ctxh[:], par[:, dh * 1024:(dh + 1) * 1024],
                            t.fl_psctx[:], ALU.add),
                            deps=[tk_cm] + war_ch)
                        t.fl_par_war.setdefault(qi, []).append(tk_ev)
                        t.fl_ctx_war.append(tk_ev)
                        ps_tr, war_tr, done_tr = t.trrot.next()

                        def tr8(e, ctxh=ctxh, ps_tr=ps_tr):
                            last = None
                            for i in range(8):
                                last = e.transpose(
                                    ps_tr[:, i * 128:(i + 1) * 128],
                                    ctxh[:, i * 128:(i + 1) * 128],
                                    t.c_ident[:])
                            return last
                        tk_tr = p.do("pe", tr8, deps=[tk_ev] + war_tr)
                        done_ch(tk_tr)
                        cth, war_ct, done_ct = t.fl_ctrot.next()
                        tk_te = p.do("dve", lambda e, cth=cth, ps_tr=ps_tr:
                                     e.tensor_copy(cth[:], ps_tr[:, 0:1024]),
                                     deps=[tk_tr] + war_ct)
                        done_tr(tk_te)
                        slot = g0 + qi
                        tk_ct = p.do("sp", lambda e, dma, dmat, slot=slot,
                                     dh=dh, cth=cth: dma(
                            sc["CT"][slot * 128:(slot + 1) * 128,
                                     dh * 1024:(dh + 1) * 1024].rearrange(
                                "p (a c) -> p a c", a=8),
                            cth.rearrange("p (a c) -> p a c", a=8)),
                            deps=[tk_te], dmas=1)
                        done_ct(tk_ct)
        g_war = list(t.fl_probs_war) + [last_cm]
        t.fl_probs_war = []
        t.fl_qt_war = [last_cm]
    p.barrier(exclude_pool=True)


def emit_attn_a_out(p, t, hb, sc):
    def ldwo(e, dma, dmat):
        for a in range(DT):
            dma(t.ao_wo[:, a * 2048:(a + 1) * 2048],
                hb["w_outT"][a * 128:(a + 1) * 128, :])
    tk_w = p.do("pool", ldwo, dmas=DT)

    for slot in range(NT_S12):
        pump(t, 8)
        ctb, war_ct, done_ct = t.ao_ctrot.next()
        tk_c = p.do("sp", lambda e, dma, dmat, slot=slot, ctb=ctb: dma(
            ctb[:], sc["CT"][slot * 128:(slot + 1) * 128, :]),
            deps=war_ct, dmas=1)
        xrb, war_x, done_x = t.ak_xrot.next()
        tk_x = p.do("sp", lambda e, dma, dmat, slot=slot, xrb=xrb: dma(
            xrb[:], hb["xs"][slot * 128:(slot + 1) * 128, :]),
            deps=war_x, dmas=1)
        add_toks = []
        for oc in range(4):
            ps, war_p, done_p = t.ak_psrot.next()

            def mmo(e, oc=oc, ps=ps, ctb=ctb):
                for a in range(DT):
                    e.matmul(ps[:, 0:512], ctb[:, a * 128:(a + 1) * 128],
                             t.ao_wo[:, a * 2048 + oc * 512:
                                     a * 2048 + (oc + 1) * 512],
                             start=(a == 0), stop=False)
                return e.matmul(ps[:, 0:512], t.c_ones[0:1, 0:128],
                                t.c_bout[0:1, oc * 512:(oc + 1) * 512],
                                start=False, stop=True)
            tk_m = p.do("pe", mmo, deps=[tk_c, tk_w] + war_p)
            tk_a = p.do("dve", lambda e, oc=oc, ps=ps, xrb=xrb:
                        e.tensor_tensor(
                t.ao_xnew[:, oc * 512:(oc + 1) * 512],
                xrb[:, oc * 512:(oc + 1) * 512], ps[:, 0:512], ALU.add),
                deps=[tk_m, tk_x] + t.ao_xnew_war)
            t.ao_xnew_war = []
            done_p(tk_a)
            add_toks.append(tk_a)
        done_x(add_toks[-1])
        done_ct(add_toks[-1])
        tk_o = p.do("sp", lambda e, dma, dmat, slot=slot: dma(
            sc["XR1"][slot * 128:(slot + 1) * 128, :], t.ao_xnew[:]),
            deps=add_toks, dmas=1)
        t.ao_xnew_war.append(tk_o)
    p.barrier(exclude_pool=True)


def emit_ffn(p, nc, ctx_outer, t, sc, xin, xout, n_tiles, tag):
    N = n_tiles * 128
    with ExitStack() as c2:
        sb2 = lambda n, s, d: c2.enter_context(
            nc.sbuf_tensor("sb_" + tag + n, s, d))
        pst2 = lambda n, s, d: c2.enter_context(nc.psum_tensor(tag + n, s, d))
        lnrot = Rot([sb2("fu_ln0", [128, DT * 512], BF16)])
        xfT = sb2("fu_xfT", [128, DT * N], BF16)
        wrot_u = Rot([sb2("fu_wu0", [128, DT * 256], BF16),
                      sb2("fu_wu1", [128, DT * 256], BF16)])
        wrot_g = Rot([sb2("fu_wg0", [128, DT * 256], BF16),
                      sb2("fu_wg1", [128, DT * 256], BF16)])
        psu = Rot([pst2("fu_psu0", [128, 512], F32),
                   pst2("fu_psu1", [128, 512], F32)])
        psg = Rot([pst2("fu_psg0", [128, 512], F32),
                   pst2("fu_psg1", [128, 512], F32)])
        uprot = Rot([sb2("fu_up0", [128, 512], BF16),
                     sb2("fu_up1", [128, 512], BF16)])
        gtrot = Rot([sb2("fu_gt0", [128, 512], BF16),
                     sb2("fu_gt1", [128, 512], BF16)])
        hrot = Rot([sb2("fu_h0", [128, 512], BF16),
                    sb2("fu_h1", [128, 512], BF16)])
        urot = Rot([sb2("fu_u0", [128, 512], BF16),
                    sb2("fu_u1", [128, 512], BF16)])

        xft_toks = []
        t.ffn_pend = None
        pgroups = [(i * 4, min(4, n_tiles - i * 4))
                   for i in range((n_tiles + 3) // 4)]
        for (g0, gn) in pgroups:
            gw = gn * 128
            lnT, war_ln, done_ln = lnrot.next()
            rows = [xin[(g0 + tt) * 128:(g0 + tt + 1) * 128, :]
                    for tt in range(gn)]
            tr_toks = emit_ln_group(p, t, rows, lnT, gn, war_ln,
                                    t.ak_xrot, t.ak_lbrot)
            last = None
            for a in range(DT):
                ps, war_p, done_p = t.ak_psrot.next()
                tk_m = p.do("pe", lambda e, a=a, gw=gw, ps=ps, lnT=lnT:
                            e.matmul(
                    ps[:, 0:gw], t.c_sh128[:, a * 128:(a + 1) * 128],
                    lnT[:, a * 512:a * 512 + gw], start=True, stop=True),
                    deps=tr_toks + war_p)
                tk_e = p.do("dve", lambda e, a=a, g0=g0, gw=gw, ps=ps:
                            e.tensor_copy(
                    xfT[:, a * N + g0 * 128:a * N + g0 * 128 + gw],
                    ps[:, 0:gw]), deps=[tk_m])
                done_p(tk_e)
                last = tk_e
            done_ln(last)
            xft_toks.append(last)

        tchunks = []
        c0 = 0
        while c0 < N:
            tchunks.append((c0, min(512, N - c0)))
            c0 += 512
        for c in range(32):
            wub, war_wu, done_wu = wrot_u.next()
            wgb, war_wg, done_wg = wrot_g.next()

            def ldw(e, dma, dmat, c=c, wub=wub, wgb=wgb):
                dma(wub[:], sc["C_up"][c * 128:(c + 1) * 128, :])
                dma(wgb[:], sc["C_gate"][c * 128:(c + 1) * 128, :])
            tk_w = p.do("sp", ldw, deps=war_wu + war_wg + t.fold_up +
                        t.fold_gate, dmas=2)

            last_store = None

            def flush_mid(pend):
                og, t0, tw, hb_, done_h, tk_h = pend
                pm, war_pm, done_pm = t.ak_psrot.next()
                tk_mm = p.do("pe", lambda e, tw=tw, hb_=hb_, pm=pm:
                             e.matmul(pm[:, 0:tw], t.c_h128[:],
                                      hb_[:, 0:tw], start=True, stop=True),
                             deps=[tk_h] + war_pm)
                done_h(tk_mm)
                ub, war_u, done_u = urot.next()
                tk_ue = p.do("act", lambda e, tw=tw, pm=pm, ub=ub: e.copy(
                    ub[:, 0:tw], pm[:, 0:tw]), deps=[tk_mm] + war_u)
                done_pm(tk_ue)
                tk_us = p.do("sp", lambda e, dma, dmat, og=og, t0=t0,
                             tw=tw, ub=ub: dma(
                    sc["US"][og * 128:(og + 1) * 128, t0:t0 + tw],
                    ub[:, 0:tw]), deps=[tk_ue], dmas=1)
                done_u(tk_us)
                return tk_us

            for ot in range(2):
                og = c * 2 + ot
                for (t0, tw) in tchunks:
                    pu, war_pu, done_pu = psu.next()
                    pg, war_pg, done_pg = psg.next()

                    def mmu(e, ot=ot, t0=t0, tw=tw, wub=wub, pu=pu):
                        last = None
                        for a in range(DT):
                            last = e.matmul(
                                pu[:, 0:tw],
                                wub[:, a * 256 + ot * 128:
                                    a * 256 + (ot + 1) * 128],
                                xfT[:, a * N + t0:a * N + t0 + tw],
                                start=(a == 0), stop=(a == DT - 1))
                        return last
                    tk_mu = p.do("pe", mmu, deps=[tk_w] + xft_toks + war_pu)

                    def mmg(e, ot=ot, t0=t0, tw=tw, wgb=wgb, pg=pg):
                        last = None
                        for a in range(DT):
                            last = e.matmul(
                                pg[:, 0:tw],
                                wgb[:, a * 256 + ot * 128:
                                    a * 256 + (ot + 1) * 128],
                                xfT[:, a * N + t0:a * N + t0 + tw],
                                start=(a == 0), stop=(a == DT - 1))
                        return last
                    tk_mg = p.do("pe", mmg, deps=[tk_w] + xft_toks + war_pg)

                    if t.ffn_pend is not None:
                        last_store = flush_mid(t.ffn_pend)
                        t.ffn_pend = None

                    upb, war_ub, done_ub = uprot.next()
                    gtb, war_gb, done_gb = gtrot.next()
                    tk_eu = p.do("dve", lambda e, og=og, tw=tw, pu=pu,
                                 upb=upb: e.tensor_scalar(
                        upb[:, 0:tw], pu[:, 0:tw], t.c_vup[:, og:og + 1],
                        None, ALU.mult), deps=[tk_mu] + war_ub)
                    done_pu(tk_eu)
                    tk_eg = p.do("act", lambda e, og=og, tw=tw, pg=pg,
                                 gtb=gtb: e.activation(
                        gtb[:, 0:tw], pg[:, 0:tw], AF.Silu, bias=0.0,
                        scale=t.c_vgate[:, og:og + 1]),
                        deps=[tk_mg] + war_gb)
                    done_pg(tk_eg)
                    hb_, war_h, done_h = hrot.next()
                    tk_h = p.do("dve", lambda e, og=og, tw=tw, hb_=hb_,
                                gtb=gtb, upb=upb: e.scalar_tensor_tensor(
                        hb_[:, 0:tw], gtb[:, 0:tw], t.c_sudp[:, og:og + 1],
                        upb[:, 0:tw], ALU.mult, ALU.mult),
                        deps=[tk_eu, tk_eg] + war_h)
                    done_ub(tk_h)
                    done_gb(tk_h)
                    t.ffn_pend = (og, t0, tw, hb_, done_h, tk_h)
            if t.ffn_pend is not None:
                last_store = flush_mid(t.ffn_pend)
                t.ffn_pend = None
            done_wu(last_store)
            done_wg(last_store)
        p.barrier()

    with ExitStack() as c2:
        sb2 = lambda n, s, d: c2.enter_context(
            nc.sbuf_tensor("sb_" + tag + n, s, d))
        pst2 = lambda n, s, d: c2.enter_context(nc.psum_tensor(tag + n, s, d))
        usb = sb2("fd_usb", [128, IT * USB_W], BF16)
        wdrot = Rot([sb2("fd_wd0", [128, IT * 128], BF16),
                     sb2("fd_wd1", [128, IT * 128], BF16)])
        psd = Rot([pst2("fd_psd0", [128, 512], F32),
                   pst2("fd_psd1", [128, 512], F32)])
        yT = sb2("fd_yT", [128, DT * USB_W], BF16)
        xnew = sb2("fd_xnew", [128, D], F32)
        usb_war = []
        xnew_war = []

        if n_tiles == 21:
            dgroups = [(0, 5), (5, 4), (9, 4), (13, 4), (17, 4)]
        else:
            dgroups = [(0, 4), (4, 4), (8, 4), (12, 4)]
        for (g0, gn) in dgroups:
            gN = gn * 128

            def ldu(e, dma, dmat, g0=g0, gN=gN):
                dma(usb.rearrange("p (a c) -> p a c", a=IT)[:, :, 0:gN],
                    sc["US"].rearrange("(a p) c -> p a c", p=128)[
                        :, :, g0 * 128:g0 * 128 + gN])
            tk_u = p.do("sp", ldu, deps=usb_war, dmas=1)
            usb_war = []

            y_toks = []
            for ot in range(DT):
                wdb, war_wd, done_wd = wdrot.next()

                def ldwd(e, dma, dmat, ot=ot, wdb=wdb):
                    dma(wdb[:], sc["C_down"][ot * 128:(ot + 1) * 128, :])
                tk_wd = p.do("sp", ldwd, deps=war_wd + t.fold_down, dmas=1)
                for ci in range((gN + 511) // 512):
                    c0 = ci * 512
                    cw = min(512, gN - c0)
                    ps, war_p, done_p = psd.next()

                    def mmd(e, c0=c0, cw=cw, wdb=wdb, ps=ps):
                        last = None
                        for a in range(IT):
                            last = e.matmul(
                                ps[:, 0:cw], wdb[:, a * 128:(a + 1) * 128],
                                usb[:, a * USB_W + c0:a * USB_W + c0 + cw],
                                start=(a == 0), stop=(a == IT - 1))
                        return last
                    tk_md = p.do("pe", mmd, deps=[tk_wd, tk_u] + war_p)
                    tk_ed = p.do("act", lambda e, ot=ot, c0=c0, cw=cw, ps=ps:
                                 e.activation(
                        yT[:, ot * USB_W + c0:ot * USB_W + c0 + cw],
                        ps[:, 0:cw], AF.Identity, bias=0.0,
                        scale=t.c_vdp[:, ot:ot + 1]), deps=[tk_md])
                    done_p(tk_ed)
                    done_wd(tk_ed)
                    y_toks.append(tk_ed)
            last_o = None
            for i in range(gn):
                tile = g0 + i
                xrb, war_x, done_x = t.ak_xrot.next()
                tk_x = p.do("sp", lambda e, dma, dmat, tile=tile, xrb=xrb: dma(
                    xrb[:], xin[tile * 128:(tile + 1) * 128, :]),
                    deps=war_x, dmas=1)
                add_toks = []
                for half in range(2):
                    ps, war_p, done_p = t.trrot.next()

                    def tr(e, i=i, half=half, ps=ps):
                        last = None
                        for k in range(8):
                            a = half * 8 + k
                            last = e.transpose(
                                ps[:, k * 128:(k + 1) * 128],
                                yT[:, a * USB_W + i * 128:
                                   a * USB_W + (i + 1) * 128],
                                t.c_ident[:])
                        return last
                    tk_tr = p.do("pe", tr, deps=y_toks + war_p)

                    def addres(e, half=half, ps=ps, xrb=xrb):
                        last = None
                        for k in range(2):
                            cc = half * 1024 + k * 512
                            last = e.tensor_tensor(
                                xnew[:, cc:cc + 512], xrb[:, cc:cc + 512],
                                ps[:, k * 512:(k + 1) * 512], ALU.add)
                        return last
                    tk_a = p.do("dve", addres,
                                deps=[tk_tr, tk_x] + xnew_war)
                    xnew_war = []
                    done_p(tk_a)
                    add_toks.append(tk_a)
                done_x(add_toks[-1])
                tk_o = p.do("sp", lambda e, dma, dmat, tile=tile: dma(
                    xout[tile * 128:(tile + 1) * 128, :], xnew[:]),
                    deps=add_toks, dmas=1)
                xnew_war.append(tk_o)
                last_o = tk_o
            usb_war.append(last_o)
        p.barrier()


def emit_kvrope(p, t, hb, sc):
    def ldcs(e, dma, dmat):
        dma(t.kr_cos[:], hb["cosq"])
        dma(t.kr_sin[:], hb["sinq"])
    tk_cs = p.do("sp", ldcs, dmas=2)

    def ldw(e, dma, dmat):
        for a in range(DT):
            dma(t.kr_wq[:, a * 2048:(a + 1) * 2048],
                hb["wq"][a * 128:(a + 1) * 128, :])
            dma(t.kr_wkv[:, a * 1024:(a + 1) * 1024],
                hb["wkv"][a * 128:(a + 1) * 128, :])
    tk_w = p.do("pool", ldw, dmas=2 * DT)

    def rope_block(src_bf, gw, g0, outbf, deps):
        cos_sl = t.kr_cos[:, g0 * 128:g0 * 128 + gw]
        sin_sl = t.kr_sin[:, g0 * 128:g0 * 128 + gw]
        psr, war_r, done_r = t.kr_psr.next()
        tk_r = p.do("pe", lambda e: e.matmul(
            psr[:, 0:gw], t.c_rope[:], src_bf, start=True, stop=True),
            deps=deps + war_r)
        tb, war_t, done_t = t.kr_t1rot.next()
        tk_1 = p.do("dve", lambda e: e.tensor_tensor(
            tb[:, 0:gw], psr[:, 0:gw], sin_sl, ALU.mult),
            deps=[tk_r] + war_t)
        done_r(tk_1)
        tk_2 = p.do("dve", lambda e: e.tensor_tensor(
            t.kr_t2[:, 0:gw], src_bf, cos_sl, ALU.mult),
            deps=list(deps) + t.kr_t2_war)
        tk_3 = p.do("dve", lambda e: e.tensor_tensor(
            outbf, tb[:, 0:gw], t.kr_t2[:, 0:gw], ALU.add),
            deps=[tk_1, tk_2])
        t.kr_t2_war = [tk_3]
        done_t(tk_3)
        return tk_3

    kgroups = [(0, 4), (4, 4), (8, 4), (12, 4), (16, 4), (20, 1)]
    for (g0, gn) in kgroups:
        gw = gn * 128
        own = (0 < g0 < 20)
        lnT, war_ln, done_ln = t.kr_lnrot.next()
        rows = [sc["XR2"][(g0 + tt) * 128:(g0 + tt + 1) * 128, :]
                for tt in range(gn)]
        tr_toks = emit_ln_group(p, t, rows, lnT, gn, war_ln,
                                t.ak_xrot, t.ak_lbrot)
        last_tok = None
        for ot in range(4):
            ps, war_p, done_p = t.ak_psrot.next()

            def mmk(e, ot=ot, gw=gw, ps=ps, lnT=lnT):
                for a in range(DT):
                    e.matmul(ps[:, 0:gw],
                             t.kr_wkv[:, a * 1024 + ot * 128:
                                      a * 1024 + (ot + 1) * 128],
                             lnT[:, a * 512:a * 512 + gw],
                             start=(a == 0), stop=False)
                return e.matmul(ps[:, 0:gw],
                                t.c_bkv[0:1, ot * 128:(ot + 1) * 128],
                                t.c_ones[0:1, 0:gw], start=False, stop=True)
            tk_m = p.do("pe", mmk, deps=tr_toks + [tk_w] + war_p)
            kb, war_k, done_k = t.ak_krot.next()
            tk_e = p.do("act", lambda e, gw=gw, kb=kb, ps=ps: e.copy(
                kb[:, 0:gw], ps[:, 0:gw]), deps=[tk_m] + war_k)
            done_p(tk_e)
            rob, war_ro, done_ro = t.kr_rorot.next()
            tk_ro = rope_block(kb[:, 0:gw], gw, g0, rob[:, 0:gw],
                               [tk_e, tk_cs] + war_ro)
            done_k(tk_ro)
            tk_s = p.do("sp", lambda e, dma, dmat, ot=ot, g0=g0, gw=gw,
                        rob=rob: dma(
                sc["K3T"][ot * 128:(ot + 1) * 128, g0 * 128:g0 * 128 + gw],
                rob[:, 0:gw]), deps=[tk_ro], dmas=1)
            done_ro(tk_s)
            last_tok = tk_s
        for tt in range(gn):
            slot = g0 + tt
            ps, war_p, done_p = t.ak_psrot.next()

            def mmv(e, tt=tt, ps=ps, lnT=lnT):
                for a in range(DT):
                    e.matmul(ps[:, 0:512],
                             lnT[:, a * 512 + tt * 128:
                                 a * 512 + (tt + 1) * 128],
                             t.kr_wkv[:, a * 1024 + 512:(a + 1) * 1024],
                             start=(a == 0), stop=False)
                return e.matmul(ps[:, 0:512], t.c_ones[0:1, 0:128],
                                t.c_bkv[0:1, 512:1024], start=False, stop=True)
            tk_m = p.do("pe", mmv, deps=tr_toks + [tk_w] + war_p)
            kb, war_k, done_k = t.ak_krot.next()
            tk_e = p.do("act", lambda e, kb=kb, ps=ps: e.copy(
                kb[:, 0:512], ps[:, 0:512]), deps=[tk_m] + war_k)
            done_p(tk_e)
            tk_s = p.do("sp", lambda e, dma, dmat, slot=slot, kb=kb: dma(
                sc["V3"][slot * 128:(slot + 1) * 128, :], kb[:, 0:512]),
                deps=[tk_e], dmas=1)
            done_k(tk_s)
            last_tok = tk_s
        if own:
            for ot in range(DT):
                ps, war_p, done_p = t.ak_psrot.next()

                def mmq(e, ot=ot, ps=ps, lnT=lnT):
                    for a in range(DT):
                        e.matmul(ps[:, 0:512],
                                 t.kr_wq[:, a * 2048 + ot * 128:
                                         a * 2048 + (ot + 1) * 128],
                                 lnT[:, a * 512:(a + 1) * 512],
                                 start=(a == 0), stop=False)
                    return e.matmul(ps[:, 0:512],
                                    t.c_bq[0:1, ot * 128:(ot + 1) * 128],
                                    t.c_ones[0:1, 0:512],
                                    start=False, stop=True)
                tk_m = p.do("pe", mmq, deps=tr_toks + [tk_w] + war_p)
                kb, war_k, done_k = t.ak_krot.next()
                tk_e = p.do("act", lambda e, kb=kb, ps=ps: e.copy(
                    kb[:, 0:512], ps[:, 0:512]), deps=[tk_m] + war_k)
                done_p(tk_e)
                rob, war_ro, done_ro = t.kr_rorot.next()
                tk_ro = rope_block(kb[:, 0:512], 512, g0, rob[:, 0:512],
                                   [tk_e, tk_cs] + war_ro)
                done_k(tk_ro)
                tk_s = p.do("sp", lambda e, dma, dmat, ot=ot, g0=g0, rob=rob:
                            dma(sc["Q3T"][ot * 128:(ot + 1) * 128,
                                          (g0 - OWN0) * 128:
                                          (g0 - OWN0) * 128 + 512],
                                rob[:, 0:512]), deps=[tk_ro], dmas=1)
                done_ro(tk_s)
                last_tok = tk_s
        done_ln(last_tok)
    p.barrier()


def emit_attn_b(p, t, hb, sc):
    def ldk(e, dma, dmat):
        dma(t.ab_k3.rearrange("p (a c) -> p a c", a=4),
            sc["K3T"].rearrange("(a p) c -> p a c", p=128))
        dma(t.ab_v3.rearrange("p (a c) -> p a c", a=NT_S12),
            sc["V3"].rearrange("(a p) c -> p a c", p=128))
    tk_k = p.do("sp", ldk, dmas=2)

    def ldwo(e, dma, dmat):
        for a in range(DT):
            dma(t.ab_wo[:, a * 2048:(a + 1) * 2048],
                hb["wo"][a * 128:(a + 1) * 128, :])
    tk_w = p.do("pool", ldwo, dmas=DT)

    for j in range(NT_OWN):
        qb, war_q, done_q = t.ab_qrot.next()
        tk_q = p.do("sp", lambda e, dma, dmat, j=j, qb=qb: dma(
            qb.rearrange("p (a c) -> p a c", a=DT),
            sc["Q3T"].rearrange("(a p) c -> p a c", p=128)[
                :, :, j * 128:(j + 1) * 128]), deps=war_q, dmas=1)
        mb, war_m, done_m = t.ab_mrot.next()
        tk_m = p.do("sp", lambda e, dma, dmat, j=j, mb=mb: dma(
            mb[:], hb["masks"][j * 128:(j + 1) * 128, :]), deps=war_m, dmas=1)
        ctx_ev = []
        for g in range(4):
            pt_done = []
            pt_toks = []
            for hh in range(4):
                qh = g * 4 + hh
                ps, war_p, done_p = t.ab_scrot.next()

                def mms(e, j=j, g=g, qh=qh, ps=ps, qb=qb):
                    e.matmul(ps[:, 0:512],
                             qb[:, qh * 128:(qh + 1) * 128],
                             t.ab_k3[:, g * N_S12 + j * 128:
                                     g * N_S12 + j * 128 + 512],
                             start=True, stop=True)
                    e.matmul(ps[:, 512:640],
                             qb[:, qh * 128:(qh + 1) * 128],
                             t.ab_k3[:, g * N_S12 + j * 128 + 512:
                                     g * N_S12 + j * 128 + 640],
                             start=True, stop=False)
                    return e.matmul(ps[:, 640:768],
                                    qb[:, qh * 128:(qh + 1) * 128],
                                    t.ab_k3[:, g * N_S12 + 2560:
                                            g * N_S12 + 2688],
                                    start=False, stop=True)
                tk_sc = p.do("pe", mms, deps=[tk_k, tk_q] + war_p)
                tk_msk = p.do("dve", lambda e, ps=ps, mb=mb:
                              e.scalar_tensor_tensor(
                    t.ab_s[:], ps[:, 0:768], SC_B, mb[:], ALU.mult, ALU.add),
                    deps=[tk_sc, tk_m] + t.ab_s_war)
                t.ab_s_war = []
                done_p(tk_msk)
                tk_mx = p.do("dve", lambda e: e.reduce_max(
                    t.ab_nmx[:], t.ab_s[:], AX.X, negate=True),
                    deps=[tk_msk])
                prb, war_pr, done_pr = t.ab_prrot.next()
                tk_ex = p.do("act", lambda e, prb=prb: e.activation(
                    prb[:], t.ab_s[:], AF.Exp, bias=t.ab_nmx[:], scale=1.0,
                    accum_out=t.ab_l[:]), deps=[tk_mx] + war_pr)
                t.ab_s_war.append(tk_ex)

                tk_rc = p.do("dve", lambda e: e.reciprocal(
                    t.ab_rl[:], t.ab_l[:]), deps=[tk_ex])
                tk_d = p.do("dve", lambda e: e.tensor_scalar(
                    t.ab_dmat[:], t.c_ident[:], t.ab_rl[:], None, ALU.mult),
                    deps=[tk_rc] + t.ab_dm_war)
                t.ab_dm_war = []
                ptb, war_pt, done_pt = t.ab_ptrot.next()
                ev_toks = []
                for piece, (k0, kn) in enumerate([(0, 4), (4, 2)]):
                    psx, war_px, done_px = t.ak_psrot.next()

                    def trp(e, piece=piece, k0=k0, kn=kn, prb=prb, psx=psx):
                        last = None
                        for ks in range(kn):
                            last = e.matmul(
                                psx[:, ks * 128:(ks + 1) * 128],
                                prb[:, (k0 + ks) * 128:(k0 + ks + 1) * 128],
                                t.ab_dmat[:], start=(ks == 0),
                                stop=(ks == kn - 1))
                        return last
                    tk_tp = p.do("pe", trp, deps=[tk_d] + war_px)
                    t.ab_dm_war.append(tk_tp)
                    tk_pte = p.do("dve", lambda e, k0=k0, kn=kn, psx=psx,
                                  ptb=ptb: e.tensor_copy(
                        ptb[:, k0 * 128:(k0 + kn) * 128],
                        psx[:, 0:kn * 128]), deps=[tk_tp] + war_pt)
                    war_pt = []
                    done_px(tk_pte)
                    ev_toks.append(tk_pte)
                done_pr(ev_toks[-1])
                pt_toks.append((hh, ptb, done_pt, ev_toks))
            psc, war_c, done_c = t.ab_ctxps.next()

            def mmc(e, j=j, g=g, pt_toks=pt_toks, psc=psc):
                last = None
                for ks in range(6):
                    vslot = j + ks if ks < 5 else 20
                    for (hh, ptb, _, _) in pt_toks:
                        last = e.matmul(
                            psc[:, hh * 128:(hh + 1) * 128],
                            t.ab_v3[:, vslot * 512 + g * 128:
                                    vslot * 512 + (g + 1) * 128],
                            ptb[:, ks * 128:(ks + 1) * 128],
                            start=(ks == 0 and hh == 0),
                            stop=(ks == 5 and hh == 3))
                return last
            all_pt = [tk for (_, _, _, evs) in pt_toks for tk in evs]
            tk_cm = p.do("pe", mmc, deps=all_pt + war_c + [tk_k])
            for (_, _, done_pt, _) in pt_toks:
                done_pt(tk_cm)
            tk_ce = p.do("act", lambda e, g=g, psc=psc: e.copy(
                t.ab_ctxT[:, g * 512:(g + 1) * 512], psc[:, 0:512]),
                deps=[tk_cm] + t.ab_ctxT_war)
            t.ab_ctxT_war = []
            done_c(tk_ce)
            ctx_ev.append(tk_ce)
        done_m(ctx_ev[-1])
        done_q(ctx_ev[-1])
        xrb, war_x, done_x = t.ak_xrot.next()
        slot = OWN0 + j
        tk_x = p.do("sp", lambda e, dma, dmat, slot=slot, xrb=xrb: dma(
            xrb[:], sc["XR2"][slot * 128:(slot + 1) * 128, :]),
            deps=war_x, dmas=1)
        add_toks = []
        for oc in range(4):
            ps, war_p, done_p = t.ak_psrot.next()

            def mmo(e, oc=oc, ps=ps):
                for a in range(DT):
                    e.matmul(ps[:, 0:512],
                             t.ab_ctxT[:, a * 128:(a + 1) * 128],
                             t.ab_wo[:, a * 2048 + oc * 512:
                                     a * 2048 + (oc + 1) * 512],
                             start=(a == 0), stop=False)
                return e.matmul(ps[:, 0:512], t.c_ones[0:1, 0:128],
                                t.c_bo[0:1, oc * 512:(oc + 1) * 512],
                                start=False, stop=True)
            tk_mo = p.do("pe", mmo, deps=ctx_ev + [tk_w] + war_p)
            t.ab_ctxT_war.append(tk_mo)
            tk_a = p.do("dve", lambda e, oc=oc, ps=ps, xrb=xrb:
                        e.tensor_tensor(
                t.ab_xnew[:, oc * 512:(oc + 1) * 512],
                xrb[:, oc * 512:(oc + 1) * 512], ps[:, 0:512], ALU.add),
                deps=[tk_mo, tk_x] + t.ab_xnew_war)
            t.ab_xnew_war = []
            done_p(tk_a)
            add_toks.append(tk_a)
        done_x(add_toks[-1])
        tk_o = p.do("sp", lambda e, dma, dmat, slot=slot: dma(
            sc["XR3"][slot * 128:(slot + 1) * 128, :], t.ab_xnew[:]),
            deps=add_toks, dmas=1)
        t.ab_xnew_war.append(tk_o)
    p.barrier()


def build_nc():
    nc = bass.Bass("TRN2", target_bir_lowering=False, debug=False)
    hb = {}

    def din(name, shape, dt=F32):
        hb[name] = nc.dram_tensor(name, list(shape), dt,
                                  kind="ExternalInput").ap()

    din("xb", (S, D))
    din("xs", (N_S12, D))
    din("w_qkvT", (D, 6144))
    din("b_qkv", (1, 6144))
    din("w_outT", (D, D))
    din("b_out", (1, D))
    din("wq", (D, D))
    din("bq", (1, D))
    din("wkv", (D, 1024))
    din("bkv", (1, 1024))
    din("wo", (D, D))
    din("bo", (1, D))
    din("su_ug", (16, 128))
    din("su_dp", (64, 128))
    din("ws_ug", (128, 128))
    din("ws_dp", (16, 128))
    din("wupTf", (D, INTER))
    din("wgateTf", (D, INTER))
    din("wdownTf", (INTER, D))
    din("cosq", (128, N_S12))
    din("sinq", (128, N_S12))
    din("masks", (16 * 128, 768))
    din("c_ident", (128, 128), BF16)
    din("c_identf", (128, 128), F32)
    din("c_h128", (128, 128), BF16)
    din("c_bf16", (128, 128), BF16)
    din("c_bf64", (128, 128), BF16)
    din("c_rope", (128, 128), BF16)
    din("c_ones", (1, 512), BF16)

    out = nc.dram_tensor("out", [N_OWN, D], F32, kind="ExternalOutput").ap()

    sc = {}

    def dsc(name, shape, dt=BF16):
        sc[name] = nc.dram_tensor(name, list(shape), dt, kind="Internal").ap()

    dsc("C_up", (32 * 128, DT * 256))
    dsc("C_gate", (32 * 128, DT * 256))
    dsc("C_down", (DT * 128, IT * 128))
    if DBG:
        dsc("DBGP", (N_S12, D))
    dsc("KT", (D, S))
    dsc("VV", (S, D))
    dsc("QT", (D, N_S12))
    dsc("CT", (N_S12, D))
    dsc("US", (INTER, N_S12))
    dsc("K3T", (512, N_S12))
    dsc("V3", (N_S12, 512))
    dsc("Q3T", (D, N_OWN))
    dsc("XR1", (N_S12, D), F32)
    dsc("XR2", (N_S12, D), F32)
    dsc("XR3", (N_S12, D), F32)

    t = NS()
    t.ln_prev = []
    with ExitStack() as ctx:
        p = Prog(nc, ctx)
        sb = lambda n, s, d: ctx.enter_context(nc.sbuf_tensor("sb_" + n, s, d))
        pst = lambda n, s, d: ctx.enter_context(nc.psum_tensor(n, s, d))

        t.c_ident = sb("c_ident", [128, 128], BF16)
        t.c_h128 = sb("c_h128", [128, 128], BF16)
        t.c_sh128 = sb("c_sh128", [128, DT * 128], BF16)
        t.c_bf16 = sb("c_bf16", [128, 128], BF16)
        t.c_bf64 = sb("c_bf64", [128, 128], BF16)
        t.c_rope = sb("c_rope", [128, 128], BF16)
        t.c_ones = sb("c_ones", [1, 512], BF16)
        t.c_bqkv = sb("c_bqkv", [1, 6144], BF16)
        t.c_bout = sb("c_bout", [1, D], BF16)
        t.c_bq = sb("c_bq", [1, D], BF16)
        t.c_bkv = sb("c_bkv", [1, 1024], BF16)
        t.c_bo = sb("c_bo", [1, D], BF16)
        t.c_vup = sb("c_vup", [128, 64], F32)
        t.c_vgate = sb("c_vgate", [128, 64], F32)
        t.c_vdp = sb("c_vdp", [128, 16], F32)
        t.c_suug = sb("c_suug", [128, 16], F32)
        t.c_sudp = sb("c_sudp", [128, 64], F32)
        t.bn6 = sb("bn6", [128, 24], F32)
        t.mv = sb("mv", [128, 2], F32)
        t.t0 = sb("t0", [128, 1], F32)
        t.rstd = sb("rstd", [128, 1], F32)
        t.negmr = sb("negmr", [128, 1], F32)
        t.c_eps = sb("c_eps", [128, 1], F32)
        t.trrot = Rot([pst("ps_tr0", [128, 1024], BF16),
                       pst("ps_tr1", [128, 1024], BF16)])
        t.ak_psrot = Rot([pst("ps_mm0", [128, 512], F32),
                          pst("ps_mm1", [128, 512], F32)])
        t.ak_xrot = Rot([sb("xf0", [128, D], F32), sb("xf1", [128, D], F32)])
        t.ak_lbrot = Rot([sb("lnbf0", [128, D], BF16),
                          sb("lnbf1", [128, D], BF16)])
        t.ak_krot = Rot([sb("kb0", [128, 512], BF16),
                         sb("kb1", [128, 512], BF16)])

        with ExitStack() as c2:
            sb2 = lambda n, s, d: c2.enter_context(
                nc.sbuf_tensor("sb_" + n, s, d))
            t.c_identf = sb2("c_identf", [128, 128], F32)
            t.v_wsug = sb2("v_wsug", [128, 128], F32)
            t.v_wsdp = sb2("v_wsdp", [16, 128], F32)
            t.v_suug = sb2("v_suug", [16, 128], F32)
            t.v_sudp = sb2("v_sudp", [64, 128], F32)
            t.ps_v0 = c2.enter_context(nc.psum_tensor("ps_v0", [128, 512], F32))
            t.ps_v1 = c2.enter_context(nc.psum_tensor("ps_v1", [128, 512], F32))
            emit_const_prep(p, t, hb)

        with ExitStack() as cf:
            sbf = lambda n, s, d: cf.enter_context(
                nc.sbuf_tensor("sb_" + n, s, d))
            t.fbrot = Rot([sbf("fb0", [128, 1024], BF16),
                           sbf("fb1", [128, 1024], BF16)])
            t.forot = Rot([sbf("fo0", [128, 1024], BF16),
                           sbf("fo1", [128, 1024], BF16)])
            t.fprot = Rot([cf.enter_context(nc.psum_tensor("fp0", [128, 512],
                                                           F32)),
                           cf.enter_context(nc.psum_tensor("fp1", [128, 512],
                                                           F32))])
            t.fold_up, t.fold_gate, t.fold_down = [], [], []

            def mk_ug_store(c_ap):
                c_r = c_ap.rearrange("(cb n s) (a v) -> n (s a) cb v",
                                     cb=32, n=16, s=8, a=16, v=256)

                def store(n, c0, cw, ob):
                    cb0 = c0 // 256
                    ncb = cw // 256
                    return (c_r[n][:, cb0:cb0 + ncb, :],
                            ob.rearrange("p (cb v) -> p cb v",
                                         cb=ncb, v=256))
                return store

            def mk_dn_store(c_ap):
                c_r = c_ap.rearrange("(o n s) (a u) -> n (s a) o u",
                                     o=16, n=64, s=2, a=64, u=128)

                def store(n, c0, cw, ob):
                    o0 = c0 // 128
                    no = cw // 128
                    return (c_r[n][:, o0:o0 + no, :],
                            ob.rearrange("p (o u) -> p o u", o=no, u=128))
                return store

            t.bg = [
                gen_cache_fold(p, t, hb["wupTf"], DT, t.c_bf16, INTER,
                               t.fold_up, mk_ug_store(sc["C_up"])),
                gen_cache_fold(p, t, hb["wgateTf"], DT, t.c_bf16, INTER,
                               t.fold_gate, mk_ug_store(sc["C_gate"])),
                gen_cache_fold(p, t, hb["wdownTf"], IT, t.c_bf64, D,
                               t.fold_down, mk_dn_store(sc["C_down"])),
            ]

            with ExitStack() as c2:
                sb2 = lambda n, s, d: c2.enter_context(
                    nc.sbuf_tensor("sb_" + n, s, d))
                t.ak_wk = sb2("ak_wk", [128, DT * 2048], BF16)
                t.ak_wv = sb2("ak_wv", [128, DT * 2048], BF16)
                t.ak_lnrot = Rot([sb2("ak_ln0", [128, DT * 256], BF16)])
                t.ak_vrot = Rot([sb2("ak_v0", [128, D], BF16)])
                emit_attn_a_kv(p, t, hb, sc)

            with ExitStack() as c2:
                sb2 = lambda n, s, d: c2.enter_context(
                    nc.sbuf_tensor("sb_" + n, s, d))
                t.aq_wq = sb2("aq_wq", [128, DT * 2048], BF16)
                t.aq_lnrot = Rot([sb2("aq_ln0", [128, DT * 512], BF16),
                                  sb2("aq_ln1", [128, DT * 512], BF16)])
                emit_attn_a_q(p, t, hb, sc)

            with ExitStack() as c2:
                sb2 = lambda n, s, d: c2.enter_context(
                    nc.sbuf_tensor("sb_" + n, s, d))
                pst2 = lambda n, s, d: c2.enter_context(
                    nc.psum_tensor(n, s, d))
                t.fl_un = sb2("fl_un", [128, DT * QT_W], BF16)
                t.fl_probs = sb2("fl_probs", [128, 5 * 4096], BF16)
                t.fl_kv = sb2("fl_kv", [128, DT * 1024], BF16)
                t.fl_kvrot = Rot([t.fl_kv[:, i * DT * 512:(i + 1) * DT * 512]
                                  for i in range(2)])
                t.fl_mx = sb2("fl_mx", [128, 48], F32)
                t.fl_bb = sb2("fl_bb", [128, 48], F32)
                t.fl_sm = sb2("fl_sm", [128, 48], F32)
                t.fl_gmx = sb2("fl_gmx", [128, 5], F32)
                t.fl_gnb = sb2("fl_gnb", [128, 5], F32)
                t.fl_rf = sb2("fl_rf", [128, 48], F32)
                t.fl_lw = sb2("fl_lw", [128, 48], F32)
                t.fl_l = sb2("fl_l", [128, 5], F32)
                t.fl_rl = sb2("fl_rl", [128, 5], F32)
                t.fl_w = sb2("fl_w", [128, 48], F32)
                t.fl_sc = sb2("fl_sc", [128, 2048], BF16)
                t.fl_sc_war = []
                t.fl_psctx = pst2("fl_ctx", [128, 1024], F32)
                t.fl_par = [sb2("fl_par%d" % i, [128, 2048], BF16)
                            for i in range(5)]
                t.fl_chrot = Rot([sb2("fl_ch0", [128, 1024], BF16),
                                  sb2("fl_ch1", [128, 1024], BF16)])
                t.fl_ctrot = Rot([sb2("fl_ct0", [128, 1024], BF16),
                                  sb2("fl_ct1", [128, 1024], BF16)])
                t.fl_qt_war = []
                t.fl_kv_war = []
                t.fl_probs_war = []
                t.fl_ctx_war = []
                t.fl_ptq_war = {}
                t.fl_par_war = {}
                emit_attn_a_flash(p, t, sc)

            with ExitStack() as c2:
                sb2 = lambda n, s, d: c2.enter_context(
                    nc.sbuf_tensor("sb_" + n, s, d))
                t.ao_wo = sb2("ao_wo", [128, DT * 2048], BF16)
                t.ao_ctrot = Rot([sb2("ao_ct0", [128, D], BF16),
                                  sb2("ao_ct1", [128, D], BF16)])
                t.ao_xnew = sb2("ao_xnew", [128, D], F32)
                t.ao_xnew_war = []
                emit_attn_a_out(p, t, hb, sc)

            pump(t, 10 ** 9)
            assert not t.bg
            p.barrier()

        emit_ffn(p, nc, ctx, t, sc, sc["XR1"], sc["XR2"], NT_S12, "f1_")

        with ExitStack() as c2:
            sb2 = lambda n, s, d: c2.enter_context(nc.sbuf_tensor("sb_" + n, s, d))
            pst2 = lambda n, s, d: c2.enter_context(nc.psum_tensor(n, s, d))
            t.kr_cos = sb2("kr_cos", [128, N_S12], F32)
            t.kr_sin = sb2("kr_sin", [128, N_S12], F32)
            t.kr_wq = sb2("kr_wq", [128, DT * 2048], BF16)
            t.kr_wkv = sb2("kr_wkv", [128, DT * 1024], BF16)
            t.kr_lnrot = Rot([sb2("kr_ln0", [128, DT * 512], BF16)])
            t.kr_psr = Rot([pst2("kr_psr0", [128, 512], F32),
                            pst2("kr_psr1", [128, 512], F32)])
            t.kr_t1rot = Rot([sb2("kr_t10", [128, 512], F32),
                              sb2("kr_t11", [128, 512], F32)])
            t.kr_t2 = sb2("kr_t2", [128, 512], F32)
            t.kr_t2_war = []
            t.kr_rorot = Rot([sb2("kr_ro0", [128, 512], BF16),
                              sb2("kr_ro1", [128, 512], BF16)])
            emit_kvrope(p, t, hb, sc)

        with ExitStack() as c2:
            sb2 = lambda n, s, d: c2.enter_context(nc.sbuf_tensor("sb_" + n, s, d))
            pst2 = lambda n, s, d: c2.enter_context(nc.psum_tensor(n, s, d))
            t.ab_k3 = sb2("ab_k3", [128, 4 * N_S12], BF16)
            t.ab_v3 = sb2("ab_v3", [128, NT_S12 * 512], BF16)
            t.ab_wo = sb2("ab_wo", [128, DT * 2048], BF16)
            t.ab_qrot = Rot([sb2("ab_q0", [128, DT * 128], BF16),
                             sb2("ab_q1", [128, DT * 128], BF16)])
            t.ab_mrot = Rot([sb2("ab_m0", [128, 768], F32),
                             sb2("ab_m1", [128, 768], F32)])
            t.ab_s = sb2("ab_s", [128, 768], F32)
            t.ab_nmx = sb2("ab_nmx", [128, 1], F32)
            t.ab_l = sb2("ab_l", [128, 1], F32)
            t.ab_rl = sb2("ab_rl", [128, 1], F32)
            t.ab_dmat = sb2("ab_dmat", [128, 128], BF16)
            t.ab_prrot = Rot([sb2("ab_pr0", [128, 768], BF16),
                              sb2("ab_pr1", [128, 768], BF16)])
            t.ab_ptrot = Rot([sb2("ab_pt0", [128, 768], BF16),
                              sb2("ab_pt1", [128, 768], BF16),
                              sb2("ab_pt2", [128, 768], BF16),
                              sb2("ab_pt3", [128, 768], BF16)])
            t.ab_scrot = Rot([pst2("ab_sc0", [128, 1024], F32)])
            t.ab_ctxps = Rot([pst2("ab_ctx0", [128, 512], F32),
                              pst2("ab_ctx1", [128, 512], F32)])
            t.ab_ctxT = sb2("ab_ctxT", [128, DT * 128], BF16)
            t.ab_xnew = sb2("ab_xnew", [128, D], F32)
            t.ab_s_war = []
            t.ab_dm_war = []
            t.ab_ctxT_war = []
            t.ab_xnew_war = []
            emit_attn_b(p, t, hb, sc)

        emit_ffn(p, nc, ctx, t, sc,
                 sc["XR3"][OWN0 * 128:(OWN0 + NT_OWN) * 128, :], out, NT_OWN,
                 "f2_")

        p.final_wait()
        p.check()
        with nc.Block() as block:
            p.emit_all(block)
    return nc


def _fold_perm(wT, a_factor, n_tiles):
    sgrp = 128 // a_factor
    r = wT.reshape(a_factor, n_tiles, sgrp, wT.shape[1])
    return np.ascontiguousarray(r.transpose(1, 2, 0, 3).reshape(wT.shape))


def stage_inputs(inputs):
    hc = host_constants()
    x = np.asarray(inputs["L0_attn_a_hidden_states"], np.float32)
    shared = dict(
        w_qkvT=np.ascontiguousarray(
            np.asarray(inputs["L0_attn_a_w_qkv"], np.float32).T),
        b_qkv=np.asarray(inputs["L0_attn_a_b_qkv"], np.float32).reshape(1, -1),
        w_outT=np.ascontiguousarray(
            np.asarray(inputs["L0_attn_a_w_out"], np.float32).T),
        b_out=np.asarray(inputs["L0_attn_a_b_out"], np.float32).reshape(1, -1),
        wq=np.ascontiguousarray(inputs["L0_attn_b_Wq"], np.float32),
        bq=np.asarray(inputs["L0_attn_b_bq"], np.float32).reshape(1, -1),
        wkv=np.ascontiguousarray(inputs["L0_attn_b_Wkv"], np.float32),
        bkv=np.asarray(inputs["L0_attn_b_bkv"], np.float32).reshape(1, -1),
        wo=np.ascontiguousarray(inputs["L0_attn_b_Wo"], np.float32),
        bo=np.asarray(inputs["L0_attn_b_bo"], np.float32).reshape(1, -1),
        su_ug=np.asarray(inputs["L0_ffn_SU_ug"], np.float32).reshape(16, 128),
        su_dp=np.asarray(inputs["L0_ffn_SU_dp"], np.float32).reshape(64, 128),
        ws_ug=np.asarray(inputs["L0_ffn_Wscale_ug"],
                         np.float32).reshape(128, 128),
        ws_dp=np.asarray(inputs["L0_ffn_Wscale_dp"],
                         np.float32).reshape(16, 128),
        wupTf=_fold_perm(np.ascontiguousarray(
            np.asarray(inputs["L0_ffn_W_up"], np.float32).T), 16, 16),
        wgateTf=_fold_perm(np.ascontiguousarray(
            np.asarray(inputs["L0_ffn_W_gate"], np.float32).T), 16, 16),
        wdownTf=_fold_perm(np.ascontiguousarray(
            np.asarray(inputs["L0_ffn_W_down"], np.float32).T), 64, 64),
        **hc,
    )
    in_maps = []
    for c in range(8):
        b, h = c // 2, c % 2
        tiles = slot_tiles(h)
        xb = np.ascontiguousarray(x[b])
        xs = np.ascontiguousarray(
            np.concatenate([x[b, tt * 128:(tt + 1) * 128, :] for tt in tiles],
                           axis=0))
        cos, sin = host_rope_tables(h)
        masks = host_masks(h).reshape(16 * 128, 768)
        m = dict(shared)
        m.update(xb=xb, xs=xs, cosq=cos, sinq=sin,
                 masks=np.ascontiguousarray(masks))
        in_maps.append(m)
    return in_maps


_CACHED = {}


def kernel(**inputs):
    if "nc" not in _CACHED:
        _CACHED["nc"] = build_nc()
    nc = _CACHED["nc"]
    in_maps = stage_inputs(inputs)
    res = run_bass_kernel_spmd(nc, in_maps, core_ids=list(range(8)))
    out = np.zeros((B, S, D), np.float32)
    for c in range(8):
        b, h = c // 2, c % 2
        out[b, h * 2048:(h + 1) * 2048, :] = res.results[c]["out"]
    return out


# version 06
# speedup vs baseline: 1.1382x; 1.0225x over previous
import numpy as np
import ml_dtypes
from contextlib import ExitStack

import concourse.bass as bass
import concourse.mybir as mybir
from concourse.bass_utils import run_bass_kernel_spmd

F32 = mybir.dt.float32
BF16 = mybir.dt.bfloat16
AF = mybir.ActivationFunctionType
ALU = mybir.AluOpType
AX = mybir.AxisListType
BF = ml_dtypes.bfloat16

D = 2048
DT = D // 128
INTER = 8192
IT = INTER // 128
B, S = 4, 4096
NT_S12 = 21
NT_OWN = 16
OWN0 = 4
N_S12 = NT_S12 * 128
N_OWN = NT_OWN * 128
HD = 128
WINDOW, NMETA = 512, 16
SC_A = 1.0 / float(np.sqrt(2048.0))
SC_B = 1.0 / float(np.sqrt(128.0))
EPS = 1e-5
NEG = -1.0e9
QT_W = 640
USB_W = 640

ENG = ("pe", "dve", "act", "pool", "sp")
DBG = False


def _hadamard(n):
    h = np.array([[1.0]])
    while h.shape[0] < n:
        h = np.block([[h, h], [h, -h]])
    return h


def host_constants():
    c = {}
    c["c_ident"] = np.eye(128).astype(BF)
    c["c_identf"] = np.eye(128).astype(np.float32)
    c["c_h128"] = _hadamard(128).astype(BF)
    c["c_bf16"] = np.kron(np.eye(8), _hadamard(16)).astype(BF)
    c["c_bf64"] = np.kron(np.eye(2), _hadamard(64)).astype(BF)
    R = np.zeros((128, 128), np.float32)
    for i in range(64):
        R[2 * i, 2 * i + 1] = -1.0
        R[2 * i + 1, 2 * i] = 1.0
    c["c_rope"] = np.ascontiguousarray(R.T).astype(BF)
    c["c_ones"] = np.ones((1, 512)).astype(BF)
    return c


def slot_tiles(h):
    if h == 0:
        return [16, 17, 18, 19] + list(range(0, 16)) + [0]
    return [12, 13, 14, 15] + list(range(16, 32)) + [0]


def host_rope_tables(h):
    tiles = slot_tiles(h)
    pos = np.concatenate([np.arange(t * 128, (t + 1) * 128) for t in tiles])
    inv = 1.0 / (10000.0 ** (np.arange(0, HD, 2, dtype=np.float64) / HD))
    ang = np.repeat(inv, 2)[:, None] * pos[None, :].astype(np.float64)
    return np.cos(ang).astype(np.float32), np.sin(ang).astype(np.float32)


def host_masks(h):
    tiles = slot_tiles(h)
    own_start_tile = 0 if h == 0 else 16
    m = np.full((16, 128, 768), NEG, np.float32)
    for j in range(16):
        gq = own_start_tile + j
        p = gq * 128 + np.arange(128)
        for bs in range(5):
            slot = j + bs
            gt = tiles[slot]
            if h == 0 and slot < OWN0:
                continue
            kp = gt * 128 + np.arange(128)
            ok = (kp[None, :] <= p[:, None]) & (
                (kp[None, :] >= p[:, None] - (WINDOW - 1)) | (kp[None, :] < NMETA))
            m[j, :, bs * 128:(bs + 1) * 128] = np.where(ok, 0.0, NEG)
        band_tiles = [tiles[j + bs] for bs in range(5)
                      if not (h == 0 and j + bs < OWN0)]
        if 0 not in band_tiles:
            kt = np.arange(128)
            ok = (kt[None, :] < NMETA) & (kt[None, :] <= p[:, None])
            m[j, :, 640:768] = np.where(ok, 0.0, NEG)
    return m


KL = 8
DMAQ = ("sp", "pool")


class Prog:

    def __init__(self, nc, ctx):
        self.nc = nc
        self.keys = list(ENG) + [q + str(i) for q in DMAQ for i in range(KL)]
        self.sems = {k: ctx.enter_context(nc.semaphore("sem_" + k))
                     for k in self.keys}
        self.cnt = {k: 0 for k in self.keys}
        self.steps = {e: [] for e in ENG}
        self.bar = {k: 0 for k in self.keys}
        self.lane_ptr = {q: 0 for q in DMAQ}

    def do(self, eng, emit, deps=(), dmas=0):
        need = {}
        for k, v in self.bar.items():
            if v > 0:
                need[k] = v
        for d in deps:
            if d is None:
                continue
            k, v = d
            need[k] = max(need.get(k, 0), v)
        if dmas:
            lane = self.lane_ptr[eng]
            self.lane_ptr[eng] = (lane + 1) % KL
            key = eng + str(lane)
            if self.cnt[key] > 0:
                need[key] = max(need.get(key, 0), self.cnt[key])
            self.steps[eng].append((need, emit, dmas, key))
            self.cnt[key] += 16 * dmas
        else:
            key = eng
            self.steps[eng].append((need, emit, 0, key))
            self.cnt[key] += 1
        assert self.cnt[key] < 60000, f"sem overflow risk on {key}"
        return (key, self.cnt[key])

    def barrier(self, exclude_pool=False):
        if exclude_pool:
            keep = {k: v for k, v in self.bar.items() if k.startswith("pool")}
            self.bar = {k: (keep.get(k, 0) if k.startswith("pool") else v)
                        for k, v in self.cnt.items()}
        else:
            self.bar = dict(self.cnt)

    def emit_all(self, block):
        prog = self

        def runner(eng):
            def run(engobj):
                water = {k: 0 for k in prog.keys}
                for need, emit, dmas, key in prog.steps[eng]:
                    for k in prog.keys:
                        v = need.get(k, 0)
                        if v > water[k]:
                            engobj.wait_ge(prog.sems[k], v)
                            water[k] = v
                    if dmas:
                        n_done = [0]

                        def dma(out, in_, **kw):
                            engobj.dma_start(out=out, in_=in_, **kw).then_inc(
                                prog.sems[key], 16)
                            n_done[0] += 1

                        def dmat(out, in_):
                            engobj.dma_start_transpose(out, in_).then_inc(
                                prog.sems[key], 16)
                            n_done[0] += 1
                        emit(engobj, dma, dmat)
                        assert n_done[0] == dmas, \
                            f"{eng}: {n_done[0]} != {dmas} dmas"
                    else:
                        last = emit(engobj)
                        assert last is not None, f"{eng} compute step -> None"
                        last.then_inc(prog.sems[key], 1)
            return run

        block.tensor(runner("pe"))
        block.vector(runner("dve"))
        block.scalar(runner("act"))
        block.gpsimd(runner("pool"))
        block.sync(runner("sp"))

    def check(self):
        ptr = {e: 0 for e in ENG}
        done = {k: 0 for k in self.keys}
        progress = True
        while progress:
            progress = False
            for e in ENG:
                while ptr[e] < len(self.steps[e]):
                    need, _, dmas, key = self.steps[e][ptr[e]]
                    if all(done[k] >= v for k, v in need.items()
                           if not (k == key and dmas)):
                        done[key] += 16 * dmas if dmas else 1
                        ptr[e] += 1
                        progress = True
                    else:
                        break
        stuck = {e: (ptr[e], len(self.steps[e])) for e in ENG
                 if ptr[e] < len(self.steps[e])}
        assert not stuck, f"DEADLOCK at {stuck}; counts={done}"

    def final_wait(self):
        tgt = dict(self.cnt)

        def emit(engobj):
            for k in self.keys:
                if k != "pool" and tgt[k] > 0:
                    engobj.wait_ge(self.sems[k], tgt[k])
            return engobj.nop()
        self.steps["pool"].append(({}, emit, 0, "pool"))
        self.cnt["pool"] += 1


class Rot:
    def __init__(self, bufs):
        self.bufs = bufs
        self.toks = [[] for _ in bufs]
        self.i = 0

    def next(self):
        s = self.i % len(self.bufs)
        self.i += 1
        war = list(self.toks[s])
        self.toks[s] = []

        def done(tok):
            self.toks[s].append(tok)
        return self.bufs[s], war, done


class NS:
    pass


def emit_ln_tile(p, t, xf, lnbf, deps):
    st, war_st, done_st = t.lnstat.next()

    def dve_stats(e):
        last = None
        for i in range(4):
            last = e.bn_stats(st.bn6[:, 6 * i:6 * i + 6],
                              xf[:, 512 * i:512 * (i + 1)])
        return last
    tk0 = p.do("dve", dve_stats, deps=list(deps) + war_st)
    tk1 = p.do("dve", lambda e: e.bn_aggr(st.mv[:], st.bn6[:]), deps=[tk0])

    tk2a = p.do("act", lambda e: e.activation(
        st.t0[:], st.mv[:, 1:2], AF.Ln, bias=t.c_eps[:], scale=1.0),
        deps=[tk1])
    tk2 = p.do("act", lambda e: e.activation(
        st.rstd[:], st.t0[:], AF.Exp, bias=0.0, scale=-0.5), deps=[tk2a])

    tk3 = p.do("dve", lambda e: e.tensor_scalar(
        st.negmr[:], st.mv[:, 0:1], st.rstd[:], -1.0, ALU.mult, ALU.mult),
        deps=[tk2])
    tk4 = p.do("act", lambda e: e.activation(
        lnbf[:], xf[:], AF.Identity, bias=st.negmr[:], scale=st.rstd[:]),
        deps=[tk3])
    done_st(tk4)
    return tk4


def emit_transpose16(p, t, src_bf, dst_fn, deps, evac_deps=()):
    toks = []
    ev_extra = list(evac_deps)
    for half in range(2):
        ps, war, done = t.trrot.next()

        def pe_tr(e, half=half, ps=ps):
            last = None
            for i in range(8):
                a = half * 8 + i
                last = e.transpose(ps[:, i * 128:(i + 1) * 128],
                                   src_bf[:, a * 128:(a + 1) * 128],
                                   t.c_ident[:])
            return last
        tk_pe = p.do("pe", pe_tr, deps=list(deps) + war)

        def evac(e, half=half, ps=ps):
            last = None
            for i in range(8):
                a = half * 8 + i
                last = e.tensor_copy(dst_fn(a), ps[:, i * 128:(i + 1) * 128])
            return last
        tk_ev = p.do("dve", evac, deps=[tk_pe] + ev_extra)
        ev_extra = []
        done(tk_ev)
        toks.append(tk_ev)
    return toks


def emit_ln_group(p, t, xin_rows, lnT, gn, lnT_war, xrot, lbrot, lnt_w=512):
    tr_toks = []
    ev_extra = list(lnT_war)
    for tt in range(gn):
        xf, war_x, done_x = xrot.next()
        src = xin_rows[tt]
        tk_x = p.do("sp", lambda e, dma, dmat, src=src, xf=xf: dma(xf[:], src),
                    deps=war_x, dmas=1)
        lnbf, war_l, done_l = lbrot.next()
        tk_ln = emit_ln_tile(p, t, xf, lnbf, [tk_x] + war_l)
        done_x(tk_ln)
        toks = emit_transpose16(
            p, t, lnbf,
            lambda a, tt=tt: lnT[:, a * lnt_w + tt * 128:
                                 a * lnt_w + (tt + 1) * 128],
            [tk_ln], evac_deps=ev_extra)
        ev_extra = []
        done_l(toks[-1])
        tr_toks += toks
    return tr_toks


def emit_const_prep(p, t, hb):
    def ld_consts(e, dma, dmat):
        dma(t.c_ident[:], hb["c_ident"])
        dma(t.c_identf[:], hb["c_identf"])
        dma(t.c_h128[:], hb["c_h128"])
        dma(t.c_bf16[:], hb["c_bf16"])
        dma(t.c_bf64[:], hb["c_bf64"])
        dma(t.c_rope[:], hb["c_rope"])
        dma(t.c_ones[:], hb["c_ones"])
    tk_c = p.do("sp", ld_consts, dmas=7)

    p.do("dve", lambda e: e.memset(t.c_eps[:], EPS))

    def ld_bias(e, dma, dmat):
        dma(t.c_bqkv[:], hb["b_qkv"])
        dma(t.c_bout[:], hb["b_out"])
        dma(t.c_bq[:], hb["bq"])
        dma(t.c_bkv[:], hb["bkv"])
        dma(t.c_bo[:], hb["bo"])
    p.do("pool", ld_bias, dmas=5)

    def ld_vecs(e, dma, dmat):
        dma(t.v_wsug[:], hb["ws_ug"])
        dma(t.v_wsdp[:], hb["ws_dp"])
        dma(t.v_suug[:], hb["su_ug"])
        dma(t.v_sudp[:], hb["su_dp"])
    tk_v = p.do("sp", ld_vecs, dmas=4)

    def pe_t(e):
        e.transpose(t.ps_v0[:, 0:128], t.v_wsug[:], t.c_identf[:])
        e.transpose(t.ps_v1[:, 0:16], t.v_wsdp[0:16, :], t.c_identf[0:16, 0:16])
        e.transpose(t.ps_v1[:, 128:144], t.v_suug[0:16, :],
                    t.c_identf[0:16, 0:16])
        return e.transpose(t.ps_v1[:, 256:320], t.v_sudp[0:64, :],
                           t.c_identf[0:64, 0:64])
    tk_t = p.do("pe", pe_t, deps=[tk_c, tk_v])

    sc2048 = float(2.0 ** (-5.5))
    sc8192 = float(2.0 ** (-6.5))

    def evac(e):
        e.tensor_scalar(t.c_vup[:], t.ps_v0[:, 0:64], sc2048, None, ALU.mult)
        e.tensor_scalar(t.c_vgate[:], t.ps_v0[:, 64:128], sc2048, None,
                        ALU.mult)
        e.tensor_scalar(t.c_vdp[:], t.ps_v1[:, 0:16], sc8192, None, ALU.mult)
        e.tensor_copy(t.c_suug[:], t.ps_v1[:, 128:144])
        return e.tensor_copy(t.c_sudp[:], t.ps_v1[:, 256:320])
    tk_e = p.do("dve", evac, deps=[tk_t])

    def mk_sh(e):
        last = None
        for a in range(DT):
            last = e.tensor_scalar(t.c_sh128[:, a * 128:(a + 1) * 128],
                                   t.c_h128[:], t.c_suug[:, a:a + 1], None,
                                   ALU.mult)
        return last
    p.do("dve", mk_sh, deps=[tk_e])
    p.barrier()


def gen_cache_fold(p, t, src_ap, n_tiles, bfmat, width, out_toks, store_ap,
                   cstep=1024):
    for n in range(n_tiles):
        for c0 in range(0, width, cstep):
            cw = min(cstep, width - c0)
            wb, war_w, done_w = t.fbrot.next()
            ob, war_o, done_o = t.forot.next()
            tk_l = p.do("pool", lambda e, dma, dmat, n=n, c0=c0, cw=cw, wb=wb:
                        dma(wb[:, 0:cw],
                            src_ap[n * 128:(n + 1) * 128, c0:c0 + cw]),
                        deps=war_w, dmas=1)
            evs = []
            for ci in range((cw + 511) // 512):
                d0 = ci * 512
                dw = min(512, cw - d0)
                ps, war_p, done_p = t.fprot.next()
                tk_m = p.do("pe", lambda e, ps=ps, d0=d0, dw=dw, wb=wb:
                            e.matmul(ps[:, 0:dw], bfmat[:],
                                     wb[:, d0:d0 + dw], start=True, stop=True),
                            deps=[tk_l] + war_p)
                tk_e = p.do("dve", lambda e, ps=ps, d0=d0, dw=dw, ob=ob:
                            e.tensor_copy(ob[:, d0:d0 + dw], ps[:, 0:dw]),
                            deps=[tk_m] + war_o)
                war_o = []
                done_p(tk_e)
                evs.append(tk_e)
            done_w(evs[-1])
            dest, src = store_ap(n, c0, cw, ob)
            tk_s = p.do("pool", lambda e, dma, dmat, dest=dest, src=src:
                        dma(dest, src), deps=evs, dmas=1)
            done_o(tk_s)
            out_toks.append(tk_s)
            yield


def pump(t, k):
    while k > 0 and t.bg:
        try:
            next(t.bg[0])
            k -= 1
        except StopIteration:
            t.bg.pop(0)


def emit_attn_a_kv(p, t, hb, sc):
    def ldw(e, dma, dmat):
        for a in range(DT):
            dma(t.ak_wk[:, a * 2048:(a + 1) * 2048],
                hb["w_qkvT"][a * 128:(a + 1) * 128, 2048:4096])
            dma(t.ak_wv[:, a * 2048:(a + 1) * 2048],
                hb["w_qkvT"][a * 128:(a + 1) * 128, 4096:6144])
    tk_w = p.do("pool", ldw, dmas=2 * DT)

    for g in range(16):
        pump(t, 6)
        lnT, war_ln, done_ln = t.ak_lnrot.next()
        rows = [hb["xb"][(g * 2 + tt) * 128:(g * 2 + tt + 1) * 128, :]
                for tt in range(2)]
        tr_toks = emit_ln_group(p, t, rows, lnT, 2, war_ln,
                                t.ak_xrot, t.ak_lbrot, lnt_w=256)
        last_k = None
        for ot in range(DT):
            ps, war_p, done_p = t.ak_psrot.next()

            def mmk(e, ot=ot, ps=ps, lnT=lnT):
                for a in range(DT):
                    e.matmul(ps[:, 0:256],
                             t.ak_wk[:, a * 2048 + ot * 128:
                                     a * 2048 + (ot + 1) * 128],
                             lnT[:, a * 256:(a + 1) * 256],
                             start=(a == 0), stop=False)
                return e.matmul(ps[:, 0:256],
                                t.c_bqkv[0:1, 2048 + ot * 128:
                                         2048 + (ot + 1) * 128],
                                t.c_ones[0:1, 0:256], start=False, stop=True)
            tk_m = p.do("pe", mmk, deps=tr_toks + [tk_w] + war_p)
            kb, war_k, done_k = t.ak_krot.next()
            tk_e = p.do("act", lambda e, kb=kb, ps=ps: e.copy(
                kb[:, 0:256], ps[:, 0:256]), deps=[tk_m] + war_k)
            done_p(tk_e)
            tk_s = p.do("sp", lambda e, dma, dmat, ot=ot, g=g, kb=kb: dma(
                sc["KT"][ot * 128:(ot + 1) * 128, g * 256:(g + 1) * 256],
                kb[:, 0:256]), deps=[tk_e], dmas=1)
            done_k(tk_s)
            last_k = tk_s
        last_v = None
        for tt in range(2):
            vb, war_v, done_v = t.ak_vrot.next()
            ev_toks = []
            for oc in range(4):
                ps, war_p, done_p = t.ak_psrot.next()

                def mmv(e, tt=tt, oc=oc, ps=ps, lnT=lnT):
                    for a in range(DT):
                        e.matmul(ps[:, 0:512],
                                 lnT[:, a * 256 + tt * 128:
                                     a * 256 + (tt + 1) * 128],
                                 t.ak_wv[:, a * 2048 + oc * 512:
                                         a * 2048 + (oc + 1) * 512],
                                 start=(a == 0), stop=False)
                    return e.matmul(ps[:, 0:512], t.c_ones[0:1, 0:128],
                                    t.c_bqkv[0:1, 4096 + oc * 512:
                                             4096 + (oc + 1) * 512],
                                    start=False, stop=True)
                tk_m = p.do("pe", mmv, deps=tr_toks + [tk_w] + war_p)
                tk_e = p.do("act", lambda e, oc=oc, vb=vb, ps=ps: e.copy(
                    vb[:, oc * 512:(oc + 1) * 512], ps[:, 0:512]),
                    deps=[tk_m] + war_v)
                war_v = []
                done_p(tk_e)
                ev_toks.append(tk_e)
            tile = g * 2 + tt
            tk_s = p.do("sp", lambda e, dma, dmat, tile=tile, vb=vb: dma(
                sc["VV"][tile * 128:(tile + 1) * 128, :], vb[:]),
                deps=ev_toks, dmas=1)
            done_v(tk_s)
            last_v = tk_s
        done_ln(last_k)
        done_ln(last_v)
    p.barrier(exclude_pool=True)


def emit_attn_a_q(p, t, hb, sc):
    def ldwq(e, dma, dmat):
        for a in range(DT):
            dma(t.aq_wq[:, a * 2048:(a + 1) * 2048],
                hb["w_qkvT"][a * 128:(a + 1) * 128, 0:2048])
    tk_w = p.do("pool", ldwq, dmas=DT)

    qgroups = [(0, 4), (4, 4), (8, 4), (12, 4), (16, 4), (20, 1)]
    for (g0, gn) in qgroups:
        pump(t, 16)
        gw = gn * 128
        lnT, war_ln, done_ln = t.aq_lnrot.next()
        rows = [hb["xs"][(g0 + tt) * 128:(g0 + tt + 1) * 128, :]
                for tt in range(gn)]
        tr_toks = emit_ln_group(p, t, rows, lnT, gn, war_ln,
                                t.ak_xrot, t.ak_lbrot)
        last_q = None
        for ot in range(DT):
            ps, war_p, done_p = t.ak_psrot.next()

            def mmq(e, ot=ot, gw=gw, ps=ps, lnT=lnT):
                for a in range(DT):
                    e.matmul(ps[:, 0:gw],
                             t.aq_wq[:, a * 2048 + ot * 128:
                                     a * 2048 + (ot + 1) * 128],
                             lnT[:, a * 512:a * 512 + gw],
                             start=(a == 0), stop=False)
                return e.matmul(ps[:, 0:gw],
                                t.c_bqkv[0:1, ot * 128:(ot + 1) * 128],
                                t.c_ones[0:1, 0:gw], start=False, stop=True)
            tk_m = p.do("pe", mmq, deps=tr_toks + [tk_w] + war_p)
            kb, war_k, done_k = t.ak_krot.next()
            tk_e = p.do("act", lambda e, kb=kb, gw=gw, ps=ps: e.copy(
                kb[:, 0:gw], ps[:, 0:gw]), deps=[tk_m] + war_k)
            done_p(tk_e)
            tk_s = p.do("sp", lambda e, dma, dmat, ot=ot, g0=g0, gw=gw, kb=kb:
                        dma(sc["QT"][ot * 128:(ot + 1) * 128,
                                     g0 * 128:g0 * 128 + gw],
                            kb[:, 0:gw]), deps=[tk_e], dmas=1)
            done_k(tk_s)
            last_q = tk_s
        done_ln(last_q)
    p.barrier(exclude_pool=True)


def emit_attn_a_flash(p, t, sc):
    fgroups = [(0, 5), (5, 4), (9, 4), (13, 4), (17, 4)]
    g_war = []
    qt_r = t.fl_un.rearrange("p (a c) -> p a c", a=DT)
    for (g0, gn) in fgroups:
        gw = gn * 128

        def ldq(e, dma, dmat, g0=g0, gw=gw):
            dma(qt_r[:, :, 0:gw],
                sc["QT"].rearrange("(a p) c -> p a c", p=128)[
                    :, :, g0 * 128:g0 * 128 + gw])
        tk_q = p.do("sp", ldq, deps=t.fl_qt_war, dmas=1)
        t.fl_qt_war = []

        exp_toks = []
        for kc in range(8):
            pump(t, 3)
            ktc, war_kt, done_kt = t.fl_kvrot.next()

            def ldk(e, dma, dmat, kc=kc, ktc=ktc):
                dma(ktc.rearrange("p (a c) -> p a c", a=DT),
                    sc["KT"].rearrange("(a p) c -> p a c", p=128)[
                        :, :, kc * 512:(kc + 1) * 512])
            tk_k = p.do("sp", ldk, deps=war_kt + g_war + [tk_q], dmas=1)
            for qi in range(gn):
                ps, war_p, done_p = t.ak_psrot.next()

                def mms(e, qi=qi, ps=ps, ktc=ktc):
                    last = None
                    for a in range(DT):
                        last = e.matmul(
                            ps[:, 0:512],
                            t.fl_un[:, a * QT_W + qi * 128:
                                    a * QT_W + (qi + 1) * 128],
                            ktc[:, a * 512:(a + 1) * 512],
                            start=(a == 0), stop=(a == DT - 1))
                    return last
                tk_m = p.do("pe", mms, deps=[tk_q, tk_k] + war_p)
                col = qi * 8 + kc
                tk_x = p.do("dve", lambda e, col=col, ps=ps: e.reduce_max(
                    t.fl_mx[:, col:col + 1], ps[:, 0:512], AX.X),
                    deps=[tk_m] + g_war)
                tk_b = p.do("dve", lambda e, col=col: e.tensor_scalar(
                    t.fl_bb[:, col:col + 1], t.fl_mx[:, col:col + 1],
                    -SC_A, None, ALU.mult), deps=[tk_x])
                tk_e = p.do("act", lambda e, qi=qi, kc=kc, col=col, ps=ps:
                            e.activation(
                    t.fl_probs[:, qi * 4096 + kc * 512:
                               qi * 4096 + (kc + 1) * 512],
                    ps[:, 0:512], AF.Exp, bias=t.fl_bb[:, col:col + 1],
                    scale=SC_A, accum_out=t.fl_sm[:, col:col + 1]),
                    deps=[tk_b] + g_war)
                done_p(tk_e)
                exp_toks.append(tk_e)
            done_kt(exp_toks[-1])
        g_war = []

        stat_toks = []
        w_toks = []
        for qi in range(gn):
            tk_g = p.do("dve", lambda e, qi=qi: e.reduce_max(
                t.fl_gmx[:, qi:qi + 1], t.fl_mx[:, qi * 8:(qi + 1) * 8], AX.X),
                deps=exp_toks)
            tk_nb = p.do("dve", lambda e, qi=qi: e.tensor_scalar(
                t.fl_gnb[:, qi:qi + 1], t.fl_gmx[:, qi:qi + 1], -SC_A, None,
                ALU.mult), deps=[tk_g])
            tk_rf = p.do("act", lambda e, qi=qi: e.activation(
                t.fl_rf[:, qi * 8:(qi + 1) * 8],
                t.fl_mx[:, qi * 8:(qi + 1) * 8],
                AF.Exp, bias=t.fl_gnb[:, qi:qi + 1], scale=SC_A),
                deps=[tk_nb])
            tk_lw = p.do("dve", lambda e, qi=qi: e.tensor_mul(
                t.fl_lw[:, qi * 8:(qi + 1) * 8],
                t.fl_sm[:, qi * 8:(qi + 1) * 8],
                t.fl_rf[:, qi * 8:(qi + 1) * 8]), deps=[tk_rf])
            tk_ls = p.do("dve", lambda e, qi=qi: e.reduce_sum(
                t.fl_l[:, qi:qi + 1], t.fl_lw[:, qi * 8:(qi + 1) * 8], AX.X),
                deps=[tk_lw])
            tk_l = p.do("dve", lambda e, qi=qi: e.reciprocal(
                t.fl_rl[:, qi:qi + 1], t.fl_l[:, qi:qi + 1]), deps=[tk_ls])
            stat_toks.append(tk_l)
            tk_w = p.do("dve", lambda e, qi=qi: e.tensor_scalar(
                t.fl_w[:, qi * 8:(qi + 1) * 8],
                t.fl_rf[:, qi * 8:(qi + 1) * 8],
                t.fl_rl[:, qi:qi + 1], None, ALU.mult), deps=[tk_l])
            w_toks.append(tk_w)

        last_cm = None
        for vh in range(2):
            pt_toks = {}
            for qi in range(gn):
                ptq = t.fl_un[:, qi * 2048:(qi + 1) * 2048]
                war0 = t.fl_ptq_war.pop(qi, [])

                def presc(e, qi=qi, vh=vh):
                    last = None
                    for c in range(4):
                        kc = vh * 4 + c
                        last = e.tensor_scalar(
                            t.fl_sc[:, c * 512:(c + 1) * 512],
                            t.fl_probs[:, qi * 4096 + kc * 512:
                                       qi * 4096 + (kc + 1) * 512],
                            t.fl_w[:, qi * 8 + kc:qi * 8 + kc + 1], None,
                            ALU.mult)
                    return last
                tk_s2 = p.do("dve", presc,
                             deps=[w_toks[qi]] + t.fl_sc_war)
                t.fl_sc_war = []
                t.fl_probs_war.append(tk_s2)
                toks = []
                for half in range(2):
                    ps_tr, war_tr, done_tr = t.trrot.next()

                    def tr8(e, half=half, ps_tr=ps_tr):
                        last = None
                        for i in range(8):
                            last = e.transpose(
                                ps_tr[:, i * 128:(i + 1) * 128],
                                t.fl_sc[:, half * 1024 + i * 128:
                                        half * 1024 + (i + 1) * 128],
                                t.c_ident[:])
                        return last
                    tk_tr = p.do("pe", tr8, deps=[tk_s2] + war_tr)
                    t.fl_sc_war.append(tk_tr)
                    tk_pte = p.do("dve", lambda e, half=half, ps_tr=ps_tr,
                                  ptq=ptq: e.tensor_copy(
                        ptq[:, half * 1024:(half + 1) * 1024],
                        ps_tr[:, 0:1024]), deps=[tk_tr] + war0)
                    war0 = []
                    done_tr(tk_pte)
                    toks.append(tk_pte)
                pt_toks[qi] = toks
            for dh in range(2):
                def ldv(e, dma, dmat, vh=vh, dh=dh):
                    dma(t.fl_kv.rearrange("p (a c) -> p a c", a=DT),
                        sc["VV"][vh * 2048:(vh + 1) * 2048,
                                 dh * 1024:(dh + 1) * 1024].rearrange(
                            "(a p) c -> p a c", p=128))
                tk_v = p.do("sp", ldv,
                            deps=t.fl_kv_war + [exp_toks[-1]], dmas=1)
                t.fl_kv_war = []
                for qi in range(gn):
                    ptq = t.fl_un[:, qi * 2048:(qi + 1) * 2048]

                    def mmc(e, qi=qi, ptq=ptq):
                        last = None
                        for kt in range(16):
                            for dc in range(2):
                                last = e.matmul(
                                    t.fl_psctx[:, dc * 512:(dc + 1) * 512],
                                    ptq[:, kt * 128:(kt + 1) * 128],
                                    t.fl_kv[:, kt * 1024 + dc * 512:
                                            kt * 1024 + (dc + 1) * 512],
                                    start=(kt == 0), stop=(kt == 15))
                        return last
                    tk_cm = p.do("pe", mmc, deps=pt_toks[qi] + [tk_v] +
                                 t.fl_ctx_war)
                    t.fl_ctx_war = []
                    t.fl_kv_war.append(tk_cm)
                    t.fl_ptq_war.setdefault(qi, []).append(tk_cm)
                    last_cm = tk_cm
                    par = t.fl_par[qi]
                    if vh == 0:
                        tk_ev = p.do("act", lambda e, dh=dh, par=par: e.copy(
                            par[:, dh * 1024:(dh + 1) * 1024], t.fl_psctx[:]),
                            deps=[tk_cm] + t.fl_par_war.pop(qi, []))
                        t.fl_ctx_war.append(tk_ev)
                    else:

                        ctxh, war_ch, done_ch = t.fl_chrot.next()
                        tk_ev = p.do("dve", lambda e, dh=dh, par=par,
                                     ctxh=ctxh: e.tensor_tensor(
                            ctxh[:], par[:, dh * 1024:(dh + 1) * 1024],
                            t.fl_psctx[:], ALU.add),
                            deps=[tk_cm] + war_ch)
                        t.fl_par_war.setdefault(qi, []).append(tk_ev)
                        t.fl_ctx_war.append(tk_ev)
                        ps_tr, war_tr, done_tr = t.trrot.next()

                        def tr8(e, ctxh=ctxh, ps_tr=ps_tr):
                            last = None
                            for i in range(8):
                                last = e.transpose(
                                    ps_tr[:, i * 128:(i + 1) * 128],
                                    ctxh[:, i * 128:(i + 1) * 128],
                                    t.c_ident[:])
                            return last
                        tk_tr = p.do("pe", tr8, deps=[tk_ev] + war_tr)
                        done_ch(tk_tr)
                        cth, war_ct, done_ct = t.fl_ctrot.next()
                        tk_te = p.do("dve", lambda e, cth=cth, ps_tr=ps_tr:
                                     e.tensor_copy(cth[:], ps_tr[:, 0:1024]),
                                     deps=[tk_tr] + war_ct)
                        done_tr(tk_te)
                        slot = g0 + qi
                        tk_ct = p.do("sp", lambda e, dma, dmat, slot=slot,
                                     dh=dh, cth=cth: dma(
                            sc["CT"][slot * 128:(slot + 1) * 128,
                                     dh * 1024:(dh + 1) * 1024].rearrange(
                                "p (a c) -> p a c", a=8),
                            cth.rearrange("p (a c) -> p a c", a=8)),
                            deps=[tk_te], dmas=1)
                        done_ct(tk_ct)
        g_war = list(t.fl_probs_war) + [last_cm]
        t.fl_probs_war = []
        t.fl_qt_war = [last_cm]
    p.barrier(exclude_pool=True)


def emit_attn_a_out(p, t, hb, sc):
    def ldwo(e, dma, dmat):
        for a in range(DT):
            dma(t.ao_wo[:, a * 2048:(a + 1) * 2048],
                hb["w_outT"][a * 128:(a + 1) * 128, :])
    tk_w = p.do("pool", ldwo, dmas=DT)

    for slot in range(NT_S12):
        pump(t, 8)
        ctb, war_ct, done_ct = t.ao_ctrot.next()
        tk_c = p.do("sp", lambda e, dma, dmat, slot=slot, ctb=ctb: dma(
            ctb[:], sc["CT"][slot * 128:(slot + 1) * 128, :]),
            deps=war_ct, dmas=1)
        xrb, war_x, done_x = t.ak_xrot.next()
        tk_x = p.do("sp", lambda e, dma, dmat, slot=slot, xrb=xrb: dma(
            xrb[:], hb["xs"][slot * 128:(slot + 1) * 128, :]),
            deps=war_x, dmas=1)
        add_toks = []
        for oc in range(4):
            ps, war_p, done_p = t.ak_psrot.next()

            def mmo(e, oc=oc, ps=ps, ctb=ctb):
                for a in range(DT):
                    e.matmul(ps[:, 0:512], ctb[:, a * 128:(a + 1) * 128],
                             t.ao_wo[:, a * 2048 + oc * 512:
                                     a * 2048 + (oc + 1) * 512],
                             start=(a == 0), stop=False)
                return e.matmul(ps[:, 0:512], t.c_ones[0:1, 0:128],
                                t.c_bout[0:1, oc * 512:(oc + 1) * 512],
                                start=False, stop=True)
            tk_m = p.do("pe", mmo, deps=[tk_c, tk_w] + war_p)
            tk_a = p.do("dve", lambda e, oc=oc, ps=ps, xrb=xrb:
                        e.tensor_tensor(
                t.ao_xnew[:, oc * 512:(oc + 1) * 512],
                xrb[:, oc * 512:(oc + 1) * 512], ps[:, 0:512], ALU.add),
                deps=[tk_m, tk_x] + t.ao_xnew_war)
            t.ao_xnew_war = []
            done_p(tk_a)
            add_toks.append(tk_a)
        done_x(add_toks[-1])
        done_ct(add_toks[-1])
        tk_o = p.do("sp", lambda e, dma, dmat, slot=slot: dma(
            sc["XR1"][slot * 128:(slot + 1) * 128, :], t.ao_xnew[:]),
            deps=add_toks, dmas=1)
        t.ao_xnew_war.append(tk_o)
    p.barrier(exclude_pool=True)


def emit_ffn(p, nc, ctx_outer, t, sc, xin, xout, n_tiles, tag):
    N = n_tiles * 128
    with ExitStack() as c2:
        sb2 = lambda n, s, d: c2.enter_context(
            nc.sbuf_tensor("sb_" + tag + n, s, d))
        pst2 = lambda n, s, d: c2.enter_context(nc.psum_tensor(tag + n, s, d))
        lnrot = Rot([sb2("fu_ln0", [128, DT * 512], BF16)])
        xfT = sb2("fu_xfT", [128, DT * N], BF16)
        wrot_u = Rot([sb2("fu_wu0", [128, DT * 256], BF16),
                      sb2("fu_wu1", [128, DT * 256], BF16)])
        wrot_g = Rot([sb2("fu_wg0", [128, DT * 256], BF16),
                      sb2("fu_wg1", [128, DT * 256], BF16)])
        psu = Rot([pst2("fu_psu0", [128, 512], F32),
                   pst2("fu_psu1", [128, 512], F32)])
        psg = Rot([pst2("fu_psg0", [128, 512], F32),
                   pst2("fu_psg1", [128, 512], F32)])
        uprot = Rot([sb2("fu_up0", [128, 512], BF16),
                     sb2("fu_up1", [128, 512], BF16)])
        gtrot = Rot([sb2("fu_gt0", [128, 512], BF16),
                     sb2("fu_gt1", [128, 512], BF16)])
        hrot = Rot([sb2("fu_h0", [128, 512], BF16),
                    sb2("fu_h1", [128, 512], BF16)])
        urot = Rot([sb2("fu_u0", [128, 512], BF16),
                    sb2("fu_u1", [128, 512], BF16)])

        xft_toks = []
        t.ffn_pend = None
        pgroups = [(i * 4, min(4, n_tiles - i * 4))
                   for i in range((n_tiles + 3) // 4)]
        for (g0, gn) in pgroups:
            gw = gn * 128
            lnT, war_ln, done_ln = lnrot.next()
            rows = [xin[(g0 + tt) * 128:(g0 + tt + 1) * 128, :]
                    for tt in range(gn)]
            tr_toks = emit_ln_group(p, t, rows, lnT, gn, war_ln,
                                    t.ak_xrot, t.ak_lbrot)
            last = None
            for a in range(DT):
                ps, war_p, done_p = t.ak_psrot.next()
                tk_m = p.do("pe", lambda e, a=a, gw=gw, ps=ps, lnT=lnT:
                            e.matmul(
                    ps[:, 0:gw], t.c_sh128[:, a * 128:(a + 1) * 128],
                    lnT[:, a * 512:a * 512 + gw], start=True, stop=True),
                    deps=tr_toks + war_p)
                tk_e = p.do("dve", lambda e, a=a, g0=g0, gw=gw, ps=ps:
                            e.tensor_copy(
                    xfT[:, a * N + g0 * 128:a * N + g0 * 128 + gw],
                    ps[:, 0:gw]), deps=[tk_m])
                done_p(tk_e)
                last = tk_e
            done_ln(last)
            xft_toks.append(last)

        tchunks = []
        c0 = 0
        while c0 < N:
            tchunks.append((c0, min(512, N - c0)))
            c0 += 512
        for c in range(32):
            wub, war_wu, done_wu = wrot_u.next()
            wgb, war_wg, done_wg = wrot_g.next()

            def ldw(e, dma, dmat, c=c, wub=wub, wgb=wgb):
                dma(wub[:], sc["C_up"][c * 128:(c + 1) * 128, :])
                dma(wgb[:], sc["C_gate"][c * 128:(c + 1) * 128, :])
            tk_w = p.do("sp", ldw, deps=war_wu + war_wg + t.fold_up +
                        t.fold_gate, dmas=2)

            last_store = None

            def flush_mid(pend):
                og, t0, tw, hb_, done_h, tk_h = pend
                pm, war_pm, done_pm = t.ak_psrot.next()
                tk_mm = p.do("pe", lambda e, tw=tw, hb_=hb_, pm=pm:
                             e.matmul(pm[:, 0:tw], t.c_h128[:],
                                      hb_[:, 0:tw], start=True, stop=True),
                             deps=[tk_h] + war_pm)
                done_h(tk_mm)
                ub, war_u, done_u = urot.next()
                tk_ue = p.do("act", lambda e, tw=tw, pm=pm, ub=ub: e.copy(
                    ub[:, 0:tw], pm[:, 0:tw]), deps=[tk_mm] + war_u)
                done_pm(tk_ue)
                tk_us = p.do("sp", lambda e, dma, dmat, og=og, t0=t0,
                             tw=tw, ub=ub: dma(
                    sc["US"][og * 128:(og + 1) * 128, t0:t0 + tw],
                    ub[:, 0:tw]), deps=[tk_ue], dmas=1)
                done_u(tk_us)
                return tk_us

            for ot in range(2):
                og = c * 2 + ot
                for (t0, tw) in tchunks:
                    pu, war_pu, done_pu = psu.next()
                    pg, war_pg, done_pg = psg.next()

                    def mmu(e, ot=ot, t0=t0, tw=tw, wub=wub, pu=pu):
                        last = None
                        for a in range(DT):
                            last = e.matmul(
                                pu[:, 0:tw],
                                wub[:, a * 256 + ot * 128:
                                    a * 256 + (ot + 1) * 128],
                                xfT[:, a * N + t0:a * N + t0 + tw],
                                start=(a == 0), stop=(a == DT - 1))
                        return last
                    tk_mu = p.do("pe", mmu, deps=[tk_w] + xft_toks + war_pu)

                    def mmg(e, ot=ot, t0=t0, tw=tw, wgb=wgb, pg=pg):
                        last = None
                        for a in range(DT):
                            last = e.matmul(
                                pg[:, 0:tw],
                                wgb[:, a * 256 + ot * 128:
                                    a * 256 + (ot + 1) * 128],
                                xfT[:, a * N + t0:a * N + t0 + tw],
                                start=(a == 0), stop=(a == DT - 1))
                        return last
                    tk_mg = p.do("pe", mmg, deps=[tk_w] + xft_toks + war_pg)

                    if t.ffn_pend is not None:
                        last_store = flush_mid(t.ffn_pend)
                        t.ffn_pend = None

                    upb, war_ub, done_ub = uprot.next()
                    gtb, war_gb, done_gb = gtrot.next()
                    tk_eu = p.do("dve", lambda e, og=og, tw=tw, pu=pu,
                                 upb=upb: e.tensor_scalar(
                        upb[:, 0:tw], pu[:, 0:tw], t.c_vup[:, og:og + 1],
                        None, ALU.mult), deps=[tk_mu] + war_ub)
                    done_pu(tk_eu)
                    tk_eg = p.do("act", lambda e, og=og, tw=tw, pg=pg,
                                 gtb=gtb: e.activation(
                        gtb[:, 0:tw], pg[:, 0:tw], AF.Silu, bias=0.0,
                        scale=t.c_vgate[:, og:og + 1]),
                        deps=[tk_mg] + war_gb)
                    done_pg(tk_eg)
                    hb_, war_h, done_h = hrot.next()
                    tk_h = p.do("dve", lambda e, og=og, tw=tw, hb_=hb_,
                                gtb=gtb, upb=upb: e.scalar_tensor_tensor(
                        hb_[:, 0:tw], gtb[:, 0:tw], t.c_sudp[:, og:og + 1],
                        upb[:, 0:tw], ALU.mult, ALU.mult),
                        deps=[tk_eu, tk_eg] + war_h)
                    done_ub(tk_h)
                    done_gb(tk_h)
                    t.ffn_pend = (og, t0, tw, hb_, done_h, tk_h)
            if t.ffn_pend is not None:
                last_store = flush_mid(t.ffn_pend)
                t.ffn_pend = None
            done_wu(last_store)
            done_wg(last_store)
        p.barrier()

    with ExitStack() as c2:
        sb2 = lambda n, s, d: c2.enter_context(
            nc.sbuf_tensor("sb_" + tag + n, s, d))
        pst2 = lambda n, s, d: c2.enter_context(nc.psum_tensor(tag + n, s, d))
        usb = sb2("fd_usb", [128, IT * USB_W], BF16)
        wdrot = Rot([sb2("fd_wd0", [128, IT * 128], BF16),
                     sb2("fd_wd1", [128, IT * 128], BF16)])
        psd = Rot([pst2("fd_psd0", [128, 512], F32),
                   pst2("fd_psd1", [128, 512], F32)])
        yT = sb2("fd_yT", [128, DT * USB_W], BF16)
        xnew = sb2("fd_xnew", [128, D], F32)
        usb_war = []
        xnew_war = []

        if n_tiles == 21:
            dgroups = [(0, 5), (5, 4), (9, 4), (13, 4), (17, 4)]
        else:
            dgroups = [(0, 4), (4, 4), (8, 4), (12, 4)]
        for (g0, gn) in dgroups:
            gN = gn * 128

            def ldu(e, dma, dmat, g0=g0, gN=gN):
                dma(usb.rearrange("p (a c) -> p a c", a=IT)[:, :, 0:gN],
                    sc["US"].rearrange("(a p) c -> p a c", p=128)[
                        :, :, g0 * 128:g0 * 128 + gN])
            tk_u = p.do("sp", ldu, deps=usb_war, dmas=1)
            usb_war = []

            y_toks = []
            for ot in range(DT):
                wdb, war_wd, done_wd = wdrot.next()

                def ldwd(e, dma, dmat, ot=ot, wdb=wdb):
                    dma(wdb[:], sc["C_down"][ot * 128:(ot + 1) * 128, :])
                tk_wd = p.do("sp", ldwd, deps=war_wd + t.fold_down, dmas=1)
                for ci in range((gN + 511) // 512):
                    c0 = ci * 512
                    cw = min(512, gN - c0)
                    ps, war_p, done_p = psd.next()

                    def mmd(e, c0=c0, cw=cw, wdb=wdb, ps=ps):
                        last = None
                        for a in range(IT):
                            last = e.matmul(
                                ps[:, 0:cw], wdb[:, a * 128:(a + 1) * 128],
                                usb[:, a * USB_W + c0:a * USB_W + c0 + cw],
                                start=(a == 0), stop=(a == IT - 1))
                        return last
                    tk_md = p.do("pe", mmd, deps=[tk_wd, tk_u] + war_p)
                    tk_ed = p.do("act", lambda e, ot=ot, c0=c0, cw=cw, ps=ps:
                                 e.activation(
                        yT[:, ot * USB_W + c0:ot * USB_W + c0 + cw],
                        ps[:, 0:cw], AF.Identity, bias=0.0,
                        scale=t.c_vdp[:, ot:ot + 1]), deps=[tk_md])
                    done_p(tk_ed)
                    done_wd(tk_ed)
                    y_toks.append(tk_ed)
            last_o = None
            for i in range(gn):
                tile = g0 + i
                xrb, war_x, done_x = t.ak_xrot.next()
                tk_x = p.do("sp", lambda e, dma, dmat, tile=tile, xrb=xrb: dma(
                    xrb[:], xin[tile * 128:(tile + 1) * 128, :]),
                    deps=war_x, dmas=1)
                add_toks = []
                for half in range(2):
                    ps, war_p, done_p = t.trrot.next()

                    def tr(e, i=i, half=half, ps=ps):
                        last = None
                        for k in range(8):
                            a = half * 8 + k
                            last = e.transpose(
                                ps[:, k * 128:(k + 1) * 128],
                                yT[:, a * USB_W + i * 128:
                                   a * USB_W + (i + 1) * 128],
                                t.c_ident[:])
                        return last
                    tk_tr = p.do("pe", tr, deps=y_toks + war_p)

                    def addres(e, half=half, ps=ps, xrb=xrb):
                        last = None
                        for k in range(2):
                            cc = half * 1024 + k * 512
                            last = e.tensor_tensor(
                                xnew[:, cc:cc + 512], xrb[:, cc:cc + 512],
                                ps[:, k * 512:(k + 1) * 512], ALU.add)
                        return last
                    tk_a = p.do("dve", addres,
                                deps=[tk_tr, tk_x] + xnew_war)
                    xnew_war = []
                    done_p(tk_a)
                    add_toks.append(tk_a)
                done_x(add_toks[-1])
                tk_o = p.do("sp", lambda e, dma, dmat, tile=tile: dma(
                    xout[tile * 128:(tile + 1) * 128, :], xnew[:]),
                    deps=add_toks, dmas=1)
                xnew_war.append(tk_o)
                last_o = tk_o
            usb_war.append(last_o)
        p.barrier()


def emit_kvrope(p, t, hb, sc):
    def ldcs(e, dma, dmat):
        dma(t.kr_cos[:], hb["cosq"])
        dma(t.kr_sin[:], hb["sinq"])
    tk_cs = p.do("sp", ldcs, dmas=2)

    def ldw(e, dma, dmat):
        for a in range(DT):
            dma(t.kr_wq[:, a * 2048:(a + 1) * 2048],
                hb["wq"][a * 128:(a + 1) * 128, :])
            dma(t.kr_wkv[:, a * 1024:(a + 1) * 1024],
                hb["wkv"][a * 128:(a + 1) * 128, :])
    tk_w = p.do("pool", ldw, dmas=2 * DT)

    def rope_block(src_bf, gw, g0, outbf, deps):
        cos_sl = t.kr_cos[:, g0 * 128:g0 * 128 + gw]
        sin_sl = t.kr_sin[:, g0 * 128:g0 * 128 + gw]
        psr, war_r, done_r = t.kr_psr.next()
        tk_r = p.do("pe", lambda e: e.matmul(
            psr[:, 0:gw], t.c_rope[:], src_bf, start=True, stop=True),
            deps=deps + war_r)
        tb, war_t, done_t = t.kr_t1rot.next()
        tk_1 = p.do("dve", lambda e: e.tensor_tensor(
            tb[:, 0:gw], psr[:, 0:gw], sin_sl, ALU.mult),
            deps=[tk_r] + war_t)
        done_r(tk_1)
        tk_2 = p.do("dve", lambda e: e.tensor_tensor(
            t.kr_t2[:, 0:gw], src_bf, cos_sl, ALU.mult),
            deps=list(deps) + t.kr_t2_war)
        tk_3 = p.do("dve", lambda e: e.tensor_tensor(
            outbf, tb[:, 0:gw], t.kr_t2[:, 0:gw], ALU.add),
            deps=[tk_1, tk_2])
        t.kr_t2_war = [tk_3]
        done_t(tk_3)
        return tk_3

    kgroups = [(0, 4), (4, 4), (8, 4), (12, 4), (16, 4), (20, 1)]
    for (g0, gn) in kgroups:
        gw = gn * 128
        own = (0 < g0 < 20)
        lnT, war_ln, done_ln = t.kr_lnrot.next()
        rows = [sc["XR2"][(g0 + tt) * 128:(g0 + tt + 1) * 128, :]
                for tt in range(gn)]
        tr_toks = emit_ln_group(p, t, rows, lnT, gn, war_ln,
                                t.ak_xrot, t.ak_lbrot)
        last_tok = None
        for ot in range(4):
            ps, war_p, done_p = t.ak_psrot.next()

            def mmk(e, ot=ot, gw=gw, ps=ps, lnT=lnT):
                for a in range(DT):
                    e.matmul(ps[:, 0:gw],
                             t.kr_wkv[:, a * 1024 + ot * 128:
                                      a * 1024 + (ot + 1) * 128],
                             lnT[:, a * 512:a * 512 + gw],
                             start=(a == 0), stop=False)
                return e.matmul(ps[:, 0:gw],
                                t.c_bkv[0:1, ot * 128:(ot + 1) * 128],
                                t.c_ones[0:1, 0:gw], start=False, stop=True)
            tk_m = p.do("pe", mmk, deps=tr_toks + [tk_w] + war_p)
            kb, war_k, done_k = t.ak_krot.next()
            tk_e = p.do("act", lambda e, gw=gw, kb=kb, ps=ps: e.copy(
                kb[:, 0:gw], ps[:, 0:gw]), deps=[tk_m] + war_k)
            done_p(tk_e)
            rob, war_ro, done_ro = t.kr_rorot.next()
            tk_ro = rope_block(kb[:, 0:gw], gw, g0, rob[:, 0:gw],
                               [tk_e, tk_cs] + war_ro)
            done_k(tk_ro)
            tk_s = p.do("sp", lambda e, dma, dmat, ot=ot, g0=g0, gw=gw,
                        rob=rob: dma(
                sc["K3T"][ot * 128:(ot + 1) * 128, g0 * 128:g0 * 128 + gw],
                rob[:, 0:gw]), deps=[tk_ro], dmas=1)
            done_ro(tk_s)
            last_tok = tk_s
        for tt in range(gn):
            slot = g0 + tt
            ps, war_p, done_p = t.ak_psrot.next()

            def mmv(e, tt=tt, ps=ps, lnT=lnT):
                for a in range(DT):
                    e.matmul(ps[:, 0:512],
                             lnT[:, a * 512 + tt * 128:
                                 a * 512 + (tt + 1) * 128],
                             t.kr_wkv[:, a * 1024 + 512:(a + 1) * 1024],
                             start=(a == 0), stop=False)
                return e.matmul(ps[:, 0:512], t.c_ones[0:1, 0:128],
                                t.c_bkv[0:1, 512:1024], start=False, stop=True)
            tk_m = p.do("pe", mmv, deps=tr_toks + [tk_w] + war_p)
            kb, war_k, done_k = t.ak_krot.next()
            tk_e = p.do("act", lambda e, kb=kb, ps=ps: e.copy(
                kb[:, 0:512], ps[:, 0:512]), deps=[tk_m] + war_k)
            done_p(tk_e)
            tk_s = p.do("sp", lambda e, dma, dmat, slot=slot, kb=kb: dma(
                sc["V3"][slot * 128:(slot + 1) * 128, :], kb[:, 0:512]),
                deps=[tk_e], dmas=1)
            done_k(tk_s)
            last_tok = tk_s
        if own:
            for ot in range(DT):
                ps, war_p, done_p = t.ak_psrot.next()

                def mmq(e, ot=ot, ps=ps, lnT=lnT):
                    for a in range(DT):
                        e.matmul(ps[:, 0:512],
                                 t.kr_wq[:, a * 2048 + ot * 128:
                                         a * 2048 + (ot + 1) * 128],
                                 lnT[:, a * 512:(a + 1) * 512],
                                 start=(a == 0), stop=False)
                    return e.matmul(ps[:, 0:512],
                                    t.c_bq[0:1, ot * 128:(ot + 1) * 128],
                                    t.c_ones[0:1, 0:512],
                                    start=False, stop=True)
                tk_m = p.do("pe", mmq, deps=tr_toks + [tk_w] + war_p)
                kb, war_k, done_k = t.ak_krot.next()
                tk_e = p.do("act", lambda e, kb=kb, ps=ps: e.copy(
                    kb[:, 0:512], ps[:, 0:512]), deps=[tk_m] + war_k)
                done_p(tk_e)
                rob, war_ro, done_ro = t.kr_rorot.next()
                tk_ro = rope_block(kb[:, 0:512], 512, g0, rob[:, 0:512],
                                   [tk_e, tk_cs] + war_ro)
                done_k(tk_ro)
                tk_s = p.do("sp", lambda e, dma, dmat, ot=ot, g0=g0, rob=rob:
                            dma(sc["Q3T"][ot * 128:(ot + 1) * 128,
                                          (g0 - OWN0) * 128:
                                          (g0 - OWN0) * 128 + 512],
                                rob[:, 0:512]), deps=[tk_ro], dmas=1)
                done_ro(tk_s)
                last_tok = tk_s
        done_ln(last_tok)
    p.barrier()


def emit_attn_b(p, t, hb, sc):
    def ldk(e, dma, dmat):
        dma(t.ab_k3.rearrange("p (a c) -> p a c", a=4),
            sc["K3T"].rearrange("(a p) c -> p a c", p=128))
        dma(t.ab_v3.rearrange("p (a c) -> p a c", a=NT_S12),
            sc["V3"].rearrange("(a p) c -> p a c", p=128))
    tk_k = p.do("sp", ldk, dmas=2)

    def ldwo(e, dma, dmat):
        for a in range(DT):
            dma(t.ab_wo[:, a * 2048:(a + 1) * 2048],
                hb["wo"][a * 128:(a + 1) * 128, :])
    tk_w = p.do("pool", ldwo, dmas=DT)

    for j in range(NT_OWN):
        qb, war_q, done_q = t.ab_qrot.next()
        tk_q = p.do("sp", lambda e, dma, dmat, j=j, qb=qb: dma(
            qb.rearrange("p (a c) -> p a c", a=DT),
            sc["Q3T"].rearrange("(a p) c -> p a c", p=128)[
                :, :, j * 128:(j + 1) * 128]), deps=war_q, dmas=1)
        mb, war_m, done_m = t.ab_mrot.next()
        tk_m = p.do("sp", lambda e, dma, dmat, j=j, mb=mb: dma(
            mb[:], hb["masks"][j * 128:(j + 1) * 128, :]), deps=war_m, dmas=1)
        ctx_ev = []
        for g in range(4):
            pt_done = []
            pt_toks = []
            for hh in range(4):
                qh = g * 4 + hh
                st, war_st, done_st = t.ab_stat.next()
                ps, war_p, done_p = t.ab_scrot.next()

                def mms(e, j=j, g=g, qh=qh, ps=ps, qb=qb):
                    e.matmul(ps[:, 0:512],
                             qb[:, qh * 128:(qh + 1) * 128],
                             t.ab_k3[:, g * N_S12 + j * 128:
                                     g * N_S12 + j * 128 + 512],
                             start=True, stop=True)
                    e.matmul(ps[:, 512:640],
                             qb[:, qh * 128:(qh + 1) * 128],
                             t.ab_k3[:, g * N_S12 + j * 128 + 512:
                                     g * N_S12 + j * 128 + 640],
                             start=True, stop=False)
                    return e.matmul(ps[:, 640:768],
                                    qb[:, qh * 128:(qh + 1) * 128],
                                    t.ab_k3[:, g * N_S12 + 2560:
                                            g * N_S12 + 2688],
                                    start=False, stop=True)
                tk_sc = p.do("pe", mms, deps=[tk_k, tk_q] + war_p)
                tk_msk = p.do("dve", lambda e, st=st, ps=ps, mb=mb:
                              e.scalar_tensor_tensor(
                    st.s[:], ps[:, 0:768], SC_B, mb[:], ALU.mult, ALU.add),
                    deps=[tk_sc, tk_m] + war_st)
                done_p(tk_msk)
                tk_mx = p.do("dve", lambda e, st=st: e.reduce_max(
                    st.nmx[:], st.s[:], AX.X, negate=True),
                    deps=[tk_msk])
                prb, war_pr, done_pr = t.ab_prrot.next()
                tk_ex = p.do("act", lambda e, st=st, prb=prb: e.activation(
                    prb[:], st.s[:], AF.Exp, bias=st.nmx[:], scale=1.0,
                    accum_out=st.l[:]), deps=[tk_mx] + war_pr)

                tk_rc = p.do("dve", lambda e, st=st: e.reciprocal(
                    st.rl[:], st.l[:]), deps=[tk_ex])
                tk_d = p.do("dve", lambda e, st=st: e.tensor_scalar(
                    st.dmat[:], t.c_ident[:], st.rl[:], None, ALU.mult),
                    deps=[tk_rc])
                ptb, war_pt, done_pt = t.ab_ptrot.next()
                ev_toks = []
                for piece, (k0, kn) in enumerate([(0, 4), (4, 2)]):
                    psx, war_px, done_px = t.ak_psrot.next()

                    def trp(e, st=st, piece=piece, k0=k0, kn=kn, prb=prb,
                            psx=psx):
                        last = None
                        for ks in range(kn):
                            last = e.matmul(
                                psx[:, ks * 128:(ks + 1) * 128],
                                prb[:, (k0 + ks) * 128:(k0 + ks + 1) * 128],
                                st.dmat[:], start=(ks == 0),
                                stop=(ks == kn - 1))
                        return last
                    tk_tp = p.do("pe", trp, deps=[tk_d] + war_px)
                    done_st(tk_tp)
                    tk_pte = p.do("dve", lambda e, k0=k0, kn=kn, psx=psx,
                                  ptb=ptb: e.tensor_copy(
                        ptb[:, k0 * 128:(k0 + kn) * 128],
                        psx[:, 0:kn * 128]), deps=[tk_tp] + war_pt)
                    war_pt = []
                    done_px(tk_pte)
                    ev_toks.append(tk_pte)
                done_pr(ev_toks[-1])
                pt_toks.append((hh, ptb, done_pt, ev_toks))
            psc, war_c, done_c = t.ab_ctxps.next()

            def mmc(e, j=j, g=g, pt_toks=pt_toks, psc=psc):
                last = None
                for ks in range(6):
                    vslot = j + ks if ks < 5 else 20
                    for (hh, ptb, _, _) in pt_toks:
                        last = e.matmul(
                            psc[:, hh * 128:(hh + 1) * 128],
                            t.ab_v3[:, vslot * 512 + g * 128:
                                    vslot * 512 + (g + 1) * 128],
                            ptb[:, ks * 128:(ks + 1) * 128],
                            start=(ks == 0 and hh == 0),
                            stop=(ks == 5 and hh == 3))
                return last
            all_pt = [tk for (_, _, _, evs) in pt_toks for tk in evs]
            tk_cm = p.do("pe", mmc, deps=all_pt + war_c + [tk_k])
            for (_, _, done_pt, _) in pt_toks:
                done_pt(tk_cm)
            tk_ce = p.do("act", lambda e, g=g, psc=psc: e.copy(
                t.ab_ctxT[:, g * 512:(g + 1) * 512], psc[:, 0:512]),
                deps=[tk_cm] + t.ab_ctxT_war)
            t.ab_ctxT_war = []
            done_c(tk_ce)
            ctx_ev.append(tk_ce)
        done_m(ctx_ev[-1])
        done_q(ctx_ev[-1])
        xrb, war_x, done_x = t.ak_xrot.next()
        slot = OWN0 + j
        tk_x = p.do("sp", lambda e, dma, dmat, slot=slot, xrb=xrb: dma(
            xrb[:], sc["XR2"][slot * 128:(slot + 1) * 128, :]),
            deps=war_x, dmas=1)
        add_toks = []
        for oc in range(4):
            ps, war_p, done_p = t.ak_psrot.next()

            def mmo(e, oc=oc, ps=ps):
                for a in range(DT):
                    e.matmul(ps[:, 0:512],
                             t.ab_ctxT[:, a * 128:(a + 1) * 128],
                             t.ab_wo[:, a * 2048 + oc * 512:
                                     a * 2048 + (oc + 1) * 512],
                             start=(a == 0), stop=False)
                return e.matmul(ps[:, 0:512], t.c_ones[0:1, 0:128],
                                t.c_bo[0:1, oc * 512:(oc + 1) * 512],
                                start=False, stop=True)
            tk_mo = p.do("pe", mmo, deps=ctx_ev + [tk_w] + war_p)
            t.ab_ctxT_war.append(tk_mo)
            tk_a = p.do("dve", lambda e, oc=oc, ps=ps, xrb=xrb:
                        e.tensor_tensor(
                t.ab_xnew[:, oc * 512:(oc + 1) * 512],
                xrb[:, oc * 512:(oc + 1) * 512], ps[:, 0:512], ALU.add),
                deps=[tk_mo, tk_x] + t.ab_xnew_war)
            t.ab_xnew_war = []
            done_p(tk_a)
            add_toks.append(tk_a)
        done_x(add_toks[-1])
        tk_o = p.do("sp", lambda e, dma, dmat, slot=slot: dma(
            sc["XR3"][slot * 128:(slot + 1) * 128, :], t.ab_xnew[:]),
            deps=add_toks, dmas=1)
        t.ab_xnew_war.append(tk_o)
    p.barrier()


def build_nc():
    nc = bass.Bass("TRN2", target_bir_lowering=False, debug=False)
    hb = {}

    def din(name, shape, dt=F32):
        hb[name] = nc.dram_tensor(name, list(shape), dt,
                                  kind="ExternalInput").ap()

    din("xb", (S, D))
    din("xs", (N_S12, D))
    din("w_qkvT", (D, 6144))
    din("b_qkv", (1, 6144))
    din("w_outT", (D, D))
    din("b_out", (1, D))
    din("wq", (D, D))
    din("bq", (1, D))
    din("wkv", (D, 1024))
    din("bkv", (1, 1024))
    din("wo", (D, D))
    din("bo", (1, D))
    din("su_ug", (16, 128))
    din("su_dp", (64, 128))
    din("ws_ug", (128, 128))
    din("ws_dp", (16, 128))
    din("wupTf", (D, INTER))
    din("wgateTf", (D, INTER))
    din("wdownTf", (INTER, D))
    din("cosq", (128, N_S12))
    din("sinq", (128, N_S12))
    din("masks", (16 * 128, 768))
    din("c_ident", (128, 128), BF16)
    din("c_identf", (128, 128), F32)
    din("c_h128", (128, 128), BF16)
    din("c_bf16", (128, 128), BF16)
    din("c_bf64", (128, 128), BF16)
    din("c_rope", (128, 128), BF16)
    din("c_ones", (1, 512), BF16)

    out = nc.dram_tensor("out", [N_OWN, D], F32, kind="ExternalOutput").ap()

    sc = {}

    def dsc(name, shape, dt=BF16):
        sc[name] = nc.dram_tensor(name, list(shape), dt, kind="Internal").ap()

    dsc("C_up", (32 * 128, DT * 256))
    dsc("C_gate", (32 * 128, DT * 256))
    dsc("C_down", (DT * 128, IT * 128))
    if DBG:
        dsc("DBGP", (N_S12, D))
    dsc("KT", (D, S))
    dsc("VV", (S, D))
    dsc("QT", (D, N_S12))
    dsc("CT", (N_S12, D))
    dsc("US", (INTER, N_S12))
    dsc("K3T", (512, N_S12))
    dsc("V3", (N_S12, 512))
    dsc("Q3T", (D, N_OWN))
    dsc("XR1", (N_S12, D), F32)
    dsc("XR2", (N_S12, D), F32)
    dsc("XR3", (N_S12, D), F32)

    t = NS()
    with ExitStack() as ctx:
        p = Prog(nc, ctx)
        sb = lambda n, s, d: ctx.enter_context(nc.sbuf_tensor("sb_" + n, s, d))
        pst = lambda n, s, d: ctx.enter_context(nc.psum_tensor(n, s, d))

        t.c_ident = sb("c_ident", [128, 128], BF16)
        t.c_h128 = sb("c_h128", [128, 128], BF16)
        t.c_sh128 = sb("c_sh128", [128, DT * 128], BF16)
        t.c_bf16 = sb("c_bf16", [128, 128], BF16)
        t.c_bf64 = sb("c_bf64", [128, 128], BF16)
        t.c_rope = sb("c_rope", [128, 128], BF16)
        t.c_ones = sb("c_ones", [1, 512], BF16)
        t.c_bqkv = sb("c_bqkv", [1, 6144], BF16)
        t.c_bout = sb("c_bout", [1, D], BF16)
        t.c_bq = sb("c_bq", [1, D], BF16)
        t.c_bkv = sb("c_bkv", [1, 1024], BF16)
        t.c_bo = sb("c_bo", [1, D], BF16)
        t.c_vup = sb("c_vup", [128, 64], F32)
        t.c_vgate = sb("c_vgate", [128, 64], F32)
        t.c_vdp = sb("c_vdp", [128, 16], F32)
        t.c_suug = sb("c_suug", [128, 16], F32)
        t.c_sudp = sb("c_sudp", [128, 64], F32)
        lnsets = []
        for i in range(2):
            st = NS()
            st.bn6 = sb("bn6_%d" % i, [128, 24], F32)
            st.mv = sb("mv_%d" % i, [128, 2], F32)
            st.t0 = sb("t0_%d" % i, [128, 1], F32)
            st.rstd = sb("rstd_%d" % i, [128, 1], F32)
            st.negmr = sb("negmr_%d" % i, [128, 1], F32)
            lnsets.append(st)
        t.lnstat = Rot(lnsets)
        t.c_eps = sb("c_eps", [128, 1], F32)
        t.trrot = Rot([pst("ps_tr0", [128, 1024], BF16),
                       pst("ps_tr1", [128, 1024], BF16)])
        t.ak_psrot = Rot([pst("ps_mm0", [128, 512], F32),
                          pst("ps_mm1", [128, 512], F32)])
        t.ak_xrot = Rot([sb("xf0", [128, D], F32), sb("xf1", [128, D], F32)])
        t.ak_lbrot = Rot([sb("lnbf0", [128, D], BF16),
                          sb("lnbf1", [128, D], BF16)])
        t.ak_krot = Rot([sb("kb0", [128, 512], BF16),
                         sb("kb1", [128, 512], BF16)])

        with ExitStack() as c2:
            sb2 = lambda n, s, d: c2.enter_context(
                nc.sbuf_tensor("sb_" + n, s, d))
            t.c_identf = sb2("c_identf", [128, 128], F32)
            t.v_wsug = sb2("v_wsug", [128, 128], F32)
            t.v_wsdp = sb2("v_wsdp", [16, 128], F32)
            t.v_suug = sb2("v_suug", [16, 128], F32)
            t.v_sudp = sb2("v_sudp", [64, 128], F32)
            t.ps_v0 = c2.enter_context(nc.psum_tensor("ps_v0", [128, 512], F32))
            t.ps_v1 = c2.enter_context(nc.psum_tensor("ps_v1", [128, 512], F32))
            emit_const_prep(p, t, hb)

        with ExitStack() as cf:
            sbf = lambda n, s, d: cf.enter_context(
                nc.sbuf_tensor("sb_" + n, s, d))
            t.fbrot = Rot([sbf("fb0", [128, 1024], BF16),
                           sbf("fb1", [128, 1024], BF16)])
            t.forot = Rot([sbf("fo0", [128, 1024], BF16),
                           sbf("fo1", [128, 1024], BF16)])
            t.fprot = Rot([cf.enter_context(nc.psum_tensor("fp0", [128, 512],
                                                           F32)),
                           cf.enter_context(nc.psum_tensor("fp1", [128, 512],
                                                           F32))])
            t.fold_up, t.fold_gate, t.fold_down = [], [], []

            def mk_ug_store(c_ap):
                c_r = c_ap.rearrange("(cb n s) (a v) -> n (s a) cb v",
                                     cb=32, n=16, s=8, a=16, v=256)

                def store(n, c0, cw, ob):
                    cb0 = c0 // 256
                    ncb = cw // 256
                    return (c_r[n][:, cb0:cb0 + ncb, :],
                            ob.rearrange("p (cb v) -> p cb v",
                                         cb=ncb, v=256))
                return store

            def mk_dn_store(c_ap):
                c_r = c_ap.rearrange("(o n s) (a u) -> n (s a) o u",
                                     o=16, n=64, s=2, a=64, u=128)

                def store(n, c0, cw, ob):
                    o0 = c0 // 128
                    no = cw // 128
                    return (c_r[n][:, o0:o0 + no, :],
                            ob.rearrange("p (o u) -> p o u", o=no, u=128))
                return store

            t.bg = [
                gen_cache_fold(p, t, hb["wupTf"], DT, t.c_bf16, INTER,
                               t.fold_up, mk_ug_store(sc["C_up"])),
                gen_cache_fold(p, t, hb["wgateTf"], DT, t.c_bf16, INTER,
                               t.fold_gate, mk_ug_store(sc["C_gate"])),
                gen_cache_fold(p, t, hb["wdownTf"], IT, t.c_bf64, D,
                               t.fold_down, mk_dn_store(sc["C_down"])),
            ]

            with ExitStack() as c2:
                sb2 = lambda n, s, d: c2.enter_context(
                    nc.sbuf_tensor("sb_" + n, s, d))
                t.ak_wk = sb2("ak_wk", [128, DT * 2048], BF16)
                t.ak_wv = sb2("ak_wv", [128, DT * 2048], BF16)
                t.ak_lnrot = Rot([sb2("ak_ln0", [128, DT * 256], BF16)])
                t.ak_vrot = Rot([sb2("ak_v0", [128, D], BF16)])
                emit_attn_a_kv(p, t, hb, sc)

            with ExitStack() as c2:
                sb2 = lambda n, s, d: c2.enter_context(
                    nc.sbuf_tensor("sb_" + n, s, d))
                t.aq_wq = sb2("aq_wq", [128, DT * 2048], BF16)
                t.aq_lnrot = Rot([sb2("aq_ln0", [128, DT * 512], BF16),
                                  sb2("aq_ln1", [128, DT * 512], BF16)])
                emit_attn_a_q(p, t, hb, sc)

            with ExitStack() as c2:
                sb2 = lambda n, s, d: c2.enter_context(
                    nc.sbuf_tensor("sb_" + n, s, d))
                pst2 = lambda n, s, d: c2.enter_context(
                    nc.psum_tensor(n, s, d))
                t.fl_un = sb2("fl_un", [128, DT * QT_W], BF16)
                t.fl_probs = sb2("fl_probs", [128, 5 * 4096], BF16)
                t.fl_kv = sb2("fl_kv", [128, DT * 1024], BF16)
                t.fl_kvrot = Rot([t.fl_kv[:, i * DT * 512:(i + 1) * DT * 512]
                                  for i in range(2)])
                t.fl_mx = sb2("fl_mx", [128, 48], F32)
                t.fl_bb = sb2("fl_bb", [128, 48], F32)
                t.fl_sm = sb2("fl_sm", [128, 48], F32)
                t.fl_gmx = sb2("fl_gmx", [128, 5], F32)
                t.fl_gnb = sb2("fl_gnb", [128, 5], F32)
                t.fl_rf = sb2("fl_rf", [128, 48], F32)
                t.fl_lw = sb2("fl_lw", [128, 48], F32)
                t.fl_l = sb2("fl_l", [128, 5], F32)
                t.fl_rl = sb2("fl_rl", [128, 5], F32)
                t.fl_w = sb2("fl_w", [128, 48], F32)
                t.fl_sc = sb2("fl_sc", [128, 2048], BF16)
                t.fl_sc_war = []
                t.fl_psctx = pst2("fl_ctx", [128, 1024], F32)
                t.fl_par = [sb2("fl_par%d" % i, [128, 2048], BF16)
                            for i in range(5)]
                t.fl_chrot = Rot([sb2("fl_ch0", [128, 1024], BF16),
                                  sb2("fl_ch1", [128, 1024], BF16)])
                t.fl_ctrot = Rot([sb2("fl_ct0", [128, 1024], BF16),
                                  sb2("fl_ct1", [128, 1024], BF16)])
                t.fl_qt_war = []
                t.fl_kv_war = []
                t.fl_probs_war = []
                t.fl_ctx_war = []
                t.fl_ptq_war = {}
                t.fl_par_war = {}
                emit_attn_a_flash(p, t, sc)

            with ExitStack() as c2:
                sb2 = lambda n, s, d: c2.enter_context(
                    nc.sbuf_tensor("sb_" + n, s, d))
                t.ao_wo = sb2("ao_wo", [128, DT * 2048], BF16)
                t.ao_ctrot = Rot([sb2("ao_ct0", [128, D], BF16),
                                  sb2("ao_ct1", [128, D], BF16)])
                t.ao_xnew = sb2("ao_xnew", [128, D], F32)
                t.ao_xnew_war = []
                emit_attn_a_out(p, t, hb, sc)

            pump(t, 10 ** 9)
            assert not t.bg
            p.barrier()

        emit_ffn(p, nc, ctx, t, sc, sc["XR1"], sc["XR2"], NT_S12, "f1_")

        with ExitStack() as c2:
            sb2 = lambda n, s, d: c2.enter_context(nc.sbuf_tensor("sb_" + n, s, d))
            pst2 = lambda n, s, d: c2.enter_context(nc.psum_tensor(n, s, d))
            t.kr_cos = sb2("kr_cos", [128, N_S12], F32)
            t.kr_sin = sb2("kr_sin", [128, N_S12], F32)
            t.kr_wq = sb2("kr_wq", [128, DT * 2048], BF16)
            t.kr_wkv = sb2("kr_wkv", [128, DT * 1024], BF16)
            t.kr_lnrot = Rot([sb2("kr_ln0", [128, DT * 512], BF16)])
            t.kr_psr = Rot([pst2("kr_psr0", [128, 512], F32),
                            pst2("kr_psr1", [128, 512], F32)])
            t.kr_t1rot = Rot([sb2("kr_t10", [128, 512], F32),
                              sb2("kr_t11", [128, 512], F32)])
            t.kr_t2 = sb2("kr_t2", [128, 512], F32)
            t.kr_t2_war = []
            t.kr_rorot = Rot([sb2("kr_ro0", [128, 512], BF16),
                              sb2("kr_ro1", [128, 512], BF16)])
            emit_kvrope(p, t, hb, sc)

        with ExitStack() as c2:
            sb2 = lambda n, s, d: c2.enter_context(nc.sbuf_tensor("sb_" + n, s, d))
            pst2 = lambda n, s, d: c2.enter_context(nc.psum_tensor(n, s, d))
            t.ab_k3 = sb2("ab_k3", [128, 4 * N_S12], BF16)
            t.ab_v3 = sb2("ab_v3", [128, NT_S12 * 512], BF16)
            t.ab_wo = sb2("ab_wo", [128, DT * 2048], BF16)
            t.ab_qrot = Rot([sb2("ab_q0", [128, DT * 128], BF16),
                             sb2("ab_q1", [128, DT * 128], BF16)])
            t.ab_mrot = Rot([sb2("ab_m0", [128, 768], F32),
                             sb2("ab_m1", [128, 768], F32)])
            absets = []
            for i in range(2):
                st = NS()
                st.s = sb2("ab_s%d" % i, [128, 768], F32)
                st.nmx = sb2("ab_nmx%d" % i, [128, 1], F32)
                st.l = sb2("ab_l%d" % i, [128, 1], F32)
                st.rl = sb2("ab_rl%d" % i, [128, 1], F32)
                st.dmat = sb2("ab_dmat%d" % i, [128, 128], BF16)
                absets.append(st)
            t.ab_stat = Rot(absets)
            t.ab_prrot = Rot([sb2("ab_pr0", [128, 768], BF16),
                              sb2("ab_pr1", [128, 768], BF16)])
            t.ab_ptrot = Rot([sb2("ab_pt0", [128, 768], BF16),
                              sb2("ab_pt1", [128, 768], BF16),
                              sb2("ab_pt2", [128, 768], BF16),
                              sb2("ab_pt3", [128, 768], BF16)])
            t.ab_scrot = Rot([pst2("ab_sc0", [128, 1024], F32),
                              pst2("ab_sc1", [128, 1024], F32)])
            t.ab_ctxps = t.ak_psrot
            t.ab_ctxT = sb2("ab_ctxT", [128, DT * 128], BF16)
            t.ab_xnew = sb2("ab_xnew", [128, D], F32)
            t.ab_ctxT_war = []
            t.ab_xnew_war = []
            emit_attn_b(p, t, hb, sc)

        emit_ffn(p, nc, ctx, t, sc,
                 sc["XR3"][OWN0 * 128:(OWN0 + NT_OWN) * 128, :], out, NT_OWN,
                 "f2_")

        p.final_wait()
        p.check()
        with nc.Block() as block:
            p.emit_all(block)
    return nc


def _fold_perm(wT, a_factor, n_tiles):
    sgrp = 128 // a_factor
    r = wT.reshape(a_factor, n_tiles, sgrp, wT.shape[1])
    return np.ascontiguousarray(r.transpose(1, 2, 0, 3).reshape(wT.shape))


def stage_inputs(inputs):
    hc = host_constants()
    x = np.asarray(inputs["L0_attn_a_hidden_states"], np.float32)
    shared = dict(
        w_qkvT=np.ascontiguousarray(
            np.asarray(inputs["L0_attn_a_w_qkv"], np.float32).T),
        b_qkv=np.asarray(inputs["L0_attn_a_b_qkv"], np.float32).reshape(1, -1),
        w_outT=np.ascontiguousarray(
            np.asarray(inputs["L0_attn_a_w_out"], np.float32).T),
        b_out=np.asarray(inputs["L0_attn_a_b_out"], np.float32).reshape(1, -1),
        wq=np.ascontiguousarray(inputs["L0_attn_b_Wq"], np.float32),
        bq=np.asarray(inputs["L0_attn_b_bq"], np.float32).reshape(1, -1),
        wkv=np.ascontiguousarray(inputs["L0_attn_b_Wkv"], np.float32),
        bkv=np.asarray(inputs["L0_attn_b_bkv"], np.float32).reshape(1, -1),
        wo=np.ascontiguousarray(inputs["L0_attn_b_Wo"], np.float32),
        bo=np.asarray(inputs["L0_attn_b_bo"], np.float32).reshape(1, -1),
        su_ug=np.asarray(inputs["L0_ffn_SU_ug"], np.float32).reshape(16, 128),
        su_dp=np.asarray(inputs["L0_ffn_SU_dp"], np.float32).reshape(64, 128),
        ws_ug=np.asarray(inputs["L0_ffn_Wscale_ug"],
                         np.float32).reshape(128, 128),
        ws_dp=np.asarray(inputs["L0_ffn_Wscale_dp"],
                         np.float32).reshape(16, 128),
        wupTf=_fold_perm(np.ascontiguousarray(
            np.asarray(inputs["L0_ffn_W_up"], np.float32).T), 16, 16),
        wgateTf=_fold_perm(np.ascontiguousarray(
            np.asarray(inputs["L0_ffn_W_gate"], np.float32).T), 16, 16),
        wdownTf=_fold_perm(np.ascontiguousarray(
            np.asarray(inputs["L0_ffn_W_down"], np.float32).T), 64, 64),
        **hc,
    )
    in_maps = []
    for c in range(8):
        b, h = c // 2, c % 2
        tiles = slot_tiles(h)
        xb = np.ascontiguousarray(x[b])
        xs = np.ascontiguousarray(
            np.concatenate([x[b, tt * 128:(tt + 1) * 128, :] for tt in tiles],
                           axis=0))
        cos, sin = host_rope_tables(h)
        masks = host_masks(h).reshape(16 * 128, 768)
        m = dict(shared)
        m.update(xb=xb, xs=xs, cosq=cos, sinq=sin,
                 masks=np.ascontiguousarray(masks))
        in_maps.append(m)
    return in_maps


_CACHED = {}


def kernel(**inputs):
    if "nc" not in _CACHED:
        _CACHED["nc"] = build_nc()
    nc = _CACHED["nc"]
    in_maps = stage_inputs(inputs)
    res = run_bass_kernel_spmd(nc, in_maps, core_ids=list(range(8)))
    out = np.zeros((B, S, D), np.float32)
    for c in range(8):
        b, h = c // 2, c % 2
        out[b, h * 2048:(h + 1) * 2048, :] = res.results[c]["out"]
    return out
